# version 06
# baseline (speedup 1.0000x reference)
import math
from contextlib import ExitStack

import numpy as np
import ml_dtypes

import concourse.bass as bass
import concourse.mybir as mybir
from concourse.bass import IndirectOffsetOnAxis
from concourse.bass_utils import run_bass_kernel_spmd

F32 = mybir.dt.float32
BF16 = mybir.dt.bfloat16
U32 = mybir.dt.uint32
AF = mybir.ActivationFunctionType
ALU = mybir.AluOpType
AX = mybir.AxisListType
BF = ml_dtypes.bfloat16

D = 2048
B = 2
T = 4096
NTOK = 1024
NTT = 8
NKC = D // 128
NE = 32
INTER = 768
NJ = INTER // 128
NH = 16
DH = 128
ISQ = 1.0 / math.sqrt(DH)
VST = 144

CAP1 = 1024
STS1 = [(0, 512), (512, 512)]
CAP2 = 512
STS2 = [(0, 512)]
XROW = D + 32
NZ1 = NE * CAP1
NZ2 = NE * CAP2
BIG = 1 << 20

C_S = [4, 8, 12, 16, 32, 28, 24, 20]
J_OF_PIECE = [0, 1, 2, 3, 7, 6, 5, 4]


def _tiles_of_core(cp):
    return [4 * i + cp for i in range(4)] + [31 - 4 * i - cp for i in range(4)]


def _fold_pos(k):
    if k <= 15:
        return k % 4, k // 4
    m = 31 - k
    return m % 4, 4 + m // 4


def _khcol(k):
    r, j = _fold_pos(k)
    return r * 8 + j


class Prog:
    SEMS = ("pe", "dve", "act", "coll")

    def __init__(self, nc, ctx):
        self.nc = nc
        self.ctx = ctx
        self.sem = {}
        self.cnt = {}
        self.hw = {}
        for name in self.SEMS:
            self.sem[name] = ctx.enter_context(nc.semaphore(name))
            self.cnt[name] = 0
        for e in ("pe", "dve", "act", "sync", "gp"):
            self.hw[e] = {}

    def new(self, name):
        if name not in self.sem:
            self.sem[name] = self.ctx.enter_context(self.nc.semaphore(name))
            self.cnt[name] = 0
        return name

    def eng(self, e):
        return {"pe": self.nc.tensor, "dve": self.nc.vector, "act": self.nc.scalar,
                "sync": self.nc.sync, "gp": self.nc.gpsimd}[e]

    def wait(self, e, sname, val=None):
        if val is None:
            val = self.cnt[sname]
        if val <= 0 or self.hw[e].get(sname, 0) >= val:
            return
        self.eng(e).wait_ge(self.sem[sname], val)
        self.hw[e][sname] = val

    def inc(self, inst, sname, k=1):
        inst.then_inc(self.sem[sname], k)
        self.cnt[sname] += k
        return self.cnt[sname]

    def barrier(self, engines=("pe", "dve", "act", "sync", "gp")):
        for e in engines:
            for s in list(self.sem):
                self.wait(e, s)


def build_program(n_experts=NE, n_heads=NH):
    nc = bass.Bass("TRN2", target_bir_lowering=False, debug=False)

    def din(name, shape, dt=F32):
        return nc.dram_tensor(name, list(shape), dt, kind="ExternalInput").ap()

    x_in = din("x_in", [4, NTOK, D])
    ind_in = din("ind_in", [128, 4])
    wgu = din("wgu", [NE, NJ, 128, 4096])
    wdn = din("wdn", [NE, 128, NJ * 2048])
    gw = din("gw", [128, 512])
    wq = din("wq", [4, 128, 8192])
    wk = din("wk", [4, 128, 8192])
    wv = din("wv", [4, 128, 8192])
    wo = din("wo", [2, 128, 16384])
    bq = din("bq", [128, 16])
    bk = din("bk", [128, 16])
    bv = din("bv", [128, 16])
    bo_bc = din("bo_bc", [128, D])
    ident = din("ident", [128, 128], BF16)
    ones_row = din("ones_row", [1, 128], BF16)
    ones_row_f = din("ones_row_f", [1, 128])
    onec_bf = din("onec_bf", [128, 1], BF16)
    triu_s = din("triu_s", [128, 128], BF16)
    iota32 = din("iota32", [128, 32])
    tbc_in = din("tbc_in", [128, 32])
    bigp_in = din("bigp_in", [128, 1])
    eps_col = din("eps_col", [128, 1])
    amask = din("amask", [8, 128, 512], BF16)

    y_out = nc.dram_tensor("y", [NTOK, D], F32, kind="ExternalOutput").ap()
    kv_dram = nc.dram_tensor("kv_dram", [2 * NH * 128, T], BF16, kind="Internal").ap()
    x1_dram = nc.dram_tensor("x1_dram", [4, NTOK, D], F32, kind="Internal").ap()
    q_dram = nc.dram_tensor("q_dram", [NH * 128, NTOK], BF16, kind="Internal").ap()
    xg1 = nc.dram_tensor("xg1", [NE * CAP1, XROW], BF16, kind="Internal").ap()
    o1 = nc.dram_tensor("o1", [NE * CAP1 + 1, D], BF16, kind="Internal").ap()
    xg2 = nc.dram_tensor("xg2", [NE * CAP2, XROW], BF16, kind="Internal").ap()
    o2 = nc.dram_tensor("o2", [NE * CAP2 + 1, D], BF16, kind="Internal").ap()
    wgub = nc.dram_tensor("wgub", [NE, 2, 128, 3 * 4096], BF16, kind="Internal").ap()
    wdb = nc.dram_tensor("wdb", [NE, 128, NJ * 2048], BF16, kind="Internal").ap()

    with ExitStack() as ctx:
        E = ctx.enter_context
        p = Prog(nc, ctx)
        te, dv, sc, gp, sy = nc.tensor, nc.vector, nc.scalar, nc.gpsimd, nc.sync

        x_sb = None
        hfm = None
        htm = E(nc.sbuf_tensor("htm", [128, D], BF16))
        stat = E(nc.sbuf_tensor("stat", [128, 16], F32))
        idn = E(nc.sbuf_tensor("idn", [128, 128], BF16))
        onesr = E(nc.sbuf_tensor("onesr", [1, 128], BF16))
        onesrf = E(nc.sbuf_tensor("onesrf", [1, 128], F32))
        onecb = E(nc.sbuf_tensor("onecb", [128, 1], BF16))
        trius = E(nc.sbuf_tensor("trius", [128, 128], BF16))
        iot32 = E(nc.sbuf_tensor("iot32", [128, 32], F32))
        tbc = E(nc.sbuf_tensor("tbc", [128, 32], F32))
        bigp = E(nc.sbuf_tensor("bigp", [128, 1], F32))
        epsc = E(nc.sbuf_tensor("epsc", [128, 1], F32))
        bqs = E(nc.sbuf_tensor("bqs", [128, 16], F32))
        bks = E(nc.sbuf_tensor("bks", [128, 16], F32))
        bvs = E(nc.sbuf_tensor("bvs", [128, 16], F32))
        gws = E(nc.sbuf_tensor("gws", [128, 512], BF16))
        comb = E(nc.sbuf_tensor("comb", [128, NTT * 32], F32))
        rt = E(nc.sbuf_tensor("rt", [128, 512], F32))
        ind_sb = E(nc.sbuf_tensor("ind_sb", [128, 4], F32))
        mcol = E(nc.sbuf_tensor("mcol", [128, 12], F32))
        mcolb = E(nc.sbuf_tensor("mcolb", [128, 1], BF16))
        mrow = E(nc.sbuf_tensor("mrow", [1, 128], BF16))
        rsum = E(nc.sbuf_tensor("rsum", [128, 1], F32))
        cstage = E(nc.sbuf_tensor("cstage", [128, 128], BF16))
        gidxT = E(nc.sbuf_tensor("gidxT", [128, 4 * NTT * 8], U32))
        off1 = E(nc.sbuf_tensor("off1", [1, 32], F32))
        ridx = E(nc.sbuf_tensor("ridx", [128, 16], F32))
        ridxu = E(nc.sbuf_tensor("ridxu", [128, 16], U32))

        with nc.Block():
            p.new("c_ld")
            p.new("xg_ld")
            p.new("xsc")
            p.new("zr")

            def load_x_group(pp):
                i = gp.dma_start(x_sb[:, :].rearrange("p (t d) -> p t d", t=NTT),
                                 x_in[pp].rearrange("(t p) d -> p t d", p=128))
                return p.inc(i, "xg_ld", 16)

            for dst, srcap in [(idn, ident), (onesr, ones_row), (onesrf, ones_row_f),
                               (onecb, onec_bf), (trius, triu_s), (iot32, iota32),
                               (tbc, tbc_in), (bigp, bigp_in), (epsc, eps_col),
                               (bqs, bq), (bks, bk), (bvs, bv), (gws, gw),
                               (ind_sb, ind_in)]:
                i = gp.dma_start(dst[:, :], srcap)
                p.inc(i, "c_ld", 16)
            for e in ("pe", "dve", "act", "sync", "gp"):
                p.wait(e, "c_ld")
            i = dv.memset(htm[0:1, :], 0.0)
            zr_d = p.inc(i, "dve")
            p.wait("act", "dve", zr_d)
            i = sc.dma_start(o1[NZ1:NZ1 + 1, :], htm[0:1, :])
            p.inc(i, "zr", 16)
            i = sc.dma_start(o2[NZ2:NZ2 + 1, :], htm[0:1, :])
            zr_c = p.inc(i, "zr", 16)
            p.wait("act", "zr", zr_c)

            bnd1 = gp.alloc_register("bnd1")
            gp.reg_mov(bnd1, NE * CAP1 - 1)
            bnd2 = gp.alloc_register("bnd2")
            gp.reg_mov(bnd2, NE * CAP2 - 1)

            def emit_ln(tp_ps, xst=None, pre_tile=None):
                tp_free = {0: 0, 1: 0}
                a_h_prev = 0
                xst_cp_prev = 0
                for tt in range(NTT):
                    xt = x_sb[:, tt * D:(tt + 1) * D]
                    if pre_tile is not None:
                        d_pre = pre_tile(tt)
                        p.wait("act", "dve", d_pre)
                    ssum = stat[:, 0:1]; ex2 = stat[:, 1:2]; mneg = stat[:, 2:3]
                    nmsq = stat[:, 3:4]; var = stat[:, 4:5]; std = stat[:, 5:6]
                    rstd = stat[:, 6:7]; nmr = stat[:, 7:8]
                    p.wait("act", "pe")
                    p.wait("act", "dve", xst_cp_prev)
                    i = sc.activation(htm[:, :], xt, AF.Square, accum_out=ex2)
                    a_sq = p.inc(i, "act")
                    p.wait("dve", "act", a_h_prev)
                    i = dv.reduce_sum(ssum, xt, axis=AX.X, negate=True)
                    p.inc(i, "dve"); p.wait("dve", "dve")
                    i = dv.tensor_scalar(mneg, ssum, 1.0 / D, None, op0=ALU.mult)
                    p.inc(i, "dve"); p.wait("dve", "dve")
                    i = dv.tensor_scalar(nmsq, mneg, mneg, -1.0, op0=ALU.mult, op1=ALU.mult)
                    p.inc(i, "dve")
                    p.wait("dve", "act", a_sq); p.wait("dve", "dve")
                    i = dv.tensor_scalar(var, ex2, 1.0 / D, None, op0=ALU.mult)
                    p.inc(i, "dve"); p.wait("dve", "dve")
                    i = dv.tensor_add(var, var, nmsq)
                    d_var = p.inc(i, "dve")
                    p.wait("act", "dve", d_var)
                    i = sc.activation(std, var, AF.Sqrt, bias=epsc[:, :])
                    a_std = p.inc(i, "act")
                    p.wait("dve", "act", a_std)
                    i = dv.reciprocal(rstd, std)
                    p.inc(i, "dve"); p.wait("dve", "dve")
                    i = dv.tensor_scalar(nmr, mneg, rstd, None, op0=ALU.mult)
                    d_nmr = p.inc(i, "dve")
                    p.wait("act", "dve", d_nmr)
                    p.wait("act", "act", a_sq)
                    i = sc.activation(htm[:, :], xt, AF.Identity, bias=nmr, scale=rstd)
                    a_h = p.inc(i, "act")
                    a_h_prev = a_h
                    if xst is not None:
                        p.wait("dve", "act", a_h)
                        p.wait("dve", "xsc")
                        i = dv.tensor_copy(xst[:, tt * XROW: tt * XROW + D],
                                           htm[:, :])
                        xst_cp_prev = p.inc(i, "dve")
                    p.wait("pe", "act", a_h)
                    for kc in range(NKC):
                        bank = kc % 2
                        p.wait("pe", "dve", tp_free[bank])
                        i = te.transpose(tp_ps[bank][:, 0:128],
                                         htm[:, kc * 128:(kc + 1) * 128], idn[:, :])
                        t_pe = p.inc(i, "pe")
                        p.wait("dve", "pe", t_pe)
                        i = dv.tensor_copy(
                            hfm[:, kc * NTOK + tt * 128: kc * NTOK + (tt + 1) * 128],
                            tp_ps[bank][:, 0:128])
                        tp_free[bank] = p.inc(i, "dve")

            def emit_routing(lg_ps, rp=None, xst=None, grp=0, rc=None):
                probs = rt[:, 0:32]; vals = rt[:, 32:40]; cum = rt[:, 40:48]
                lt = rt[:, 48:56]; act8 = rt[:, 56:64]; rw = rt[:, 64:72]
                smax = rt[:, 72:73]; ssum = rt[:, 73:74]; sinv = rt[:, 74:75]
                s8 = rt[:, 75:76]; den = rt[:, 76:77]; dinv = rt[:, 77:78]
                s2 = rt[:, 78:79]; den2 = rt[:, 79:80]; d2i = rt[:, 80:81]
                sca = rt[:, 81:82]
                mk = [rt[:, 96 + k * 32: 128 + k * 32] for k in range(8)]
                m32 = rt[:, 352:384]; slotg = rt[:, 384:416]
                tmp32 = rt[:, 416:448]
                ek = rt[:, 448:449]; sel = rt[:, 449:450]; base = rt[:, 450:451]
                t1 = rt[:, 451:452]; t2 = rt[:, 452:453]

                def dstep(inst):
                    c = p.inc(inst, "dve")
                    p.wait("dve", "dve", c)
                    return c

                for tt in range(NTT):
                    p.wait("pe", "dve")
                    p.wait("pe", "act")
                    for kc in range(NKC):
                        i = te.matmul(lg_ps[:, 0:32],
                                      hfm[:, kc * NTOK + tt * 128: kc * NTOK + (tt + 1) * 128],
                                      gws[:, kc * 32:(kc + 1) * 32],
                                      start=(kc == 0), stop=(kc == NKC - 1))
                    m_pe = p.inc(i, "pe")
                    p.wait("dve", "pe", m_pe)
                    dstep(dv.reduce_max(smax, lg_ps[:, 0:32], axis=AX.X, negate=True))
                    p.wait("act", "dve")
                    p.wait("act", "pe", m_pe)
                    i = sc.activation(probs, lg_ps[:, 0:32], AF.Exp, bias=smax,
                                      accum_out=ssum)
                    a = p.inc(i, "act")
                    p.wait("dve", "act", a)
                    dstep(dv.reciprocal(sinv, ssum))
                    dstep(dv.tensor_scalar(probs, probs, sinv, None, op0=ALU.mult))
                    dstep(dv.tensor_mul(probs, probs, tbc[:, :]))
                    for k in range(8):
                        dstep(dv.reduce_max(vals[:, k:k + 1], probs, axis=AX.X))
                        dstep(dv.tensor_scalar(mk[k], probs, vals[:, k:k + 1], None,
                                               op0=ALU.is_equal))
                        dstep(dv.scalar_tensor_tensor(probs, mk[k], -2.0, probs,
                                                      op0=ALU.mult, op1=ALU.add))
                    dstep(dv.reduce_sum(s8, vals, axis=AX.X))
                    dstep(dv.tensor_scalar(den, s8, 1e-12, None, op0=ALU.max))
                    dstep(dv.reciprocal(dinv, den))
                    dstep(dv.tensor_scalar(cum, vals, dinv, None, op0=ALU.mult))
                    for k in range(1, 8):
                        dstep(dv.tensor_add(cum[:, k:k + 1], cum[:, k:k + 1],
                                            cum[:, k - 1:k]))
                    dstep(dv.tensor_scalar(lt, cum, 0.8, None, op0=ALU.is_lt))
                    dstep(dv.memset(act8[:, 0:1], 1.0))
                    dstep(dv.tensor_copy(act8[:, 1:8], lt[:, 0:7]))
                    dstep(dv.tensor_mul(rw, vals, act8))
                    dstep(dv.reduce_sum(s2, rw, axis=AX.X))
                    dstep(dv.tensor_scalar(den2, s2, 1e-12, None, op0=ALU.max))
                    dstep(dv.reciprocal(d2i, den2))
                    dstep(dv.tensor_scalar(sca, d2i, 0.5, None, op0=ALU.mult))
                    dstep(dv.tensor_scalar(rw, rw, sca, None, op0=ALU.mult))
                    ctt = comb[:, tt * 32:(tt + 1) * 32]
                    dstep(dv.tensor_scalar(ctt, mk[0], rw[:, 0:1], None, op0=ALU.mult))
                    for k in range(1, 8):
                        dstep(dv.scalar_tensor_tensor(ctt, mk[k], rw[:, k:k + 1], ctt,
                                                      op0=ALU.mult, op1=ALU.add))
                    if rp is None:
                        continue
                    dstep(dv.tensor_copy(
                        xst[:, tt * XROW + D: tt * XROW + D + 32], ctt))
                    dstep(dv.tensor_scalar(m32, ctt, 0.0, None, op0=ALU.is_gt))
                    d_mbf = dstep(dv.tensor_copy(htm[:, 0:32], m32))
                    p.wait("pe", "dve", d_mbf)
                    i = te.matmul(rp[:, 0:32], trius[:, :], htm[:, 0:32],
                                  start=True, stop=True)
                    p.inc(i, "pe")
                    i = te.matmul(rp[0:1, 64:96], onecb[:, :], htm[:, 0:32],
                                  start=True, stop=True)
                    p.inc(i, "pe")
                    i = te.matmul(rp[:, 128:160], onesrf[:, :], off1[0:1, :],
                                  start=True, stop=True)
                    bc_pe = p.inc(i, "pe")
                    p.wait("dve", "pe", bc_pe)
                    dstep(dv.tensor_copy(tmp32, rp[:, 0:32]))
                    dstep(dv.tensor_add(slotg, tmp32, rp[:, 128:160]))
                    dstep(dv.tensor_add(off1[0:1, :], off1[0:1, :], rp[0:1, 64:96]))
                    gcols = ridx[:, 0:8]
                    scols = ridx[:, 8:16]
                    for k in range(8):
                        dstep(dv.tensor_mul(tmp32, mk[k], iot32[:, :]))
                        dstep(dv.reduce_sum(ek, tmp32, axis=AX.X))
                        dstep(dv.tensor_mul(tmp32, mk[k], slotg))
                        dstep(dv.reduce_sum(sel, tmp32, axis=AX.X))
                        dstep(dv.scalar_tensor_tensor(base, ek, float(rc["cap"]), sel,
                                                      op0=ALU.mult, op1=ALU.add))
                        ak = act8[:, k:k + 1]
                        dstep(dv.tensor_scalar(t1, base, float(-rc["nz"]), None,
                                               op0=ALU.add))
                        dstep(dv.tensor_mul(t1, t1, ak))
                        dstep(dv.tensor_scalar(gcols[:, k:k + 1], t1, float(rc["nz"]),
                                               None, op0=ALU.add))
                        dstep(dv.tensor_sub(t2, base, bigp[:, :]))
                        dstep(dv.tensor_mul(t2, t2, ak))
                        dstep(dv.tensor_add(scols[:, k:k + 1], t2, bigp[:, :]))
                    gc_u = gidxT[:, (grp * NTT + tt) * 8:(grp * NTT + tt) * 8 + 8]
                    dstep(dv.tensor_copy(gc_u, gcols))
                    d_sc = dstep(dv.tensor_copy(ridxu[:, 0:8], scols))
                    p.wait("gp", "dve", d_sc)
                    for k in range(8):
                        i = gp.indirect_dma_start(
                            rc["xg"],
                            IndirectOffsetOnAxis(ap=ridxu[:, k:k + 1], axis=0),
                            xst[:, tt * XROW:(tt + 1) * XROW], None,
                            bounds_check=rc["bnd"], oob_is_err=False)
                        p.inc(i, "xsc", 16)

            def emit_ffn_routed(tag, cap, sts, xg_ap, o_ap, n_exp, f32_src=True):
                nsub_tot = cap // 128
                subs_of_st = [n // 128 for (_, n) in sts]
                with ExitStack() as fs:
                    Ef = fs.enter_context
                    gin = Ef(nc.sbuf_tensor(f"gin{tag}", [128, 4 * XROW], BF16))
                    ginT = Ef(nc.sbuf_tensor(f"ginT{tag}", [128, NKC * 512], BF16))
                    act_sb = Ef(nc.sbuf_tensor(f"acts{tag}", [128, NJ * cap], BF16))
                    cbuf = Ef(nc.sbuf_tensor(f"cbuf{tag}", [128, 16], F32))
                    sil = Ef(nc.sbuf_tensor(f"sil{tag}", [128, 512], F32))
                    outst = [Ef(nc.sbuf_tensor(f"o{tag}{s}", [128, D], BF16))
                             for s in range(2)]
                    wgu_sb = [Ef(nc.sbuf_tensor(f"wgu{tag}{s}", [128, 3 * 4096], BF16))
                              for s in range(2)]
                    wd_sb = Ef(nc.sbuf_tensor(f"wd{tag}", [128, NJ * 2048], BF16))
                    g_ps = Ef(nc.psum_tensor(f"g{tag}", [128, 512], F32))
                    u_ps = Ef(nc.psum_tensor(f"u{tag}", [128, 512], F32))
                    b_ps = Ef(nc.psum_tensor(f"b{tag}", [128, 2048], F32))
                    tr_ps = [Ef(nc.psum_tensor(f"tr{tag}{s}", [128, 512], BF16))
                             for s in range(2)]

                    sgu = [p.new(f"wgu{tag}{s}") for s in range(2)]
                    swd = p.new(f"swd{tag}")
                    sgin = p.new(f"gin{tag}")
                    so1 = p.new(f"so{tag}")
                    swb = p.new(f"swb{tag}")
                    half_free = {}
                    half_ld = {}
                    wb_done = {}
                    wd_free = {}
                    wd_ld = {}
                    wdwb_done = {}
                    gin_free = {}
                    gin_free_dve = {}
                    gin_ld = {}

                    nst = len(sts)

                    def load_half(e, h):
                        idx = 2 * e + h
                        if idx >= 2:
                            p.wait("gp", "pe", half_free[idx - 2])
                            if f32_src:
                                p.wait("gp", swb, wb_done[idx - 2])
                        if f32_src:
                            i = gp.dma_start(
                                wgu_sb[idx % 2][:, :].rearrange("p (j n) -> p j n", j=3),
                                wgu[e, 3 * h:3 * h + 3].rearrange("j p n -> p j n"))
                        else:
                            i = gp.dma_start(wgu_sb[idx % 2][:, :], wgub[e, h])
                        half_ld[idx] = (sgu[idx % 2], p.inc(i, sgu[idx % 2], 16))

                    def load_wd(e):
                        qe, q = ("gp", gp) if f32_src else ("sync", sy)
                        if e >= 1:
                            p.wait(qe, "pe", wd_free[e - 1])
                            if f32_src:
                                p.wait(qe, swb, wdwb_done[e - 1])
                        i = q.dma_start(wd_sb[:, :], wdn[e] if f32_src else wdb[e])
                        wd_ld[e] = p.inc(i, swd, 16)

                    def load_gin(gst):
                        e, sti = divmod(gst, nst)
                        if e >= n_exp:
                            return
                        st_off, st_n = sts[sti]
                        ns = st_n // 128
                        if gst >= 1:
                            p.wait("sync", "pe", gin_free[gst - 1])
                            p.wait("sync", "dve", gin_free_dve[gst - 1])
                        i = sy.dma_start(
                            gin[:, 0:ns * XROW].rearrange(
                                "p (s c) -> p s c", s=ns),
                            xg_ap[e * cap + st_off: e * cap + st_off + st_n].rearrange(
                                "(s p) c -> p s c", p=128))
                        gin_ld[gst] = (sgin, p.inc(i, sgin, 16))

                    load_half(0, 0)
                    load_half(0, 1)
                    load_wd(0)
                    load_gin(0)

                    a_sg = None
                    a_tt = None
                    tp_free = {0: 0, 1: 0}
                    out_free = {0: 0, 1: 0}
                    d_cb_prev = 0

                    for e in range(n_exp):
                        if f32_src:
                            for h in range(2):
                                idx = 2 * e + h
                                p.wait("act", *half_ld[idx])
                                i = sc.dma_start(wgub[e, h], wgu_sb[idx % 2][:, :])
                                wb_done[idx] = p.inc(i, swb, 16)
                        for sti, (st_off, st_n) in enumerate(sts):
                            gst = e * nst + sti
                            ns = subs_of_st[sti]
                            p.wait("pe", *gin_ld[gst])
                            p.wait("dve", *gin_ld[gst])
                            p.wait("dve", "dve", d_cb_prev)
                            for s in range(ns):
                                i = dv.tensor_copy(
                                    cbuf[:, st_off // 128 + s: st_off // 128 + s + 1],
                                    gin[:, s * XROW + D + e: s * XROW + D + e + 1])
                                d_cb = p.inc(i, "dve")
                            d_cb_prev = d_cb
                            for s in range(ns):
                                for kc in range(NKC):
                                    bank = kc % 2
                                    p.wait("pe", "dve", tp_free[bank])
                                    i = te.transpose(
                                        tr_ps[bank][:, 0:128],
                                        gin[:, s * XROW + kc * 128: s * XROW + (kc + 1) * 128],
                                        idn[:, :])
                                    t_pe = p.inc(i, "pe")
                                    p.wait("dve", "pe", t_pe)
                                    i = dv.tensor_copy(
                                        ginT[:, kc * 512 + s * 128: kc * 512 + (s + 1) * 128],
                                        tr_ps[bank][:, 0:128])
                                    tp_free[bank] = p.inc(i, "dve")
                            gin_free[gst] = p.cnt["pe"]
                            gin_free_dve[gst] = p.cnt["dve"]
                            load_gin(gst + 1)
                            for j in range(NJ):
                                wt = wgu_sb[(2 * e + j // 3) % 2]
                                joff = (j % 3) * 4096
                                p.wait("pe", *half_ld[2 * e + j // 3])
                                if a_sg is not None:
                                    p.wait("pe", "act", a_sg)
                                p.wait("pe", "dve")
                                for kc in range(NKC):
                                    i = te.matmul(
                                        g_ps[:, 0:st_n],
                                        wt[:, joff + kc * 256: joff + kc * 256 + 128],
                                        ginT[:, kc * 512: kc * 512 + st_n],
                                        start=(kc == 0), stop=(kc == NKC - 1))
                                g_pe = p.inc(i, "pe")
                                if a_tt is not None:
                                    p.wait("pe", "dve", a_tt)
                                for kc in range(NKC):
                                    i = te.matmul(
                                        u_ps[:, 0:st_n],
                                        wt[:, joff + kc * 256 + 128: joff + kc * 256 + 256],
                                        ginT[:, kc * 512: kc * 512 + st_n],
                                        start=(kc == 0), stop=(kc == NKC - 1))
                                u_pe = p.inc(i, "pe")
                                if sti == nst - 1 and (j == 2 or j == NJ - 1):
                                    half_free[2 * e + (0 if j == 2 else 1)] = u_pe
                                p.wait("act", "pe", g_pe)
                                if a_tt is not None:
                                    p.wait("act", "dve", a_tt)
                                i = sc.activation(sil[:, 0:st_n], g_ps[:, 0:st_n],
                                                  AF.Silu)
                                a_sg = p.inc(i, "act")
                                p.wait("dve", "pe", u_pe)
                                p.wait("dve", "act", a_sg)
                                i = dv.tensor_mul(
                                    act_sb[:, j * cap + st_off: j * cap + st_off + st_n],
                                    sil[:, 0:st_n], u_ps[:, 0:st_n])
                                a_tt = p.inc(i, "dve")
                        if e + 1 < n_exp:
                            load_half(e + 1, 0)
                            load_half(e + 1, 1)
                        p.wait("pe", swd, wd_ld[e])
                        if f32_src:
                            p.wait("sync", swd, wd_ld[e])
                            i = sy.dma_start(wdb[e], wd_sb[:, :])
                            wdwb_done[e] = p.inc(i, swb, 16)
                        p.wait("pe", "dve", a_tt)
                        for sub in range(nsub_tot):
                            oslot = sub % 2
                            for dq in range(4):
                                if dq == 0:
                                    p.wait("pe", so1, out_free[oslot])
                                    p.wait("pe", "dve")
                                for j in range(NJ):
                                    i = te.matmul(
                                        b_ps[:, dq * 512:(dq + 1) * 512],
                                        act_sb[:, j * cap + sub * 128: j * cap + (sub + 1) * 128],
                                        wd_sb[:, j * 2048 + dq * 512: j * 2048 + (dq + 1) * 512],
                                        start=(j == 0), stop=(j == NJ - 1))
                                b_pe = p.inc(i, "pe")
                                p.wait("dve", "pe", b_pe)
                                p.wait("dve", so1, out_free[oslot])
                                i = dv.tensor_scalar(
                                    outst[oslot][:, dq * 512:(dq + 1) * 512],
                                    b_ps[:, dq * 512:(dq + 1) * 512],
                                    cbuf[:, sub:sub + 1], None, op0=ALU.mult)
                                d_ev = p.inc(i, "dve")
                            p.wait("act", "dve", d_ev)
                            i = sc.dma_start(
                                o_ap[e * cap + sub * 128: e * cap + (sub + 1) * 128, :],
                                outst[oslot][:, :])
                            out_free[oslot] = p.inc(i, so1, 16)
                        wd_free[e] = p.cnt["pe"]
                        if e + 1 < n_exp:
                            load_wd(e + 1)

            def emit_second_gather(o_ap, grp, tag=""):
                with ExitStack() as fs:
                    Ef = fs.enter_context
                    gbuf = [Ef(nc.sbuf_tensor(f"gb{tag}{grp}{s}", [128, D], BF16))
                            for s in range(2)]
                    sgb = p.new(f"sgb{tag}{grp}")
                    gb_free = {0: 0, 1: 0}
                    for tt in range(NTT):
                        xt = x_sb[:, tt * D:(tt + 1) * D]
                        for k in range(8):
                            slot = k % 2
                            p.wait("gp", "dve", gb_free[slot])
                            i = gp.indirect_dma_start(
                                gbuf[slot][:, :], None, o_ap,
                                IndirectOffsetOnAxis(
                                    ap=gidxT[:, (grp * NTT + tt) * 8 + k:
                                             (grp * NTT + tt) * 8 + k + 1],
                                    axis=0))
                            g_ld = p.inc(i, sgb, 16)
                            p.wait("dve", sgb, g_ld)
                            p.wait("dve", "dve")
                            i = dv.tensor_add(xt, xt, gbuf[slot][:, :])
                            gb_free[slot] = p.inc(i, "dve")

            swq = [p.new(f"wqk{s}") for s in range(2)]
            sstg = [p.new(f"stg{s}") for s in range(3)]
            sx1 = p.new("sx1")
            stg_free = {}

            def emit_proj(specs, tag):
                with ExitStack() as fs:
                    Ef = fs.enter_context
                    wbuf = [Ef(nc.sbuf_tensor(f"wqkv{tag}{s}", [128, 8192], BF16))
                            for s in range(2)]
                    stg = [Ef(nc.sbuf_tensor(f"qkvstg{tag}{s}", [128, NTOK], BF16))
                           for s in range(3)]
                    mg_ps = [Ef(nc.psum_tensor(f"mg{tag}{s}", [128, NTOK], F32))
                             for s in range(4)]
                    wld = {}
                    grp_evac = 0
                    i = gp.dma_start(wbuf[0][:, :], specs[0][1][0])
                    wld[0] = (swq[0], p.inc(i, swq[0], 16))
                    gidx = 0
                    for si, (nm, wsrc, bsb, roff, coff, dst) in enumerate(specs):
                        for mg in range(4):
                            slot = gidx % 2
                            nxt = (wsrc, mg + 1) if mg < 3 else (
                                (specs[si + 1][1], 0) if si + 1 < len(specs) else None)
                            if nxt is not None:
                                p.wait("gp", "pe")
                                i = gp.dma_start(wbuf[1 - slot][:, :], nxt[0][nxt[1]])
                                wld[gidx + 1] = (swq[1 - slot], p.inc(i, swq[1 - slot], 16))
                            p.wait("pe", *wld[gidx])
                            p.wait("pe", "act", grp_evac)
                            last_evac = 0
                            for mc in range(4):
                                for half in range(2):
                                    for kc in range(NKC):
                                        i = te.matmul(
                                            mg_ps[mc][:, half * 512:(half + 1) * 512],
                                            wbuf[slot][:, kc * 512 + mc * 128: kc * 512 + (mc + 1) * 128],
                                            hfm[:, kc * NTOK + half * 512: kc * NTOK + (half + 1) * 512],
                                            start=(kc == 0), stop=(kc == NKC - 1))
                                mc_pe = p.inc(i, "pe")
                                m_abs = mg * 4 + mc
                                sslot = m_abs % 3
                                p.wait("act", "pe", mc_pe)
                                if sslot in stg_free:
                                    p.wait("act", sstg[sslot], stg_free[sslot])
                                i = sc.activation(stg[sslot][:, :], mg_ps[mc][:, :],
                                                  AF.Identity, bias=bsb[:, m_abs:m_abs + 1])
                                a_e = p.inc(i, "act")
                                last_evac = a_e
                                p.wait("sync", "act", a_e)
                                i = sy.dma_start(
                                    dst[roff + m_abs * 128: roff + (m_abs + 1) * 128,
                                        coff:coff + NTOK],
                                    stg[sslot][:, :])
                                stg_free[sslot] = p.inc(i, sstg[sslot], 16)
                            grp_evac = last_evac
                            gidx += 1

            def emit_attention(n_heads_):
                with ExitStack() as fs:
                    Ef = fs.enter_context
                    qh = [Ef(nc.sbuf_tensor(f"qh{s}", [128, NTOK], BF16)) for s in range(2)]
                    kh = [Ef(nc.sbuf_tensor(f"kh{s}", [128, 4096], BF16)) for s in range(2)]
                    vh = [Ef(nc.sbuf_tensor(f"vh{s}", [128, 32 * VST], BF16))
                          for s in range(2)]
                    pt = [Ef(nc.sbuf_tensor(f"pt{s}", [128, 512], BF16)) for s in range(2)]
                    masks = Ef(nc.sbuf_tensor("masks", [128, 8 * 512], BF16))
                    smsk = p.new("msk")
                    i = gp.dma_start(masks[:, :].rearrange("p (s n) -> p s n", s=8),
                                     amask.rearrange("s p n -> p s n"))
                    mask_ld = p.inc(i, smsk, 16)
                    p.wait("dve", smsk, mask_ld)
                    p1_ps = [Ef(nc.psum_tensor(f"p1{s}", [128, 512], F32)) for s in range(2)]
                    sT_ps = [Ef(nc.psum_tensor(f"sT{s}", [128, 512], F32)) for s in range(2)]
                    cs_ps = [Ef(nc.psum_tensor(f"cs{s}", [128, 512], F32)) for s in range(2)]
                    tr_ps = [Ef(nc.psum_tensor(f"tr{s}", [128, 1024], BF16)) for s in range(2)]

                    for s in range(2):
                        i = dv.memset(
                            vh[s][:, :].rearrange("p (c n) -> p c n", c=32)[:, :, 128:129],
                            1.0)
                        d_ones = p.inc(i, "dve")
                    p.wait("sync", "dve", d_ones)
                    p.wait("pe", "dve", d_ones)

                    skq = [p.new(f"kq{s}") for s in range(2)]
                    svh = [p.new(f"vh{s}") for s in range(2)]
                    kld = {}
                    vld = {}
                    head_pe_done = {}

                    def load_head(h):
                        slot = h % 2
                        if h >= 2:
                            p.wait("sync", "pe", head_pe_done[h - 2])
                        i = sy.dma_start(qh[slot][:, :], q_dram[h * 128:(h + 1) * 128, :])
                        p.inc(i, skq[slot], 16)
                        i = sy.dma_start(kh[slot][:, :],
                                         kv_dram[h * 128:(h + 1) * 128, :])
                        p.inc(i, skq[slot], 16)
                        kld[h] = (skq[slot], p.cnt[skq[slot]])
                        for cc in range(32):
                            i = sy.dma_start(
                                vh[slot][:, cc * VST: cc * VST + 128],
                                kv_dram[NH * 128 + h * 128: NH * 128 + (h + 1) * 128,
                                        cc * 128:(cc + 1) * 128],
                                transpose=True)
                            p.inc(i, svh[slot], 16)
                        vld[h] = (svh[slot], p.cnt[svh[slot]])

                    for s3 in range(3):
                        p.wait("sync", f"stg{s3}")
                        p.wait("act", f"stg{s3}")
                    load_head(0)
                    cs_free = {0: 0, 1: 0}
                    pt_free = {0: 0, 1: 0}
                    p1_free = {0: 0, 1: 0}
                    sT_free = {0: 0, 1: 0}
                    for h in range(n_heads_):
                        slot = h % 2
                        if h + 1 < n_heads_:
                            load_head(h + 1)
                        p.wait("pe", *kld[h])
                        p.wait("pe", *vld[h])
                        kh4 = kh[slot][:, :].rearrange("p (r j c) -> p r j c", r=4, j=8)
                        for s in range(8):
                            C = C_S[s]
                            qsl = qh[slot][:, s * 128:(s + 1) * 128]
                            for g in range(C // 4):
                                bank = g % 2
                                p.wait("pe", "dve", p1_free[bank])
                                i = te.matmul(p1_ps[bank][:, :], qsl,
                                              kh4[:, :, J_OF_PIECE[g], :],
                                              start=True, stop=True)
                                pe_c = p.inc(i, "pe")
                                p.wait("dve", "pe", pe_c)
                                i = dv.reduce_max(mcol[:, g:g + 1], p1_ps[bank][:, :],
                                                  axis=AX.X)
                                p1_free[bank] = p.inc(i, "dve")
                            p.wait("dve", "dve")
                            i = dv.reduce_max(mcol[:, 8:9], mcol[:, 0:C // 4],
                                              axis=AX.X, negate=True)
                            p.inc(i, "dve"); p.wait("dve", "dve")
                            i = dv.tensor_copy(mcolb[:, :], mcol[:, 8:9])
                            d_mb = p.inc(i, "dve")
                            p.wait("pe", "dve", d_mb)
                            i = te.transpose(tr_ps[0][0:1, 0:128], mcolb[:, :], idn[:, :])
                            t_pe = p.inc(i, "pe")
                            p.wait("dve", "pe", t_pe)
                            i = dv.tensor_copy(mrow[:, :], tr_ps[0][0:1, 0:128])
                            d_mr = p.inc(i, "dve")
                            p.wait("pe", "dve", d_mr)
                            ngr = C // 4
                            cb = s % 2
                            ctx_pe = 0
                            for g in range(ngr):
                                bank = g % 2
                                p.wait("pe", "act", sT_free[bank])
                                for ci in range(4):
                                    k = g * 4 + ci
                                    col = _khcol(k)
                                    i = te.matmul(sT_ps[bank][:, ci * 128:(ci + 1) * 128],
                                                  kh[slot][:, col * 128:(col + 1) * 128],
                                                  qsl, start=(ci == 0), stop=False)
                                for ci in range(4):
                                    i = te.matmul(sT_ps[bank][:, ci * 128:(ci + 1) * 128],
                                                  onesr[:, :], mrow[:, :],
                                                  start=False, stop=(ci == 3))
                                g_pe = p.inc(i, "pe")
                                p.wait("act", "pe", g_pe)
                                p.wait("act", "pe", pt_free[bank])
                                i = sc.activation(pt[bank][:, :], sT_ps[bank][:, :],
                                                  AF.Exp, scale=ISQ)
                                a_exp = p.inc(i, "act")
                                sT_free[bank] = a_exp
                                if g == ngr - 1:
                                    p.wait("dve", "act", a_exp)
                                    i = dv.tensor_mul(pt[bank][:, :], pt[bank][:, :],
                                                      masks[:, s * 512:(s + 1) * 512])
                                    d_msk = p.inc(i, "dve")
                                    p.wait("pe", "dve", d_msk)
                                else:
                                    p.wait("pe", "act", a_exp)
                                if g == 0:
                                    p.wait("pe", "dve", cs_free[cb])
                                for ci in range(4):
                                    k = g * 4 + ci
                                    col = _khcol(k)
                                    i = te.matmul(cs_ps[cb][:, 0:129],
                                                  pt[bank][:, ci * 128:(ci + 1) * 128],
                                                  vh[slot][:, col * VST: col * VST + 129],
                                                  start=(k == 0), stop=(k == C - 1))
                                ctx_pe = p.inc(i, "pe")
                                pt_free[bank] = ctx_pe
                            p.wait("dve", "pe", ctx_pe)
                            i = dv.reciprocal(rsum[:, :], cs_ps[cb][:, 128:129])
                            p.inc(i, "dve"); p.wait("dve", "dve")
                            i = dv.tensor_scalar(cstage[:, :], cs_ps[cb][:, 0:128],
                                                 rsum[:, :], None, op0=ALU.mult)
                            d_cs = p.inc(i, "dve")
                            cs_free[cb] = d_cs
                            p.wait("pe", "dve", d_cs)
                            i = te.transpose(tr_ps[1][:, 0:128], cstage[:, :], idn[:, :])
                            t2 = p.inc(i, "pe")
                            p.wait("dve", "pe", t2)
                            i = dv.tensor_copy(
                                hfm[:, h * NTOK + s * 128: h * NTOK + (s + 1) * 128],
                                tr_ps[1][:, 0:128])
                            p.inc(i, "dve")
                        head_pe_done[h] = p.cnt["pe"]

            def emit_outproj():
                with ExitStack() as fs:
                    Ef = fs.enter_context
                    wo_sb = Ef(nc.sbuf_tensor("wo_sb", [128, 16384], BF16))
                    bos = Ef(nc.sbuf_tensor("bos", [128, D], F32))
                    op_ps = [Ef(nc.psum_tensor(f"op{s}", [128, D], F32)) for s in range(2)]
                    sbo = p.new("bob")
                    swo = p.new("wol")
                    i = gp.dma_start(bos[:, :], bo_bc)
                    bo_ld = p.inc(i, sbo, 16)
                    p.wait("dve", sbo, bo_ld)
                    for tt in range(NTT):
                        p.wait("dve", "dve")
                        i = dv.tensor_add(x_sb[:, tt * D:(tt + 1) * D],
                                          x_sb[:, tt * D:(tt + 1) * D], bos[:, :])
                        p.inc(i, "dve")
                    op_free = {0: 0, 1: 0}
                    for halfk in range(2):
                        if halfk:
                            p.wait("gp", "pe")
                        i = gp.dma_start(
                            wo_sb[:, :].rearrange("p (k n) -> p k n", k=8),
                            wo[halfk].rearrange("p (k n) -> p k n", k=8))
                        ld = p.inc(i, swo, 16)
                        p.wait("pe", swo, ld)
                        for tt in range(NTT):
                            bank = tt % 2
                            p.wait("pe", "dve", op_free[bank])
                            for ns in range(4):
                                for kc8 in range(8):
                                    kc = halfk * 8 + kc8
                                    i = te.matmul(
                                        op_ps[bank][:, ns * 512:(ns + 1) * 512],
                                        hfm[:, kc * NTOK + tt * 128: kc * NTOK + (tt + 1) * 128],
                                        wo_sb[:, kc8 * 2048 + ns * 512: kc8 * 2048 + (ns + 1) * 512],
                                        start=(kc8 == 0), stop=(kc8 == 7))
                            o_pe = p.inc(i, "pe")
                            p.wait("dve", "pe", o_pe)
                            p.wait("dve", "dve")
                            i = dv.tensor_add(x_sb[:, tt * D:(tt + 1) * D],
                                              x_sb[:, tt * D:(tt + 1) * D],
                                              op_ps[bank][:, :])
                            op_free[bank] = p.inc(i, "dve")

            def ln_pass(name, xst=None, pre_tile=None, bar=True):
                with ExitStack() as ps0:
                    tp = [ps0.enter_context(nc.psum_tensor(f"{name}{s}", [128, 1024], BF16))
                          for s in range(2)]
                    emit_ln(tp, xst=xst, pre_tile=pre_tile)
                    if bar:
                        p.barrier()

            with ExitStack() as p1s:
                x_sb = p1s.enter_context(
                    nc.sbuf_tensor("x_sbA", [128, NTT * D], F32))
                hfm = p1s.enter_context(
                    nc.sbuf_tensor("hfmA", [128, NKC * NTOK], BF16))
                xst = p1s.enter_context(
                    nc.sbuf_tensor("xst", [128, NTT * XROW], BF16))
                i = dv.memset(off1[0:1, :], 0.0)
                p.inc(i, "dve")
                for cp in range(4):
                    if cp > 0:
                        p.barrier()
                    xg = load_x_group(cp)
                    for e in ("pe", "dve", "act"):
                        p.wait(e, "xg_ld", xg)
                    ln_pass(f"lnA{cp}", xst=xst)
                    with ExitStack() as ps1:
                        rp = ps1.enter_context(
                            nc.psum_tensor(f"rp{cp}", [128, 512], F32))
                        lg = ps1.enter_context(
                            nc.psum_tensor(f"lg1{cp}", [128, 512], F32))
                        emit_routing(lg, rp=rp, xst=xst, grp=cp,
                                     rc=dict(cap=CAP1, xg=xg1, bnd=bnd1, nz=NZ1))
                        p.barrier()
                p.barrier()

            emit_ffn_routed("f1", CAP1, STS1, xg1, o1, n_experts)
            p.barrier()

            p2s = ctx.enter_context(ExitStack())
            x_sb = p2s.enter_context(
                nc.sbuf_tensor("x_sbB", [128, NTT * D], F32))
            hs = ExitStack()
            hfm = hs.enter_context(
                nc.sbuf_tensor("hfmB", [128, NKC * NTOK], BF16))

            with ExitStack() as g2s:
                gbuf = [g2s.enter_context(nc.sbuf_tensor(f"gbP{s}", [128, D], BF16))
                        for s in range(2)]
                sgb = p.new("sgbP")
                gb_state = {0: 0, 1: 0}

                def mk_pre(grp):
                    def pre(tt):
                        xt = x_sb[:, tt * D:(tt + 1) * D]
                        last = 0
                        for k in range(8):
                            slot = k % 2
                            p.wait("gp", "dve", gb_state[slot])
                            i = gp.indirect_dma_start(
                                gbuf[slot][:, :], None, o1,
                                IndirectOffsetOnAxis(
                                    ap=gidxT[:, (grp * NTT + tt) * 8 + k:
                                             (grp * NTT + tt) * 8 + k + 1],
                                    axis=0))
                            g_ld = p.inc(i, sgb, 16)
                            p.wait("dve", sgb, g_ld)
                            p.wait("dve", "dve")
                            i = dv.tensor_add(xt, xt, gbuf[slot][:, :])
                            gb_state[slot] = p.inc(i, "dve")
                            last = gb_state[slot]
                        return last
                    return pre

                xg = load_x_group(0)
                for cp in range(4):
                    for e in ("pe", "dve", "act"):
                        p.wait(e, "xg_ld", xg)
                    ln_pass(f"lnB{cp}", pre_tile=mk_pre(cp), bar=False)
                    p.wait("sync", "dve")
                    i = sy.dma_start(
                        x1_dram[cp].rearrange("(t p) d -> p t d", p=128),
                        x_sb[:, :].rearrange("p (t d) -> p t d", t=NTT))
                    p.inc(i, sx1, 16)
                    if cp < 3:
                        p.wait("gp", "act")
                        p.wait("gp", "dve")
                        p.wait("gp", sx1)
                        xg = load_x_group(cp + 1)
                    p.wait("pe", "dve")
                    emit_proj([("k", wk, bks, 0, cp * NTOK, kv_dram),
                               ("v", wv, bvs, NH * 128, cp * NTOK, kv_dram)],
                              f"{cp}")
                p.barrier()

            with ExitStack() as bs:
                Eb = bs.enter_context
                xtmp = [Eb(nc.sbuf_tensor(f"xtmp{s}", [128, D], F32)) for s in range(2)]
                sbl = p.new("xbl")
                p.wait("sync", sx1)
                for tt in range(NTT):
                    for cp in range(4):
                        slot = cp % 2
                        p.wait("sync", "dve")
                        i = sy.dma_start(
                            xtmp[slot][:, :],
                            x1_dram[cp, tt * 128:(tt + 1) * 128, :])
                        ldc = p.inc(i, sbl, 16)
                        p.wait("dve", sbl, ldc)
                        xt = x_sb[:, tt * D:(tt + 1) * D]
                        if cp == 0:
                            i = dv.tensor_scalar(xt, xtmp[slot][:, :],
                                                 ind_sb[:, 0:1], None, op0=ALU.mult)
                        else:
                            i = dv.scalar_tensor_tensor(
                                xt, xtmp[slot][:, :], ind_sb[:, cp:cp + 1], xt,
                                op0=ALU.mult, op1=ALU.add)
                        p.inc(i, "dve")
                p.barrier()
            ln_pass("lnQ")
            emit_proj([("q", wq, bqs, 0, 0, q_dram)], "q")
            p.barrier()
            emit_attention(n_heads)
            p.barrier()
            emit_outproj()
            p.barrier()
            i = dv.memset(off1[0:1, :], 0.0)
            p.inc(i, "dve"); p.wait("dve", "dve")
            with ExitStack() as f2s:
                xst2 = f2s.enter_context(
                    nc.sbuf_tensor("xst2", [128, NTT * XROW], BF16))
                ln_pass("lnC", xst=xst2)
                with ExitStack() as ps1:
                    rp2 = ps1.enter_context(
                        nc.psum_tensor("rpF2", [128, 512], F32))
                    lg = ps1.enter_context(nc.psum_tensor("lg2", [128, 512], F32))
                    emit_routing(lg, rp=rp2, xst=xst2, grp=0,
                                 rc=dict(cap=CAP2, xg=xg2, bnd=bnd2, nz=NZ2))
                    p.barrier()
            hs.close()
            emit_ffn_routed("f2", CAP2, STS2, xg2, o2, n_experts, f32_src=False)
            p.barrier()
            emit_second_gather(o2, 0, tag="s2")
            p.barrier()

            p.barrier(("sync",))
            sst = p.new("st")
            i = sy.dma_start(y_out.rearrange("(t p) d -> p t d", p=128),
                             x_sb[:, :].rearrange("p (t d) -> p t d", t=NTT))
            st = p.inc(i, sst, 16)
            p.wait("sync", sst, st)

    return nc


def prep_shared(Wq, bq_, Wk, bk_, Wv, bv_, Wo, bo_, gate_w, gate_up, down):
    def f32(a):
        return np.ascontiguousarray(np.asarray(a, dtype=np.float32))

    gup = f32(gate_up)
    wgu_a = np.empty((NE, NJ, 128, 4096), np.float32)
    for e in range(NE):
        Wt = gup[e].T
        for j in range(NJ):
            blk = np.concatenate([Wt[:, j * 128:(j + 1) * 128],
                                  Wt[:, 768 + j * 128: 768 + (j + 1) * 128]], axis=1)
            wgu_a[e, j] = blk.reshape(16, 128, 256).transpose(1, 0, 2).reshape(128, 4096)
    dn = f32(down)
    wdn_a = np.empty((NE, 128, NJ * 2048), np.float32)
    for e in range(NE):
        Wt = dn[e].T
        wdn_a[e] = Wt.reshape(NJ, 128, 2048).transpose(1, 0, 2).reshape(128, NJ * 2048)
    gwt = f32(gate_w).T
    gw_a = np.ascontiguousarray(
        gwt.reshape(16, 128, 32).transpose(1, 0, 2).reshape(128, 512))

    def qkvw(W):
        Wt = f32(W).T
        out = np.empty((4, 128, 8192), np.float32)
        for mg in range(4):
            blk = Wt[:, mg * 512:(mg + 1) * 512]
            out[mg] = blk.reshape(16, 128, 512).transpose(1, 0, 2).reshape(128, 8192)
        return out

    Wot = f32(Wo).T
    wo_a = np.empty((2, 128, 16384), np.float32)
    for hf in range(2):
        blk = Wot[hf * 1024:(hf + 1) * 1024, :]
        wo_a[hf] = blk.reshape(8, 128, 2048).transpose(1, 0, 2).reshape(128, 16384)

    def bias16(b):
        return np.ascontiguousarray(f32(b).reshape(16, 128).T)

    return dict(
        wgu=wgu_a, wdn=wdn_a, gw=gw_a,
        wq=qkvw(Wq), wk=qkvw(Wk), wv=qkvw(Wv), wo=wo_a,
        bq=bias16(bq_), bk=bias16(bk_), bv=bias16(bv_),
        bo_bc=np.ascontiguousarray(np.tile(f32(bo_)[None, :], (128, 1))),
        ident=np.eye(128, dtype=BF),
        ones_row=np.ones((1, 128), dtype=BF),
        ones_row_f=np.ones((1, 128), np.float32),
        onec_bf=np.ones((128, 1), dtype=BF),
        triu_s=np.triu(np.ones((128, 128), np.float32), 1).astype(BF),
        iota32=np.tile(np.arange(32, dtype=np.float32)[None, :], (128, 1)),
        tbc_in=np.tile(1.0 + np.arange(32, dtype=np.float32)[None, :] * 1e-6,
                       (128, 1)),
        bigp_in=(float(BIG) + np.arange(128, dtype=np.float32))[:, None].copy(),
        eps_col=np.full((128, 1), 1e-5, np.float32),
    )


def core_mask(cp):
    tri = np.triu(np.ones((128, 128), np.float32))
    out = np.zeros((8, 128, 512), np.float32)
    tiles = _tiles_of_core(cp)
    for s in range(8):
        C = C_S[s]
        C_own = tiles[s] + 1
        for idx, k in enumerate(range(C - 4, C)):
            if k < C_own - 1:
                out[s, :, idx * 128:(idx + 1) * 128] = 1.0
            elif k == C_own - 1:
                out[s, :, idx * 128:(idx + 1) * 128] = tri
    return out.astype(BF)


def make_in_maps(inputs):
    x = np.ascontiguousarray(np.asarray(inputs["L0_attn_x"], dtype=np.float32))
    shared = prep_shared(inputs["L0_attn_Wq"], inputs["L0_attn_bq"],
                         inputs["L0_attn_Wk"], inputs["L0_attn_bk"],
                         inputs["L0_attn_Wv"], inputs["L0_attn_bv"],
                         inputs["L0_attn_Wo"], inputs["L0_attn_bo"],
                         inputs["L0_ffn_gate_weight"], inputs["L0_ffn_gate_up_proj"],
                         inputs["L0_ffn_down_proj"])
    xcov = []
    masks = [core_mask(cp) for cp in range(4)]
    for g in range(B):
        xg = x[g].reshape(32, 128, D)
        xcov.append(np.ascontiguousarray(np.stack(
            [xg[_tiles_of_core(p)].reshape(NTOK, D) for p in range(4)])))
    in_maps = []
    for c in range(8):
        cp, g = c % 4, c // 4
        ind = np.zeros((128, 4), np.float32)
        ind[:, cp] = 1.0
        m = dict(shared)
        m["x_in"] = xcov[g]
        m["ind_in"] = ind
        m["amask"] = masks[cp]
        in_maps.append(m)
    return in_maps


def gather_out(results):
    out = np.empty((B, T, D), np.float32)
    for c in range(8):
        cp, g = c % 4, c // 4
        tiles = _tiles_of_core(cp)
        yc = np.asarray(results[c]["y"], dtype=np.float32).reshape(8, 128, D)
        for s, t in enumerate(tiles):
            out[g, t * 128:(t + 1) * 128, :] = yc[s]
    return out


def kernel(**inputs):
    in_maps = make_in_maps(inputs)
    nc = build_program()
    res = run_bass_kernel_spmd(nc, in_maps, core_ids=list(range(8)))
    return gather_out(res.results)


# version 07
# speedup vs baseline: 1.1932x; 1.1932x over previous
import math
from contextlib import ExitStack

import numpy as np
import ml_dtypes

import concourse.bass as bass
import concourse.mybir as mybir
from concourse.bass import IndirectOffsetOnAxis
from concourse.bass_utils import run_bass_kernel_spmd

F32 = mybir.dt.float32
BF16 = mybir.dt.bfloat16
U32 = mybir.dt.uint32
AF = mybir.ActivationFunctionType
ALU = mybir.AluOpType
AX = mybir.AxisListType
BF = ml_dtypes.bfloat16

D = 2048
B = 2
T = 4096
NTOK = 1024
NTT = 8
NKC = D // 128
NE = 32
INTER = 768
NJ = INTER // 128
NH = 16
DH = 128
ISQ = 1.0 / math.sqrt(DH)
VST = 144

CAP1 = 1024
STS1 = [(0, 512), (512, 512)]
CAP2 = 512
STS2 = [(0, 512)]
XROW = D + 32
NZ1 = NE * CAP1
NZ2 = NE * CAP2
BIG = 1 << 20

C_S = [4, 8, 12, 16, 32, 28, 24, 20]
J_OF_PIECE = [0, 1, 2, 3, 7, 6, 5, 4]


def _tiles_of_core(cp):
    return [4 * i + cp for i in range(4)] + [31 - 4 * i - cp for i in range(4)]


def _fold_pos(k):
    if k <= 15:
        return k % 4, k // 4
    m = 31 - k
    return m % 4, 4 + m // 4


def _khcol(k):
    r, j = _fold_pos(k)
    return r * 8 + j


class Prog:
    SEMS = ("pe", "dve", "act", "coll")

    def __init__(self, nc, ctx):
        self.nc = nc
        self.ctx = ctx
        self.sem = {}
        self.cnt = {}
        self.hw = {}
        for name in self.SEMS:
            self.sem[name] = ctx.enter_context(nc.semaphore(name))
            self.cnt[name] = 0
        for e in ("pe", "dve", "act", "sync", "gp"):
            self.hw[e] = {}

    def new(self, name):
        if name not in self.sem:
            self.sem[name] = self.ctx.enter_context(self.nc.semaphore(name))
            self.cnt[name] = 0
        return name

    def eng(self, e):
        return {"pe": self.nc.tensor, "dve": self.nc.vector, "act": self.nc.scalar,
                "sync": self.nc.sync, "gp": self.nc.gpsimd}[e]

    def wait(self, e, sname, val=None):
        if val is None:
            val = self.cnt[sname]
        if val <= 0 or self.hw[e].get(sname, 0) >= val:
            return
        self.eng(e).wait_ge(self.sem[sname], val)
        self.hw[e][sname] = val

    def inc(self, inst, sname, k=1):
        inst.then_inc(self.sem[sname], k)
        self.cnt[sname] += k
        return self.cnt[sname]

    def barrier(self, engines=("pe", "dve", "act", "sync", "gp")):
        for e in engines:
            for s in list(self.sem):
                self.wait(e, s)


def build_program(n_experts=NE, n_heads=NH):
    nc = bass.Bass("TRN2", target_bir_lowering=False, debug=False)

    def din(name, shape, dt=F32):
        return nc.dram_tensor(name, list(shape), dt, kind="ExternalInput").ap()

    x_in = din("x_in", [4, NTOK, D])
    ind_in = din("ind_in", [128, 4])
    wgu = din("wgu", [NE, NJ, 128, 4096])
    wdn = din("wdn", [NE, 128, NJ * 2048])
    gw = din("gw", [128, 512])
    wq = din("wq", [4, 128, 8192])
    wk = din("wk", [4, 128, 8192])
    wv = din("wv", [4, 128, 8192])
    wo = din("wo", [2, 128, 16384])
    bq = din("bq", [128, 16])
    bk = din("bk", [128, 16])
    bv = din("bv", [128, 16])
    bo_bc = din("bo_bc", [128, D])
    ident = din("ident", [128, 128], BF16)
    ones_row = din("ones_row", [1, 128], BF16)
    ones_row_f = din("ones_row_f", [1, 128])
    onec_bf = din("onec_bf", [128, 1], BF16)
    triu_s = din("triu_s", [128, 128], BF16)
    iota32 = din("iota32", [128, 32])
    tbc_in = din("tbc_in", [128, 32])
    bigp_in = din("bigp_in", [128, 1])
    eps_col = din("eps_col", [128, 1])
    amask = din("amask", [8, 128, 512], BF16)

    y_out = nc.dram_tensor("y", [NTOK, D], F32, kind="ExternalOutput").ap()
    kv_dram = nc.dram_tensor("kv_dram", [2 * NH * 128, T], BF16, kind="Internal").ap()
    x1_dram = nc.dram_tensor("x1_dram", [4, NTOK, D], F32, kind="Internal").ap()
    q_dram = nc.dram_tensor("q_dram", [NH * 128, NTOK], BF16, kind="Internal").ap()
    xg1 = nc.dram_tensor("xg1", [NE * CAP1, XROW], BF16, kind="Internal").ap()
    o1 = nc.dram_tensor("o1", [NE * CAP1 + 1, D], BF16, kind="Internal").ap()
    xg2 = nc.dram_tensor("xg2", [NE * CAP2, XROW], BF16, kind="Internal").ap()
    o2 = nc.dram_tensor("o2", [NE * CAP2 + 1, D], BF16, kind="Internal").ap()
    wgub = nc.dram_tensor("wgub", [NE, 2, 128, 3 * 4096], BF16, kind="Internal").ap()
    wdb = nc.dram_tensor("wdb", [NE, 128, NJ * 2048], BF16, kind="Internal").ap()

    with ExitStack() as ctx:
        E = ctx.enter_context
        p = Prog(nc, ctx)
        te, dv, sc, gp, sy = nc.tensor, nc.vector, nc.scalar, nc.gpsimd, nc.sync

        x_sb = None
        hfm = None
        htm = E(nc.sbuf_tensor("htm", [128, D], BF16))
        stat = E(nc.sbuf_tensor("stat", [128, 16], F32))
        idn = E(nc.sbuf_tensor("idn", [128, 128], BF16))
        onesr = E(nc.sbuf_tensor("onesr", [1, 128], BF16))
        onesrf = E(nc.sbuf_tensor("onesrf", [1, 128], F32))
        onecb = E(nc.sbuf_tensor("onecb", [128, 1], BF16))
        trius = E(nc.sbuf_tensor("trius", [128, 128], BF16))
        iot32 = E(nc.sbuf_tensor("iot32", [128, 32], F32))
        tbc = E(nc.sbuf_tensor("tbc", [128, 32], F32))
        bigp = E(nc.sbuf_tensor("bigp", [128, 1], F32))
        epsc = E(nc.sbuf_tensor("epsc", [128, 1], F32))
        bqs = E(nc.sbuf_tensor("bqs", [128, 16], F32))
        bks = E(nc.sbuf_tensor("bks", [128, 16], F32))
        bvs = E(nc.sbuf_tensor("bvs", [128, 16], F32))
        gws = E(nc.sbuf_tensor("gws", [128, 512], BF16))
        comb = E(nc.sbuf_tensor("comb", [128, NTT * 32], F32))
        rt = E(nc.sbuf_tensor("rt", [128, 512], F32))
        ind_sb = E(nc.sbuf_tensor("ind_sb", [128, 4], F32))
        mcol = E(nc.sbuf_tensor("mcol", [128, 12], F32))
        mcolb = E(nc.sbuf_tensor("mcolb", [128, 1], BF16))
        mrow = E(nc.sbuf_tensor("mrow", [1, 128], BF16))
        rsum = E(nc.sbuf_tensor("rsum", [128, 1], F32))
        cstage = E(nc.sbuf_tensor("cstage", [128, 128], BF16))
        gidxT = E(nc.sbuf_tensor("gidxT", [128, 4 * NTT * 8], U32))
        off1 = E(nc.sbuf_tensor("off1", [1, 32], F32))
        ridx = E(nc.sbuf_tensor("ridx", [128, 16], F32))
        ridxu = E(nc.sbuf_tensor("ridxu", [128, 16], U32))

        with nc.Block():
            p.new("c_ld")
            p.new("xg_ld")
            p.new("xsc")
            p.new("zr")

            def load_x_group(pp, q=None):
                i = (q or gp).dma_start(
                    x_sb[:, :].rearrange("p (t d) -> p t d", t=NTT),
                    x_in[pp].rearrange("(t p) d -> p t d", p=128))
                return p.inc(i, "xg_ld", 16)

            for dst, srcap in [(idn, ident), (onesr, ones_row), (onesrf, ones_row_f),
                               (onecb, onec_bf), (trius, triu_s), (iot32, iota32),
                               (tbc, tbc_in), (bigp, bigp_in), (epsc, eps_col),
                               (bqs, bq), (bks, bk), (bvs, bv), (gws, gw),
                               (ind_sb, ind_in)]:
                i = gp.dma_start(dst[:, :], srcap)
                p.inc(i, "c_ld", 16)
            for e in ("pe", "dve", "act", "sync", "gp"):
                p.wait(e, "c_ld")
            i = dv.memset(htm[0:1, :], 0.0)
            zr_d = p.inc(i, "dve")
            p.wait("act", "dve", zr_d)
            i = sc.dma_start(o1[NZ1:NZ1 + 1, :], htm[0:1, :])
            p.inc(i, "zr", 16)
            i = sc.dma_start(o2[NZ2:NZ2 + 1, :], htm[0:1, :])
            zr_c = p.inc(i, "zr", 16)
            p.wait("act", "zr", zr_c)

            bnd1 = gp.alloc_register("bnd1")
            gp.reg_mov(bnd1, NE * CAP1 - 1)
            bnd2 = gp.alloc_register("bnd2")
            gp.reg_mov(bnd2, NE * CAP2 - 1)

            def emit_ln(tp_ps, xst=None, pre_tile=None):
                tp_free = {0: 0, 1: 0}
                a_h_prev = 0
                xst_cp_prev = 0
                for tt in range(NTT):
                    xt = x_sb[:, tt * D:(tt + 1) * D]
                    if pre_tile is not None:
                        d_pre = pre_tile(tt)
                        p.wait("act", "dve", d_pre)
                    ssum = stat[:, 0:1]; ex2 = stat[:, 1:2]; mneg = stat[:, 2:3]
                    nmsq = stat[:, 3:4]; var = stat[:, 4:5]; std = stat[:, 5:6]
                    rstd = stat[:, 6:7]; nmr = stat[:, 7:8]
                    p.wait("act", "pe")
                    p.wait("act", "dve", xst_cp_prev)
                    i = sc.activation(htm[:, :], xt, AF.Square, accum_out=ex2)
                    a_sq = p.inc(i, "act")
                    p.wait("dve", "act", a_h_prev)
                    i = dv.reduce_sum(ssum, xt, axis=AX.X, negate=True)
                    p.inc(i, "dve"); p.wait("dve", "dve")
                    i = dv.tensor_scalar(mneg, ssum, 1.0 / D, None, op0=ALU.mult)
                    p.inc(i, "dve"); p.wait("dve", "dve")
                    i = dv.tensor_scalar(nmsq, mneg, mneg, -1.0, op0=ALU.mult, op1=ALU.mult)
                    p.inc(i, "dve")
                    p.wait("dve", "act", a_sq); p.wait("dve", "dve")
                    i = dv.tensor_scalar(var, ex2, 1.0 / D, None, op0=ALU.mult)
                    p.inc(i, "dve"); p.wait("dve", "dve")
                    i = dv.tensor_add(var, var, nmsq)
                    d_var = p.inc(i, "dve")
                    p.wait("act", "dve", d_var)
                    i = sc.activation(std, var, AF.Sqrt, bias=epsc[:, :])
                    a_std = p.inc(i, "act")
                    p.wait("dve", "act", a_std)
                    i = dv.reciprocal(rstd, std)
                    p.inc(i, "dve"); p.wait("dve", "dve")
                    i = dv.tensor_scalar(nmr, mneg, rstd, None, op0=ALU.mult)
                    d_nmr = p.inc(i, "dve")
                    p.wait("act", "dve", d_nmr)
                    p.wait("act", "act", a_sq)
                    i = sc.activation(htm[:, :], xt, AF.Identity, bias=nmr, scale=rstd)
                    a_h = p.inc(i, "act")
                    a_h_prev = a_h
                    if xst is not None:
                        p.wait("dve", "act", a_h)
                        p.wait("dve", "xsc")
                        i = dv.tensor_copy(xst[:, tt * XROW: tt * XROW + D],
                                           htm[:, :])
                        xst_cp_prev = p.inc(i, "dve")
                    p.wait("pe", "act", a_h)
                    for kc in range(NKC):
                        bank = kc % 2
                        p.wait("pe", "dve", tp_free[bank])
                        i = te.transpose(tp_ps[bank][:, 0:128],
                                         htm[:, kc * 128:(kc + 1) * 128], idn[:, :])
                        t_pe = p.inc(i, "pe")
                        p.wait("dve", "pe", t_pe)
                        i = dv.tensor_copy(
                            hfm[:, kc * NTOK + tt * 128: kc * NTOK + (tt + 1) * 128],
                            tp_ps[bank][:, 0:128])
                        tp_free[bank] = p.inc(i, "dve")

            def emit_routing(lg_ps, rp=None, xst=None, grp=0, rc=None):
                probs = rt[:, 0:32]; vals = rt[:, 32:40]; cum = rt[:, 40:48]
                lt = rt[:, 48:56]; act8 = rt[:, 56:64]; rw = rt[:, 64:72]
                smax = rt[:, 72:73]; ssum = rt[:, 73:74]; sinv = rt[:, 74:75]
                s8 = rt[:, 75:76]; den = rt[:, 76:77]; dinv = rt[:, 77:78]
                s2 = rt[:, 78:79]; den2 = rt[:, 79:80]; d2i = rt[:, 80:81]
                sca = rt[:, 81:82]
                mk = [rt[:, 96 + k * 32: 128 + k * 32] for k in range(8)]
                m32 = rt[:, 352:384]; slotg = rt[:, 384:416]
                tmp32 = rt[:, 416:448]
                ek = rt[:, 448:449]; sel = rt[:, 449:450]; base = rt[:, 450:451]
                t1 = rt[:, 451:452]; t2 = rt[:, 452:453]

                def dstep(inst):
                    c = p.inc(inst, "dve")
                    p.wait("dve", "dve", c)
                    return c

                for tt in range(NTT):
                    p.wait("pe", "dve")
                    p.wait("pe", "act")
                    for kc in range(NKC):
                        i = te.matmul(lg_ps[:, 0:32],
                                      hfm[:, kc * NTOK + tt * 128: kc * NTOK + (tt + 1) * 128],
                                      gws[:, kc * 32:(kc + 1) * 32],
                                      start=(kc == 0), stop=(kc == NKC - 1))
                    m_pe = p.inc(i, "pe")
                    p.wait("dve", "pe", m_pe)
                    dstep(dv.reduce_max(smax, lg_ps[:, 0:32], axis=AX.X, negate=True))
                    p.wait("act", "dve")
                    p.wait("act", "pe", m_pe)
                    i = sc.activation(probs, lg_ps[:, 0:32], AF.Exp, bias=smax,
                                      accum_out=ssum)
                    a = p.inc(i, "act")
                    p.wait("dve", "act", a)
                    dstep(dv.reciprocal(sinv, ssum))
                    dstep(dv.tensor_scalar(probs, probs, sinv, None, op0=ALU.mult))
                    dstep(dv.tensor_mul(probs, probs, tbc[:, :]))
                    for k in range(8):
                        dstep(dv.reduce_max(vals[:, k:k + 1], probs, axis=AX.X))
                        dstep(dv.tensor_scalar(mk[k], probs, vals[:, k:k + 1], None,
                                               op0=ALU.is_equal))
                        dstep(dv.scalar_tensor_tensor(probs, mk[k], -2.0, probs,
                                                      op0=ALU.mult, op1=ALU.add))
                    dstep(dv.reduce_sum(s8, vals, axis=AX.X))
                    dstep(dv.tensor_scalar(den, s8, 1e-12, None, op0=ALU.max))
                    dstep(dv.reciprocal(dinv, den))
                    dstep(dv.tensor_scalar(cum, vals, dinv, None, op0=ALU.mult))
                    for k in range(1, 8):
                        dstep(dv.tensor_add(cum[:, k:k + 1], cum[:, k:k + 1],
                                            cum[:, k - 1:k]))
                    dstep(dv.tensor_scalar(lt, cum, 0.8, None, op0=ALU.is_lt))
                    dstep(dv.memset(act8[:, 0:1], 1.0))
                    dstep(dv.tensor_copy(act8[:, 1:8], lt[:, 0:7]))
                    dstep(dv.tensor_mul(rw, vals, act8))
                    dstep(dv.reduce_sum(s2, rw, axis=AX.X))
                    dstep(dv.tensor_scalar(den2, s2, 1e-12, None, op0=ALU.max))
                    dstep(dv.reciprocal(d2i, den2))
                    dstep(dv.tensor_scalar(sca, d2i, 0.5, None, op0=ALU.mult))
                    dstep(dv.tensor_scalar(rw, rw, sca, None, op0=ALU.mult))
                    ctt = comb[:, tt * 32:(tt + 1) * 32]
                    dstep(dv.tensor_scalar(ctt, mk[0], rw[:, 0:1], None, op0=ALU.mult))
                    for k in range(1, 8):
                        dstep(dv.scalar_tensor_tensor(ctt, mk[k], rw[:, k:k + 1], ctt,
                                                      op0=ALU.mult, op1=ALU.add))
                    if rp is None:
                        continue
                    dstep(dv.tensor_copy(
                        xst[:, tt * XROW + D: tt * XROW + D + 32], ctt))
                    dstep(dv.tensor_scalar(m32, ctt, 0.0, None, op0=ALU.is_gt))
                    d_mbf = dstep(dv.tensor_copy(htm[:, 0:32], m32))
                    p.wait("pe", "dve", d_mbf)
                    i = te.matmul(rp[:, 0:32], trius[:, :], htm[:, 0:32],
                                  start=True, stop=True)
                    p.inc(i, "pe")
                    i = te.matmul(rp[0:1, 64:96], onecb[:, :], htm[:, 0:32],
                                  start=True, stop=True)
                    p.inc(i, "pe")
                    i = te.matmul(rp[:, 128:160], onesrf[:, :], off1[0:1, :],
                                  start=True, stop=True)
                    bc_pe = p.inc(i, "pe")
                    p.wait("dve", "pe", bc_pe)
                    dstep(dv.tensor_copy(tmp32, rp[:, 0:32]))
                    dstep(dv.tensor_add(slotg, tmp32, rp[:, 128:160]))
                    dstep(dv.tensor_add(off1[0:1, :], off1[0:1, :], rp[0:1, 64:96]))
                    gcols = ridx[:, 0:8]
                    scols = ridx[:, 8:16]
                    for k in range(8):
                        dstep(dv.tensor_mul(tmp32, mk[k], iot32[:, :]))
                        dstep(dv.reduce_sum(ek, tmp32, axis=AX.X))
                        dstep(dv.tensor_mul(tmp32, mk[k], slotg))
                        dstep(dv.reduce_sum(sel, tmp32, axis=AX.X))
                        dstep(dv.scalar_tensor_tensor(base, ek, float(rc["cap"]), sel,
                                                      op0=ALU.mult, op1=ALU.add))
                        ak = act8[:, k:k + 1]
                        dstep(dv.tensor_scalar(t1, base, float(-rc["nz"]), None,
                                               op0=ALU.add))
                        dstep(dv.tensor_mul(t1, t1, ak))
                        dstep(dv.tensor_scalar(gcols[:, k:k + 1], t1, float(rc["nz"]),
                                               None, op0=ALU.add))
                        dstep(dv.tensor_sub(t2, base, bigp[:, :]))
                        dstep(dv.tensor_mul(t2, t2, ak))
                        dstep(dv.tensor_add(scols[:, k:k + 1], t2, bigp[:, :]))
                    gc_u = gidxT[:, (grp * NTT + tt) * 8:(grp * NTT + tt) * 8 + 8]
                    dstep(dv.tensor_copy(gc_u, gcols))
                    d_sc = dstep(dv.tensor_copy(ridxu[:, 0:8], scols))
                    p.wait("gp", "dve", d_sc)
                    for k in range(8):
                        i = gp.indirect_dma_start(
                            rc["xg"],
                            IndirectOffsetOnAxis(ap=ridxu[:, k:k + 1], axis=0),
                            xst[:, tt * XROW:(tt + 1) * XROW], None,
                            bounds_check=rc["bnd"], oob_is_err=False)
                        p.inc(i, "xsc", 16)

            def emit_ffn_routed(tag, cap, sts, xg_ap, o_ap, n_exp, f32_src=True):
                nsub_tot = cap // 128
                subs_of_st = [n // 128 for (_, n) in sts]
                with ExitStack() as fs:
                    Ef = fs.enter_context
                    gin = Ef(nc.sbuf_tensor(f"gin{tag}", [128, 4 * XROW], BF16))
                    ginT = Ef(nc.sbuf_tensor(f"ginT{tag}", [128, NKC * 512], BF16))
                    act_sb = Ef(nc.sbuf_tensor(f"acts{tag}", [128, NJ * cap], BF16))
                    cbuf = Ef(nc.sbuf_tensor(f"cbuf{tag}", [128, 16], F32))
                    sil = Ef(nc.sbuf_tensor(f"sil{tag}", [128, 512], F32))
                    outst = [Ef(nc.sbuf_tensor(f"o{tag}{s}", [128, D], BF16))
                             for s in range(2)]
                    wgu_sb = [Ef(nc.sbuf_tensor(f"wgu{tag}{s}", [128, 3 * 4096], BF16))
                              for s in range(2)]
                    wd_sb = Ef(nc.sbuf_tensor(f"wd{tag}", [128, NJ * 2048], BF16))
                    g_ps = Ef(nc.psum_tensor(f"g{tag}", [128, 512], F32))
                    u_ps = Ef(nc.psum_tensor(f"u{tag}", [128, 512], F32))
                    b_ps = Ef(nc.psum_tensor(f"b{tag}", [128, 2048], F32))
                    tr_ps = [Ef(nc.psum_tensor(f"tr{tag}{s}", [128, 512], BF16))
                             for s in range(2)]

                    sgu = [p.new(f"wgu{tag}{s}") for s in range(2)]
                    swd = p.new(f"swd{tag}")
                    sgin = p.new(f"gin{tag}")
                    so1 = p.new(f"so{tag}")
                    swb = p.new(f"swb{tag}")
                    half_free = {}
                    half_ld = {}
                    wb_done = {}
                    wd_free = {}
                    wd_ld = {}
                    wdwb_done = {}
                    gin_free = {}
                    gin_free_dve = {}
                    gin_ld = {}

                    nst = len(sts)

                    def load_half(e, h):
                        idx = 2 * e + h
                        if idx >= 2:
                            p.wait("gp", "pe", half_free[idx - 2])
                            if f32_src:
                                p.wait("gp", swb, wb_done[idx - 2])
                        if f32_src:
                            i = gp.dma_start(
                                wgu_sb[idx % 2][:, :].rearrange("p (j n) -> p j n", j=3),
                                wgu[e, 3 * h:3 * h + 3].rearrange("j p n -> p j n"))
                        else:
                            i = gp.dma_start(wgu_sb[idx % 2][:, :], wgub[e, h])
                        half_ld[idx] = (sgu[idx % 2], p.inc(i, sgu[idx % 2], 16))

                    def load_wd(e):
                        qe, q = ("gp", gp) if f32_src else ("sync", sy)
                        if e >= 1:
                            p.wait(qe, "pe", wd_free[e - 1])
                            if f32_src:
                                p.wait(qe, swb, wdwb_done[e - 1])
                        i = q.dma_start(wd_sb[:, :], wdn[e] if f32_src else wdb[e])
                        wd_ld[e] = p.inc(i, swd, 16)

                    def load_gin(gst):
                        e, sti = divmod(gst, nst)
                        if e >= n_exp:
                            return
                        st_off, st_n = sts[sti]
                        ns = st_n // 128
                        if gst >= 1:
                            p.wait("sync", "pe", gin_free[gst - 1])
                            p.wait("sync", "dve", gin_free_dve[gst - 1])
                        i = sy.dma_start(
                            gin[:, 0:ns * XROW].rearrange(
                                "p (s c) -> p s c", s=ns),
                            xg_ap[e * cap + st_off: e * cap + st_off + st_n].rearrange(
                                "(s p) c -> p s c", p=128))
                        gin_ld[gst] = (sgin, p.inc(i, sgin, 16))

                    load_half(0, 0)
                    load_half(0, 1)
                    load_wd(0)
                    load_gin(0)

                    a_sg = None
                    a_tt = None
                    tp_free = {0: 0, 1: 0}
                    out_free = {0: 0, 1: 0}
                    d_cb_prev = 0

                    for e in range(n_exp):
                        if f32_src:
                            for h in range(2):
                                idx = 2 * e + h
                                p.wait("act", *half_ld[idx])
                                i = sc.dma_start(wgub[e, h], wgu_sb[idx % 2][:, :])
                                wb_done[idx] = p.inc(i, swb, 16)
                        for sti, (st_off, st_n) in enumerate(sts):
                            gst = e * nst + sti
                            ns = subs_of_st[sti]
                            p.wait("pe", *gin_ld[gst])
                            p.wait("dve", *gin_ld[gst])
                            p.wait("dve", "dve", d_cb_prev)
                            for s in range(ns):
                                i = dv.tensor_copy(
                                    cbuf[:, st_off // 128 + s: st_off // 128 + s + 1],
                                    gin[:, s * XROW + D + e: s * XROW + D + e + 1])
                                d_cb = p.inc(i, "dve")
                            d_cb_prev = d_cb
                            for s in range(ns):
                                for kc in range(NKC):
                                    bank = kc % 2
                                    p.wait("pe", "dve", tp_free[bank])
                                    i = te.transpose(
                                        tr_ps[bank][:, 0:128],
                                        gin[:, s * XROW + kc * 128: s * XROW + (kc + 1) * 128],
                                        idn[:, :])
                                    t_pe = p.inc(i, "pe")
                                    p.wait("dve", "pe", t_pe)
                                    i = dv.tensor_copy(
                                        ginT[:, kc * 512 + s * 128: kc * 512 + (s + 1) * 128],
                                        tr_ps[bank][:, 0:128])
                                    tp_free[bank] = p.inc(i, "dve")
                            gin_free[gst] = p.cnt["pe"]
                            gin_free_dve[gst] = p.cnt["dve"]
                            load_gin(gst + 1)
                            for j in range(NJ):
                                wt = wgu_sb[(2 * e + j // 3) % 2]
                                joff = (j % 3) * 4096
                                p.wait("pe", *half_ld[2 * e + j // 3])
                                if a_sg is not None:
                                    p.wait("pe", "act", a_sg)
                                p.wait("pe", "dve")
                                for kc in range(NKC):
                                    i = te.matmul(
                                        g_ps[:, 0:st_n],
                                        wt[:, joff + kc * 256: joff + kc * 256 + 128],
                                        ginT[:, kc * 512: kc * 512 + st_n],
                                        start=(kc == 0), stop=(kc == NKC - 1))
                                g_pe = p.inc(i, "pe")
                                if a_tt is not None:
                                    p.wait("pe", "dve", a_tt)
                                for kc in range(NKC):
                                    i = te.matmul(
                                        u_ps[:, 0:st_n],
                                        wt[:, joff + kc * 256 + 128: joff + kc * 256 + 256],
                                        ginT[:, kc * 512: kc * 512 + st_n],
                                        start=(kc == 0), stop=(kc == NKC - 1))
                                u_pe = p.inc(i, "pe")
                                if sti == nst - 1 and (j == 2 or j == NJ - 1):
                                    half_free[2 * e + (0 if j == 2 else 1)] = u_pe
                                p.wait("act", "pe", g_pe)
                                if a_tt is not None:
                                    p.wait("act", "dve", a_tt)
                                i = sc.activation(sil[:, 0:st_n], g_ps[:, 0:st_n],
                                                  AF.Silu)
                                a_sg = p.inc(i, "act")
                                p.wait("dve", "pe", u_pe)
                                p.wait("dve", "act", a_sg)
                                i = dv.tensor_mul(
                                    act_sb[:, j * cap + st_off: j * cap + st_off + st_n],
                                    sil[:, 0:st_n], u_ps[:, 0:st_n])
                                a_tt = p.inc(i, "dve")
                        if e + 1 < n_exp:
                            load_half(e + 1, 0)
                            load_half(e + 1, 1)
                        p.wait("pe", swd, wd_ld[e])
                        if f32_src:
                            p.wait("sync", swd, wd_ld[e])
                            i = sy.dma_start(wdb[e], wd_sb[:, :])
                            wdwb_done[e] = p.inc(i, swb, 16)
                        p.wait("pe", "dve", a_tt)
                        for sub in range(nsub_tot):
                            oslot = sub % 2
                            for dq in range(4):
                                if dq == 0:
                                    p.wait("pe", so1, out_free[oslot])
                                    p.wait("pe", "dve")
                                for j in range(NJ):
                                    i = te.matmul(
                                        b_ps[:, dq * 512:(dq + 1) * 512],
                                        act_sb[:, j * cap + sub * 128: j * cap + (sub + 1) * 128],
                                        wd_sb[:, j * 2048 + dq * 512: j * 2048 + (dq + 1) * 512],
                                        start=(j == 0), stop=(j == NJ - 1))
                                b_pe = p.inc(i, "pe")
                                p.wait("dve", "pe", b_pe)
                                p.wait("dve", so1, out_free[oslot])
                                i = dv.tensor_scalar(
                                    outst[oslot][:, dq * 512:(dq + 1) * 512],
                                    b_ps[:, dq * 512:(dq + 1) * 512],
                                    cbuf[:, sub:sub + 1], None, op0=ALU.mult)
                                d_ev = p.inc(i, "dve")
                            p.wait("act", "dve", d_ev)
                            i = sc.dma_start(
                                o_ap[e * cap + sub * 128: e * cap + (sub + 1) * 128, :],
                                outst[oslot][:, :])
                            out_free[oslot] = p.inc(i, so1, 16)
                        wd_free[e] = p.cnt["pe"]
                        if e + 1 < n_exp:
                            load_wd(e + 1)

            def emit_second_gather(o_ap, grp, tag=""):
                with ExitStack() as fs:
                    Ef = fs.enter_context
                    gbuf = [Ef(nc.sbuf_tensor(f"gb{tag}{grp}{s}", [128, D], BF16))
                            for s in range(2)]
                    sgb = p.new(f"sgb{tag}{grp}")
                    gb_free = {0: 0, 1: 0}
                    for tt in range(NTT):
                        xt = x_sb[:, tt * D:(tt + 1) * D]
                        for k in range(8):
                            slot = k % 2
                            p.wait("gp", "dve", gb_free[slot])
                            i = gp.indirect_dma_start(
                                gbuf[slot][:, :], None, o_ap,
                                IndirectOffsetOnAxis(
                                    ap=gidxT[:, (grp * NTT + tt) * 8 + k:
                                             (grp * NTT + tt) * 8 + k + 1],
                                    axis=0))
                            g_ld = p.inc(i, sgb, 16)
                            p.wait("dve", sgb, g_ld)
                            p.wait("dve", "dve")
                            i = dv.tensor_add(xt, xt, gbuf[slot][:, :])
                            gb_free[slot] = p.inc(i, "dve")

            swq = [p.new(f"wqk{s}") for s in range(2)]
            sstg = [p.new(f"stg{s}") for s in range(3)]
            sx1 = p.new("sx1")
            stg_free = {}

            def emit_proj(specs, tag):
                with ExitStack() as fs:
                    Ef = fs.enter_context
                    wbuf = [Ef(nc.sbuf_tensor(f"wqkv{tag}{s}", [128, 8192], BF16))
                            for s in range(2)]
                    stg = [Ef(nc.sbuf_tensor(f"qkvstg{tag}{s}", [128, NTOK], BF16))
                           for s in range(3)]
                    mg_ps = [Ef(nc.psum_tensor(f"mg{tag}{s}", [128, NTOK], F32))
                             for s in range(4)]
                    wld = {}
                    grp_evac = 0
                    i = gp.dma_start(wbuf[0][:, :], specs[0][1][0])
                    wld[0] = (swq[0], p.inc(i, swq[0], 16))
                    gidx = 0
                    for si, (nm, wsrc, bsb, roff, coff, dst) in enumerate(specs):
                        for mg in range(4):
                            slot = gidx % 2
                            nxt = (wsrc, mg + 1) if mg < 3 else (
                                (specs[si + 1][1], 0) if si + 1 < len(specs) else None)
                            if nxt is not None:
                                p.wait("gp", "pe")
                                i = gp.dma_start(wbuf[1 - slot][:, :], nxt[0][nxt[1]])
                                wld[gidx + 1] = (swq[1 - slot], p.inc(i, swq[1 - slot], 16))
                            p.wait("pe", *wld[gidx])
                            p.wait("pe", "act", grp_evac)
                            last_evac = 0
                            for mc in range(4):
                                for half in range(2):
                                    for kc in range(NKC):
                                        i = te.matmul(
                                            mg_ps[mc][:, half * 512:(half + 1) * 512],
                                            wbuf[slot][:, kc * 512 + mc * 128: kc * 512 + (mc + 1) * 128],
                                            hfm[:, kc * NTOK + half * 512: kc * NTOK + (half + 1) * 512],
                                            start=(kc == 0), stop=(kc == NKC - 1))
                                mc_pe = p.inc(i, "pe")
                                m_abs = mg * 4 + mc
                                sslot = m_abs % 3
                                p.wait("act", "pe", mc_pe)
                                if sslot in stg_free:
                                    p.wait("act", sstg[sslot], stg_free[sslot])
                                i = sc.activation(stg[sslot][:, :], mg_ps[mc][:, :],
                                                  AF.Identity, bias=bsb[:, m_abs:m_abs + 1])
                                a_e = p.inc(i, "act")
                                last_evac = a_e
                                p.wait("sync", "act", a_e)
                                i = sy.dma_start(
                                    dst[roff + m_abs * 128: roff + (m_abs + 1) * 128,
                                        coff:coff + NTOK],
                                    stg[sslot][:, :])
                                stg_free[sslot] = p.inc(i, sstg[sslot], 16)
                            grp_evac = last_evac
                            gidx += 1

            def emit_attention(n_heads_):
                with ExitStack() as fs:
                    Ef = fs.enter_context
                    qh = [Ef(nc.sbuf_tensor(f"qh{s}", [128, NTOK], BF16)) for s in range(2)]
                    kh = [Ef(nc.sbuf_tensor(f"kh{s}", [128, 4096], BF16)) for s in range(2)]
                    vh = [Ef(nc.sbuf_tensor(f"vh{s}", [128, 32 * VST], BF16))
                          for s in range(2)]
                    pt = [Ef(nc.sbuf_tensor(f"pt{s}", [128, 512], BF16)) for s in range(2)]
                    masks = Ef(nc.sbuf_tensor("masks", [128, 8 * 512], BF16))
                    smsk = p.new("msk")
                    i = gp.dma_start(masks[:, :].rearrange("p (s n) -> p s n", s=8),
                                     amask.rearrange("s p n -> p s n"))
                    mask_ld = p.inc(i, smsk, 16)
                    p.wait("dve", smsk, mask_ld)
                    p1_ps = [Ef(nc.psum_tensor(f"p1{s}", [128, 512], F32)) for s in range(2)]
                    sT_ps = [Ef(nc.psum_tensor(f"sT{s}", [128, 512], F32)) for s in range(2)]
                    cs_ps = [Ef(nc.psum_tensor(f"cs{s}", [128, 512], F32)) for s in range(2)]
                    tr_ps = [Ef(nc.psum_tensor(f"tr{s}", [128, 1024], BF16)) for s in range(2)]

                    for s in range(2):
                        i = dv.memset(
                            vh[s][:, :].rearrange("p (c n) -> p c n", c=32)[:, :, 128:129],
                            1.0)
                        d_ones = p.inc(i, "dve")
                    p.wait("sync", "dve", d_ones)
                    p.wait("pe", "dve", d_ones)

                    skq = [p.new(f"kq{s}") for s in range(2)]
                    svh = [p.new(f"vh{s}") for s in range(2)]
                    kld = {}
                    vld = {}
                    head_pe_done = {}

                    def load_head(h):
                        slot = h % 2
                        if h >= 2:
                            p.wait("sync", "pe", head_pe_done[h - 2])
                        i = sy.dma_start(qh[slot][:, :], q_dram[h * 128:(h + 1) * 128, :])
                        p.inc(i, skq[slot], 16)
                        i = sy.dma_start(kh[slot][:, :],
                                         kv_dram[h * 128:(h + 1) * 128, :])
                        p.inc(i, skq[slot], 16)
                        kld[h] = (skq[slot], p.cnt[skq[slot]])
                        for cc in range(32):
                            i = sy.dma_start(
                                vh[slot][:, cc * VST: cc * VST + 128],
                                kv_dram[NH * 128 + h * 128: NH * 128 + (h + 1) * 128,
                                        cc * 128:(cc + 1) * 128],
                                transpose=True)
                            p.inc(i, svh[slot], 16)
                        vld[h] = (svh[slot], p.cnt[svh[slot]])

                    for s3 in range(3):
                        p.wait("sync", f"stg{s3}")
                        p.wait("act", f"stg{s3}")
                    load_head(0)
                    cs_free = {0: 0, 1: 0}
                    pt_free = {0: 0, 1: 0}
                    p1_free = {0: 0, 1: 0}
                    sT_free = {0: 0, 1: 0}
                    for h in range(n_heads_):
                        slot = h % 2
                        if h + 1 < n_heads_:
                            load_head(h + 1)
                        p.wait("pe", *kld[h])
                        p.wait("pe", *vld[h])
                        kh4 = kh[slot][:, :].rearrange("p (r j c) -> p r j c", r=4, j=8)
                        for s in range(8):
                            C = C_S[s]
                            qsl = qh[slot][:, s * 128:(s + 1) * 128]
                            for g in range(C // 4):
                                bank = g % 2
                                p.wait("pe", "dve", p1_free[bank])
                                i = te.matmul(p1_ps[bank][:, :], qsl,
                                              kh4[:, :, J_OF_PIECE[g], :],
                                              start=True, stop=True)
                                pe_c = p.inc(i, "pe")
                                p.wait("dve", "pe", pe_c)
                                i = dv.reduce_max(mcol[:, g:g + 1], p1_ps[bank][:, :],
                                                  axis=AX.X)
                                p1_free[bank] = p.inc(i, "dve")
                            p.wait("dve", "dve")
                            i = dv.reduce_max(mcol[:, 8:9], mcol[:, 0:C // 4],
                                              axis=AX.X, negate=True)
                            p.inc(i, "dve"); p.wait("dve", "dve")
                            i = dv.tensor_copy(mcolb[:, :], mcol[:, 8:9])
                            d_mb = p.inc(i, "dve")
                            p.wait("pe", "dve", d_mb)
                            i = te.transpose(tr_ps[0][0:1, 0:128], mcolb[:, :], idn[:, :])
                            t_pe = p.inc(i, "pe")
                            p.wait("dve", "pe", t_pe)
                            i = dv.tensor_copy(mrow[:, :], tr_ps[0][0:1, 0:128])
                            d_mr = p.inc(i, "dve")
                            p.wait("pe", "dve", d_mr)
                            ngr = C // 4
                            cb = s % 2
                            ctx_pe = 0
                            for g in range(ngr):
                                bank = g % 2
                                p.wait("pe", "act", sT_free[bank])
                                for ci in range(4):
                                    k = g * 4 + ci
                                    col = _khcol(k)
                                    i = te.matmul(sT_ps[bank][:, ci * 128:(ci + 1) * 128],
                                                  kh[slot][:, col * 128:(col + 1) * 128],
                                                  qsl, start=(ci == 0), stop=False)
                                for ci in range(4):
                                    i = te.matmul(sT_ps[bank][:, ci * 128:(ci + 1) * 128],
                                                  onesr[:, :], mrow[:, :],
                                                  start=False, stop=(ci == 3))
                                g_pe = p.inc(i, "pe")
                                p.wait("act", "pe", g_pe)
                                p.wait("act", "pe", pt_free[bank])
                                i = sc.activation(pt[bank][:, :], sT_ps[bank][:, :],
                                                  AF.Exp, scale=ISQ)
                                a_exp = p.inc(i, "act")
                                sT_free[bank] = a_exp
                                if g == ngr - 1:
                                    p.wait("dve", "act", a_exp)
                                    i = dv.tensor_mul(pt[bank][:, :], pt[bank][:, :],
                                                      masks[:, s * 512:(s + 1) * 512])
                                    d_msk = p.inc(i, "dve")
                                    p.wait("pe", "dve", d_msk)
                                else:
                                    p.wait("pe", "act", a_exp)
                                if g == 0:
                                    p.wait("pe", "dve", cs_free[cb])
                                for ci in range(4):
                                    k = g * 4 + ci
                                    col = _khcol(k)
                                    i = te.matmul(cs_ps[cb][:, 0:129],
                                                  pt[bank][:, ci * 128:(ci + 1) * 128],
                                                  vh[slot][:, col * VST: col * VST + 129],
                                                  start=(k == 0), stop=(k == C - 1))
                                ctx_pe = p.inc(i, "pe")
                                pt_free[bank] = ctx_pe
                            p.wait("dve", "pe", ctx_pe)
                            i = dv.reciprocal(rsum[:, :], cs_ps[cb][:, 128:129])
                            p.inc(i, "dve"); p.wait("dve", "dve")
                            i = dv.tensor_scalar(cstage[:, :], cs_ps[cb][:, 0:128],
                                                 rsum[:, :], None, op0=ALU.mult)
                            d_cs = p.inc(i, "dve")
                            cs_free[cb] = d_cs
                            p.wait("pe", "dve", d_cs)
                            i = te.transpose(tr_ps[1][:, 0:128], cstage[:, :], idn[:, :])
                            t2 = p.inc(i, "pe")
                            p.wait("dve", "pe", t2)
                            i = dv.tensor_copy(
                                hfm[:, h * NTOK + s * 128: h * NTOK + (s + 1) * 128],
                                tr_ps[1][:, 0:128])
                            p.inc(i, "dve")
                        head_pe_done[h] = p.cnt["pe"]

            def emit_outproj():
                with ExitStack() as fs:
                    Ef = fs.enter_context
                    wo_sb = Ef(nc.sbuf_tensor("wo_sb", [128, 16384], BF16))
                    bos = Ef(nc.sbuf_tensor("bos", [128, D], F32))
                    op_ps = [Ef(nc.psum_tensor(f"op{s}", [128, D], F32)) for s in range(2)]
                    sbo = p.new("bob")
                    swo = p.new("wol")
                    i = gp.dma_start(bos[:, :], bo_bc)
                    bo_ld = p.inc(i, sbo, 16)
                    p.wait("dve", sbo, bo_ld)
                    for tt in range(NTT):
                        p.wait("dve", "dve")
                        i = dv.tensor_add(x_sb[:, tt * D:(tt + 1) * D],
                                          x_sb[:, tt * D:(tt + 1) * D], bos[:, :])
                        p.inc(i, "dve")
                    op_free = {0: 0, 1: 0}
                    for halfk in range(2):
                        if halfk:
                            p.wait("gp", "pe")
                        i = gp.dma_start(
                            wo_sb[:, :].rearrange("p (k n) -> p k n", k=8),
                            wo[halfk].rearrange("p (k n) -> p k n", k=8))
                        ld = p.inc(i, swo, 16)
                        p.wait("pe", swo, ld)
                        for tt in range(NTT):
                            bank = tt % 2
                            p.wait("pe", "dve", op_free[bank])
                            for ns in range(4):
                                for kc8 in range(8):
                                    kc = halfk * 8 + kc8
                                    i = te.matmul(
                                        op_ps[bank][:, ns * 512:(ns + 1) * 512],
                                        hfm[:, kc * NTOK + tt * 128: kc * NTOK + (tt + 1) * 128],
                                        wo_sb[:, kc8 * 2048 + ns * 512: kc8 * 2048 + (ns + 1) * 512],
                                        start=(kc8 == 0), stop=(kc8 == 7))
                            o_pe = p.inc(i, "pe")
                            p.wait("dve", "pe", o_pe)
                            p.wait("dve", "dve")
                            i = dv.tensor_add(x_sb[:, tt * D:(tt + 1) * D],
                                              x_sb[:, tt * D:(tt + 1) * D],
                                              op_ps[bank][:, :])
                            op_free[bank] = p.inc(i, "dve")

            def ln_pass(name, xst=None, pre_tile=None, bar=True):
                with ExitStack() as ps0:
                    tp = [ps0.enter_context(nc.psum_tensor(f"{name}{s}", [128, 1024], BF16))
                          for s in range(2)]
                    emit_ln(tp, xst=xst, pre_tile=pre_tile)
                    if bar:
                        p.barrier()

            with ExitStack() as p1s:
                x_sb = p1s.enter_context(
                    nc.sbuf_tensor("x_sbA", [128, NTT * D], F32))
                hfm = p1s.enter_context(
                    nc.sbuf_tensor("hfmA", [128, NKC * NTOK], BF16))
                xst = p1s.enter_context(
                    nc.sbuf_tensor("xst", [128, NTT * XROW], BF16))
                i = dv.memset(off1[0:1, :], 0.0)
                p.inc(i, "dve")
                for cp in range(4):
                    if cp > 0:
                        p.barrier()
                    xg = load_x_group(cp)
                    for e in ("pe", "dve", "act"):
                        p.wait(e, "xg_ld", xg)
                    ln_pass(f"lnA{cp}", xst=xst)
                    with ExitStack() as ps1:
                        rp = ps1.enter_context(
                            nc.psum_tensor(f"rp{cp}", [128, 512], F32))
                        lg = ps1.enter_context(
                            nc.psum_tensor(f"lg1{cp}", [128, 512], F32))
                        emit_routing(lg, rp=rp, xst=xst, grp=cp,
                                     rc=dict(cap=CAP1, xg=xg1, bnd=bnd1, nz=NZ1))
                        p.barrier()
                p.barrier()

            emit_ffn_routed("f1", CAP1, STS1, xg1, o1, n_experts)
            p.barrier()

            p2s = ctx.enter_context(ExitStack())
            x_sb = p2s.enter_context(
                nc.sbuf_tensor("x_sbB", [128, NTT * D], F32))
            hs = ExitStack()
            hfm = hs.enter_context(
                nc.sbuf_tensor("hfmB", [128, NKC * NTOK], BF16))

            with ExitStack() as g2s:
                gbuf = [g2s.enter_context(nc.sbuf_tensor(f"gbP{s}", [128, D], BF16))
                        for s in range(6)]
                sgb = p.new("sgbP")
                gb_state = {s: 0 for s in range(6)}

                def mk_pre(grp):
                    def pre(tt):
                        xt = x_sb[:, tt * D:(tt + 1) * D]
                        last = 0
                        for k in range(8):
                            slot = (tt * 8 + k) % 6
                            p.wait("gp", "dve", gb_state[slot])
                            i = gp.indirect_dma_start(
                                gbuf[slot][:, :], None, o1,
                                IndirectOffsetOnAxis(
                                    ap=gidxT[:, (grp * NTT + tt) * 8 + k:
                                             (grp * NTT + tt) * 8 + k + 1],
                                    axis=0))
                            g_ld = p.inc(i, sgb, 16)
                            p.wait("dve", sgb, g_ld)
                            p.wait("dve", "dve")
                            i = dv.tensor_add(xt, xt, gbuf[slot][:, :])
                            gb_state[slot] = p.inc(i, "dve")
                            last = gb_state[slot]
                        return last
                    return pre

                xg = load_x_group(0)
                for cp in range(4):
                    for e in ("pe", "dve", "act"):
                        p.wait(e, "xg_ld", xg)
                    ln_pass(f"lnB{cp}", pre_tile=mk_pre(cp), bar=False)
                    p.wait("sync", "dve")
                    i = sy.dma_start(
                        x1_dram[cp].rearrange("(t p) d -> p t d", p=128),
                        x_sb[:, :].rearrange("p (t d) -> p t d", t=NTT))
                    p.inc(i, sx1, 16)
                    p.wait("pe", "dve")
                    act_mark = p.cnt["act"]
                    dve_mark = p.cnt["dve"]
                    emit_proj([("k", wk, bks, 0, cp * NTOK, kv_dram),
                               ("v", wv, bvs, NH * 128, cp * NTOK, kv_dram)],
                              f"{cp}")
                    if cp < 3:
                        p.wait("gp", "act", act_mark)
                        p.wait("gp", "dve", dve_mark)
                        p.wait("gp", sx1)
                        xg = load_x_group(cp + 1)
                p.barrier()

            with ExitStack() as bs:
                Eb = bs.enter_context
                xtmp = [Eb(nc.sbuf_tensor(f"xtmp{s}", [128, D], F32)) for s in range(2)]
                sbl = p.new("xbl")
                p.wait("sync", sx1)
                for tt in range(NTT):
                    for cp in range(4):
                        slot = cp % 2
                        p.wait("sync", "dve")
                        i = sy.dma_start(
                            xtmp[slot][:, :],
                            x1_dram[cp, tt * 128:(tt + 1) * 128, :])
                        ldc = p.inc(i, sbl, 16)
                        p.wait("dve", sbl, ldc)
                        xt = x_sb[:, tt * D:(tt + 1) * D]
                        if cp == 0:
                            i = dv.tensor_scalar(xt, xtmp[slot][:, :],
                                                 ind_sb[:, 0:1], None, op0=ALU.mult)
                        else:
                            i = dv.scalar_tensor_tensor(
                                xt, xtmp[slot][:, :], ind_sb[:, cp:cp + 1], xt,
                                op0=ALU.mult, op1=ALU.add)
                        p.inc(i, "dve")
                p.barrier()
            ln_pass("lnQ")
            emit_proj([("q", wq, bqs, 0, 0, q_dram)], "q")
            p.barrier()
            emit_attention(n_heads)
            p.barrier()
            emit_outproj()
            p.barrier()
            i = dv.memset(off1[0:1, :], 0.0)
            p.inc(i, "dve"); p.wait("dve", "dve")
            with ExitStack() as f2s:
                xst2 = f2s.enter_context(
                    nc.sbuf_tensor("xst2", [128, NTT * XROW], BF16))
                ln_pass("lnC", xst=xst2)
                with ExitStack() as ps1:
                    rp2 = ps1.enter_context(
                        nc.psum_tensor("rpF2", [128, 512], F32))
                    lg = ps1.enter_context(nc.psum_tensor("lg2", [128, 512], F32))
                    emit_routing(lg, rp=rp2, xst=xst2, grp=0,
                                 rc=dict(cap=CAP2, xg=xg2, bnd=bnd2, nz=NZ2))
                    p.barrier()
            hs.close()
            emit_ffn_routed("f2", CAP2, STS2, xg2, o2, n_experts, f32_src=False)
            p.barrier()
            emit_second_gather(o2, 0, tag="s2")
            p.barrier()

            p.barrier(("sync",))
            sst = p.new("st")
            i = sy.dma_start(y_out.rearrange("(t p) d -> p t d", p=128),
                             x_sb[:, :].rearrange("p (t d) -> p t d", t=NTT))
            st = p.inc(i, sst, 16)
            p.wait("sync", sst, st)

    return nc


def prep_shared(Wq, bq_, Wk, bk_, Wv, bv_, Wo, bo_, gate_w, gate_up, down):
    def f32(a):
        return np.ascontiguousarray(np.asarray(a, dtype=np.float32))

    gup = f32(gate_up)
    wgu_a = np.empty((NE, NJ, 128, 4096), np.float32)
    for e in range(NE):
        Wt = gup[e].T
        for j in range(NJ):
            blk = np.concatenate([Wt[:, j * 128:(j + 1) * 128],
                                  Wt[:, 768 + j * 128: 768 + (j + 1) * 128]], axis=1)
            wgu_a[e, j] = blk.reshape(16, 128, 256).transpose(1, 0, 2).reshape(128, 4096)
    dn = f32(down)
    wdn_a = np.empty((NE, 128, NJ * 2048), np.float32)
    for e in range(NE):
        Wt = dn[e].T
        wdn_a[e] = Wt.reshape(NJ, 128, 2048).transpose(1, 0, 2).reshape(128, NJ * 2048)
    gwt = f32(gate_w).T
    gw_a = np.ascontiguousarray(
        gwt.reshape(16, 128, 32).transpose(1, 0, 2).reshape(128, 512))

    def qkvw(W):
        Wt = f32(W).T
        out = np.empty((4, 128, 8192), np.float32)
        for mg in range(4):
            blk = Wt[:, mg * 512:(mg + 1) * 512]
            out[mg] = blk.reshape(16, 128, 512).transpose(1, 0, 2).reshape(128, 8192)
        return out

    Wot = f32(Wo).T
    wo_a = np.empty((2, 128, 16384), np.float32)
    for hf in range(2):
        blk = Wot[hf * 1024:(hf + 1) * 1024, :]
        wo_a[hf] = blk.reshape(8, 128, 2048).transpose(1, 0, 2).reshape(128, 16384)

    def bias16(b):
        return np.ascontiguousarray(f32(b).reshape(16, 128).T)

    return dict(
        wgu=wgu_a, wdn=wdn_a, gw=gw_a,
        wq=qkvw(Wq), wk=qkvw(Wk), wv=qkvw(Wv), wo=wo_a,
        bq=bias16(bq_), bk=bias16(bk_), bv=bias16(bv_),
        bo_bc=np.ascontiguousarray(np.tile(f32(bo_)[None, :], (128, 1))),
        ident=np.eye(128, dtype=BF),
        ones_row=np.ones((1, 128), dtype=BF),
        ones_row_f=np.ones((1, 128), np.float32),
        onec_bf=np.ones((128, 1), dtype=BF),
        triu_s=np.triu(np.ones((128, 128), np.float32), 1).astype(BF),
        iota32=np.tile(np.arange(32, dtype=np.float32)[None, :], (128, 1)),
        tbc_in=np.tile(1.0 + np.arange(32, dtype=np.float32)[None, :] * 1e-6,
                       (128, 1)),
        bigp_in=(float(BIG) + np.arange(128, dtype=np.float32))[:, None].copy(),
        eps_col=np.full((128, 1), 1e-5, np.float32),
    )


def core_mask(cp):
    tri = np.triu(np.ones((128, 128), np.float32))
    out = np.zeros((8, 128, 512), np.float32)
    tiles = _tiles_of_core(cp)
    for s in range(8):
        C = C_S[s]
        C_own = tiles[s] + 1
        for idx, k in enumerate(range(C - 4, C)):
            if k < C_own - 1:
                out[s, :, idx * 128:(idx + 1) * 128] = 1.0
            elif k == C_own - 1:
                out[s, :, idx * 128:(idx + 1) * 128] = tri
    return out.astype(BF)


def make_in_maps(inputs):
    x = np.ascontiguousarray(np.asarray(inputs["L0_attn_x"], dtype=np.float32))
    shared = prep_shared(inputs["L0_attn_Wq"], inputs["L0_attn_bq"],
                         inputs["L0_attn_Wk"], inputs["L0_attn_bk"],
                         inputs["L0_attn_Wv"], inputs["L0_attn_bv"],
                         inputs["L0_attn_Wo"], inputs["L0_attn_bo"],
                         inputs["L0_ffn_gate_weight"], inputs["L0_ffn_gate_up_proj"],
                         inputs["L0_ffn_down_proj"])
    xcov = []
    masks = [core_mask(cp) for cp in range(4)]
    for g in range(B):
        xg = x[g].reshape(32, 128, D)
        xcov.append(np.ascontiguousarray(np.stack(
            [xg[_tiles_of_core(p)].reshape(NTOK, D) for p in range(4)])))
    in_maps = []
    for c in range(8):
        cp, g = c % 4, c // 4
        ind = np.zeros((128, 4), np.float32)
        ind[:, cp] = 1.0
        m = dict(shared)
        m["x_in"] = xcov[g]
        m["ind_in"] = ind
        m["amask"] = masks[cp]
        in_maps.append(m)
    return in_maps


def gather_out(results):
    out = np.empty((B, T, D), np.float32)
    for c in range(8):
        cp, g = c % 4, c // 4
        tiles = _tiles_of_core(cp)
        yc = np.asarray(results[c]["y"], dtype=np.float32).reshape(8, 128, D)
        for s, t in enumerate(tiles):
            out[g, t * 128:(t + 1) * 128, :] = yc[s]
    return out


def kernel(**inputs):
    in_maps = make_in_maps(inputs)
    nc = build_program()
    res = run_bass_kernel_spmd(nc, in_maps, core_ids=list(range(8)))
    return gather_out(res.results)


# version 08 (final)
# speedup vs baseline: 1.2402x; 1.0394x over previous
import math
from contextlib import ExitStack

import numpy as np
import ml_dtypes

import concourse.bass as bass
import concourse.mybir as mybir
from concourse.bass import IndirectOffsetOnAxis
from concourse.bass_utils import run_bass_kernel_spmd

F32 = mybir.dt.float32
BF16 = mybir.dt.bfloat16
U32 = mybir.dt.uint32
AF = mybir.ActivationFunctionType
ALU = mybir.AluOpType
AX = mybir.AxisListType
BF = ml_dtypes.bfloat16

D = 2048
B = 2
T = 4096
NTOK = 1024
NTT = 8
NKC = D // 128
NE = 32
INTER = 768
NJ = INTER // 128
NH = 16
DH = 128
ISQ = 1.0 / math.sqrt(DH)
VST = 144

CAP1 = 1024
STS1 = [(0, 512), (512, 512)]
CAP2 = 512
STS2 = [(0, 512)]
XROW = D + 32
NZ1 = NE * CAP1
NZ2 = NE * CAP2
BIG = 1 << 20

C_S = [4, 8, 12, 16, 32, 28, 24, 20]
J_OF_PIECE = [0, 1, 2, 3, 7, 6, 5, 4]


def _tiles_of_core(cp):
    return [4 * i + cp for i in range(4)] + [31 - 4 * i - cp for i in range(4)]


def _fold_pos(k):
    if k <= 15:
        return k % 4, k // 4
    m = 31 - k
    return m % 4, 4 + m // 4


def _khcol(k):
    r, j = _fold_pos(k)
    return r * 8 + j


class Prog:
    SEMS = ("pe", "dve", "act", "coll")

    def __init__(self, nc, ctx):
        self.nc = nc
        self.ctx = ctx
        self.sem = {}
        self.cnt = {}
        self.hw = {}
        for name in self.SEMS:
            self.sem[name] = ctx.enter_context(nc.semaphore(name))
            self.cnt[name] = 0
        for e in ("pe", "dve", "act", "sync", "gp"):
            self.hw[e] = {}

    def new(self, name):
        if name not in self.sem:
            self.sem[name] = self.ctx.enter_context(self.nc.semaphore(name))
            self.cnt[name] = 0
        return name

    def eng(self, e):
        return {"pe": self.nc.tensor, "dve": self.nc.vector, "act": self.nc.scalar,
                "sync": self.nc.sync, "gp": self.nc.gpsimd}[e]

    def wait(self, e, sname, val=None):
        if val is None:
            val = self.cnt[sname]
        if val <= 0 or self.hw[e].get(sname, 0) >= val:
            return
        self.eng(e).wait_ge(self.sem[sname], val)
        self.hw[e][sname] = val

    def inc(self, inst, sname, k=1):
        inst.then_inc(self.sem[sname], k)
        self.cnt[sname] += k
        return self.cnt[sname]

    def barrier(self, engines=("pe", "dve", "act", "sync", "gp")):
        for e in engines:
            for s in list(self.sem):
                self.wait(e, s)


def build_program(n_experts=NE, n_heads=NH):
    nc = bass.Bass("TRN2", target_bir_lowering=False, debug=False)

    def din(name, shape, dt=F32):
        return nc.dram_tensor(name, list(shape), dt, kind="ExternalInput").ap()

    x_in = din("x_in", [4, NTOK, D])
    ind_in = din("ind_in", [128, 4])
    wgu = din("wgu", [NE, NJ, 128, 4096])
    wdn = din("wdn", [NE, 128, NJ * 2048])
    gw = din("gw", [128, 512])
    wq = din("wq", [4, 128, 8192])
    wk = din("wk", [4, 128, 8192])
    wv = din("wv", [4, 128, 8192])
    wo = din("wo", [2, 128, 16384])
    bq = din("bq", [128, 16])
    bk = din("bk", [128, 16])
    bv = din("bv", [128, 16])
    bo_bc = din("bo_bc", [128, D])
    ident = din("ident", [128, 128], BF16)
    ones_row = din("ones_row", [1, 128], BF16)
    ones_row_f = din("ones_row_f", [1, 128])
    onec_bf = din("onec_bf", [128, 1], BF16)
    triu_s = din("triu_s", [128, 128], BF16)
    iota32 = din("iota32", [128, 256])
    tbc_in = din("tbc_in", [128, 32])
    bigp_in = din("bigp_in", [128, 1])
    eps_col = din("eps_col", [128, 1])
    amask = din("amask", [8, 128, 512], BF16)

    y_out = nc.dram_tensor("y", [NTOK, D], F32, kind="ExternalOutput").ap()
    kv_dram = nc.dram_tensor("kv_dram", [2 * NH * 128, T], BF16, kind="Internal").ap()
    x1_dram = nc.dram_tensor("x1_dram", [4, NTOK, D], F32, kind="Internal").ap()
    q_dram = nc.dram_tensor("q_dram", [NH * 128, NTOK], BF16, kind="Internal").ap()
    xg1 = nc.dram_tensor("xg1", [NE * CAP1, XROW], BF16, kind="Internal").ap()
    o1 = nc.dram_tensor("o1", [NE * CAP1 + 1, D], BF16, kind="Internal").ap()
    xg2 = nc.dram_tensor("xg2", [NE * CAP2, XROW], BF16, kind="Internal").ap()
    o2 = nc.dram_tensor("o2", [NE * CAP2 + 1, D], BF16, kind="Internal").ap()
    wgub = nc.dram_tensor("wgub", [NE, 2, 128, 3 * 4096], BF16, kind="Internal").ap()
    wdb = nc.dram_tensor("wdb", [NE, 128, NJ * 2048], BF16, kind="Internal").ap()

    with ExitStack() as ctx:
        E = ctx.enter_context
        p = Prog(nc, ctx)
        te, dv, sc, gp, sy = nc.tensor, nc.vector, nc.scalar, nc.gpsimd, nc.sync

        x_sb = None
        hfm = None
        htm = E(nc.sbuf_tensor("htm", [128, D], BF16))
        stat = E(nc.sbuf_tensor("stat", [128, 16], F32))
        idn = E(nc.sbuf_tensor("idn", [128, 128], BF16))
        onesr = E(nc.sbuf_tensor("onesr", [1, 128], BF16))
        onesrf = E(nc.sbuf_tensor("onesrf", [1, 128], F32))
        onecb = E(nc.sbuf_tensor("onecb", [128, 1], BF16))
        trius = E(nc.sbuf_tensor("trius", [128, 128], BF16))
        iotar_sb = E(nc.sbuf_tensor("iotar_sb", [128, 256], F32))
        idxs = E(nc.sbuf_tensor("idxs", [128, 256], F32))
        tbc = E(nc.sbuf_tensor("tbc", [128, 32], F32))
        bigp = E(nc.sbuf_tensor("bigp", [128, 1], F32))
        epsc = E(nc.sbuf_tensor("epsc", [128, 1], F32))
        bqs = E(nc.sbuf_tensor("bqs", [128, 16], F32))
        bks = E(nc.sbuf_tensor("bks", [128, 16], F32))
        bvs = E(nc.sbuf_tensor("bvs", [128, 16], F32))
        gws = E(nc.sbuf_tensor("gws", [128, 512], BF16))
        comb = E(nc.sbuf_tensor("comb", [128, NTT * 32], F32))
        rt = E(nc.sbuf_tensor("rt", [128, 512], F32))
        ind_sb = E(nc.sbuf_tensor("ind_sb", [128, 4], F32))
        mcol = E(nc.sbuf_tensor("mcol", [128, 12], F32))
        mcolb = E(nc.sbuf_tensor("mcolb", [128, 1], BF16))
        mrow = E(nc.sbuf_tensor("mrow", [1, 128], BF16))
        rsum = E(nc.sbuf_tensor("rsum", [128, 1], F32))
        cstage = E(nc.sbuf_tensor("cstage", [128, 128], BF16))
        gidxT = E(nc.sbuf_tensor("gidxT", [128, 4 * NTT * 8], U32))
        off1 = E(nc.sbuf_tensor("off1", [1, 32], F32))
        ridx = E(nc.sbuf_tensor("ridx", [128, 48], F32))
        ridxu = E(nc.sbuf_tensor("ridxu", [128, 16], U32))

        with nc.Block():
            p.new("c_ld")
            p.new("xg_ld")
            p.new("xsc")
            p.new("zr")

            def load_x_group(pp, q=None):
                i = (q or gp).dma_start(
                    x_sb[:, :].rearrange("p (t d) -> p t d", t=NTT),
                    x_in[pp].rearrange("(t p) d -> p t d", p=128))
                return p.inc(i, "xg_ld", 16)

            for dst, srcap in [(idn, ident), (onesr, ones_row), (onesrf, ones_row_f),
                               (onecb, onec_bf), (trius, triu_s), (iotar_sb, iota32),
                               (tbc, tbc_in), (bigp, bigp_in), (epsc, eps_col),
                               (bqs, bq), (bks, bk), (bvs, bv), (gws, gw),
                               (ind_sb, ind_in)]:
                i = gp.dma_start(dst[:, :], srcap)
                p.inc(i, "c_ld", 16)
            for e in ("pe", "dve", "act", "sync", "gp"):
                p.wait(e, "c_ld")
            i = dv.memset(htm[0:1, :], 0.0)
            zr_d = p.inc(i, "dve")
            p.wait("act", "dve", zr_d)
            i = sc.dma_start(o1[NZ1:NZ1 + 1, :], htm[0:1, :])
            p.inc(i, "zr", 16)
            i = sc.dma_start(o2[NZ2:NZ2 + 1, :], htm[0:1, :])
            zr_c = p.inc(i, "zr", 16)
            p.wait("act", "zr", zr_c)

            bnd1 = gp.alloc_register("bnd1")
            gp.reg_mov(bnd1, NE * CAP1 - 1)
            bnd2 = gp.alloc_register("bnd2")
            gp.reg_mov(bnd2, NE * CAP2 - 1)

            def emit_ln(tp_ps, xst=None, pre_tile=None):
                tp_free = {0: 0, 1: 0}
                a_h_prev = 0
                xst_cp_prev = 0
                for tt in range(NTT):
                    xt = x_sb[:, tt * D:(tt + 1) * D]
                    if pre_tile is not None:
                        d_pre = pre_tile(tt)
                        p.wait("act", "dve", d_pre)
                    ssum = stat[:, 0:1]; ex2 = stat[:, 1:2]; mneg = stat[:, 2:3]
                    nmsq = stat[:, 3:4]; var = stat[:, 4:5]; std = stat[:, 5:6]
                    rstd = stat[:, 6:7]; nmr = stat[:, 7:8]
                    p.wait("act", "pe")
                    p.wait("act", "dve", xst_cp_prev)
                    i = sc.activation(htm[:, :], xt, AF.Square, accum_out=ex2)
                    a_sq = p.inc(i, "act")
                    p.wait("dve", "act", a_h_prev)
                    i = dv.reduce_sum(ssum, xt, axis=AX.X, negate=True)
                    p.inc(i, "dve"); p.wait("dve", "dve")
                    i = dv.tensor_scalar(mneg, ssum, 1.0 / D, None, op0=ALU.mult)
                    p.inc(i, "dve"); p.wait("dve", "dve")
                    i = dv.tensor_scalar(nmsq, mneg, mneg, -1.0, op0=ALU.mult, op1=ALU.mult)
                    p.inc(i, "dve")
                    p.wait("dve", "act", a_sq); p.wait("dve", "dve")
                    i = dv.tensor_scalar(var, ex2, 1.0 / D, None, op0=ALU.mult)
                    p.inc(i, "dve"); p.wait("dve", "dve")
                    i = dv.tensor_add(var, var, nmsq)
                    d_var = p.inc(i, "dve")
                    p.wait("act", "dve", d_var)
                    i = sc.activation(std, var, AF.Sqrt, bias=epsc[:, :])
                    a_std = p.inc(i, "act")
                    p.wait("dve", "act", a_std)
                    i = dv.reciprocal(rstd, std)
                    p.inc(i, "dve"); p.wait("dve", "dve")
                    i = dv.tensor_scalar(nmr, mneg, rstd, None, op0=ALU.mult)
                    d_nmr = p.inc(i, "dve")
                    p.wait("act", "dve", d_nmr)
                    p.wait("act", "act", a_sq)
                    i = sc.activation(htm[:, :], xt, AF.Identity, bias=nmr, scale=rstd)
                    a_h = p.inc(i, "act")
                    a_h_prev = a_h
                    if xst is not None:
                        p.wait("dve", "act", a_h)
                        p.wait("dve", "xsc")
                        i = dv.tensor_copy(xst[:, tt * XROW: tt * XROW + D],
                                           htm[:, :])
                        xst_cp_prev = p.inc(i, "dve")
                    p.wait("pe", "act", a_h)
                    for kc in range(NKC):
                        bank = kc % 2
                        p.wait("pe", "dve", tp_free[bank])
                        i = te.transpose(tp_ps[bank][:, 0:128],
                                         htm[:, kc * 128:(kc + 1) * 128], idn[:, :])
                        t_pe = p.inc(i, "pe")
                        p.wait("dve", "pe", t_pe)
                        i = dv.tensor_copy(
                            hfm[:, kc * NTOK + tt * 128: kc * NTOK + (tt + 1) * 128],
                            tp_ps[bank][:, 0:128])
                        tp_free[bank] = p.inc(i, "dve")

            def emit_routing(lg_ps, rp=None, xst=None, grp=0, rc=None):
                probs = rt[:, 0:32]; vals = rt[:, 32:40]; cum = rt[:, 40:48]
                lt = rt[:, 48:56]; act8 = rt[:, 56:64]; rw = rt[:, 64:72]
                smax = rt[:, 72:73]; ssum = rt[:, 73:74]; sinv = rt[:, 74:75]
                s8 = rt[:, 75:76]; den = rt[:, 76:77]; dinv = rt[:, 77:78]
                s2 = rt[:, 78:79]; den2 = rt[:, 79:80]; d2i = rt[:, 80:81]
                sca = rt[:, 81:82]
                mk = [rt[:, 96 + k * 32: 128 + k * 32] for k in range(8)]
                m32 = rt[:, 352:384]; slotg = rt[:, 384:416]
                tmp32 = rt[:, 416:448]

                def dstep(inst):
                    c = p.inc(inst, "dve")
                    p.wait("dve", "dve", c)
                    return c

                for tt in range(NTT):
                    p.wait("pe", "dve")
                    p.wait("pe", "act")
                    for kc in range(NKC):
                        i = te.matmul(lg_ps[:, 0:32],
                                      hfm[:, kc * NTOK + tt * 128: kc * NTOK + (tt + 1) * 128],
                                      gws[:, kc * 32:(kc + 1) * 32],
                                      start=(kc == 0), stop=(kc == NKC - 1))
                    m_pe = p.inc(i, "pe")
                    p.wait("dve", "pe", m_pe)
                    dstep(dv.reduce_max(smax, lg_ps[:, 0:32], axis=AX.X, negate=True))
                    p.wait("act", "dve")
                    p.wait("act", "pe", m_pe)
                    i = sc.activation(probs, lg_ps[:, 0:32], AF.Exp, bias=smax,
                                      accum_out=ssum)
                    a = p.inc(i, "act")
                    p.wait("dve", "act", a)
                    dstep(dv.reciprocal(sinv, ssum))
                    dstep(dv.tensor_scalar(probs, probs, sinv, None, op0=ALU.mult))
                    dstep(dv.tensor_mul(probs, probs, tbc[:, :]))
                    for k in range(8):
                        dstep(dv.reduce_max(vals[:, k:k + 1], probs, axis=AX.X))
                        dstep(dv.tensor_scalar(mk[k], probs, vals[:, k:k + 1], None,
                                               op0=ALU.is_equal))
                        dstep(dv.scalar_tensor_tensor(probs, mk[k], -2.0, probs,
                                                      op0=ALU.mult, op1=ALU.add))
                    dstep(dv.reduce_sum(s8, vals, axis=AX.X))
                    dstep(dv.tensor_scalar(den, s8, 1e-12, None, op0=ALU.max))
                    dstep(dv.reciprocal(dinv, den))
                    dstep(dv.tensor_scalar(cum, vals, dinv, None, op0=ALU.mult))
                    for k in range(1, 8):
                        dstep(dv.tensor_add(cum[:, k:k + 1], cum[:, k:k + 1],
                                            cum[:, k - 1:k]))
                    dstep(dv.tensor_scalar(lt, cum, 0.8, None, op0=ALU.is_lt))
                    dstep(dv.memset(act8[:, 0:1], 1.0))
                    dstep(dv.tensor_copy(act8[:, 1:8], lt[:, 0:7]))
                    dstep(dv.tensor_mul(rw, vals, act8))
                    dstep(dv.reduce_sum(s2, rw, axis=AX.X))
                    dstep(dv.tensor_scalar(den2, s2, 1e-12, None, op0=ALU.max))
                    dstep(dv.reciprocal(d2i, den2))
                    dstep(dv.tensor_scalar(sca, d2i, 0.5, None, op0=ALU.mult))
                    dstep(dv.tensor_scalar(rw, rw, sca, None, op0=ALU.mult))
                    ctt = comb[:, tt * 32:(tt + 1) * 32]
                    dstep(dv.tensor_scalar(ctt, mk[0], rw[:, 0:1], None, op0=ALU.mult))
                    for k in range(1, 8):
                        dstep(dv.scalar_tensor_tensor(ctt, mk[k], rw[:, k:k + 1], ctt,
                                                      op0=ALU.mult, op1=ALU.add))
                    if rp is None:
                        continue
                    dstep(dv.tensor_copy(
                        xst[:, tt * XROW + D: tt * XROW + D + 32], ctt))
                    dstep(dv.tensor_scalar(m32, ctt, 0.0, None, op0=ALU.is_gt))
                    d_mbf = dstep(dv.tensor_copy(htm[:, 0:32], m32))
                    p.wait("pe", "dve", d_mbf)
                    i = te.matmul(rp[:, 0:32], trius[:, :], htm[:, 0:32],
                                  start=True, stop=True)
                    p.inc(i, "pe")
                    i = te.matmul(rp[0:1, 64:96], onecb[:, :], htm[:, 0:32],
                                  start=True, stop=True)
                    p.inc(i, "pe")
                    i = te.matmul(rp[:, 128:160], onesrf[:, :], off1[0:1, :],
                                  start=True, stop=True)
                    bc_pe = p.inc(i, "pe")
                    p.wait("dve", "pe", bc_pe)
                    dstep(dv.tensor_copy(tmp32, rp[:, 0:32]))
                    dstep(dv.tensor_add(slotg, tmp32, rp[:, 128:160]))
                    dstep(dv.tensor_add(off1[0:1, :], off1[0:1, :], rp[0:1, 64:96]))
                    mkall = rt[:, 96:352]
                    gcols = ridx[:, 0:8]; scols = ridx[:, 8:16]
                    ek8 = ridx[:, 16:24]; sel8 = ridx[:, 24:32]
                    base8 = ridx[:, 32:40]
                    k8 = idxs[:, :].rearrange("p (k c) -> p k c", k=8)
                    dstep(dv.tensor_mul(idxs[:, :], mkall, iotar_sb[:, :]))
                    dstep(dv.reduce_sum(ek8, k8, axis=AX.X))
                    for k in range(8):
                        dstep(dv.tensor_mul(idxs[:, k * 32:(k + 1) * 32], mk[k],
                                            slotg))
                    dstep(dv.reduce_sum(sel8, k8, axis=AX.X))
                    dstep(dv.scalar_tensor_tensor(base8, ek8, float(rc["cap"]), sel8,
                                                  op0=ALU.mult, op1=ALU.add))
                    dstep(dv.tensor_scalar(gcols, base8, float(-rc["nz"]), None,
                                           op0=ALU.add))
                    dstep(dv.tensor_mul(gcols, gcols, act8))
                    dstep(dv.tensor_scalar(gcols, gcols, float(rc["nz"]), None,
                                           op0=ALU.add))
                    dstep(dv.tensor_scalar(scols, base8, bigp[:, 0:1], None,
                                           op0=ALU.subtract))
                    dstep(dv.tensor_mul(scols, scols, act8))
                    dstep(dv.tensor_scalar(scols, scols, bigp[:, 0:1], None,
                                           op0=ALU.add))
                    gc_u = gidxT[:, (grp * NTT + tt) * 8:(grp * NTT + tt) * 8 + 8]
                    dstep(dv.tensor_copy(gc_u, gcols))
                    d_sc = dstep(dv.tensor_copy(ridxu[:, 0:8], scols))
                    p.wait("gp", "dve", d_sc)
                    for k in range(8):
                        i = gp.indirect_dma_start(
                            rc["xg"],
                            IndirectOffsetOnAxis(ap=ridxu[:, k:k + 1], axis=0),
                            xst[:, tt * XROW:(tt + 1) * XROW], None,
                            bounds_check=rc["bnd"], oob_is_err=False)
                        p.inc(i, "xsc", 16)

            def emit_ffn_routed(tag, cap, sts, xg_ap, o_ap, n_exp, f32_src=True):
                nsub_tot = cap // 128
                subs_of_st = [n // 128 for (_, n) in sts]
                with ExitStack() as fs:
                    Ef = fs.enter_context
                    gin = Ef(nc.sbuf_tensor(f"gin{tag}", [128, 4 * XROW], BF16))
                    ginT = Ef(nc.sbuf_tensor(f"ginT{tag}", [128, NKC * 512], BF16))
                    act_sb = Ef(nc.sbuf_tensor(f"acts{tag}", [128, NJ * cap], BF16))
                    cbuf = Ef(nc.sbuf_tensor(f"cbuf{tag}", [128, 16], F32))
                    sil = Ef(nc.sbuf_tensor(f"sil{tag}", [128, 512], F32))
                    outst = [Ef(nc.sbuf_tensor(f"o{tag}{s}", [128, D], BF16))
                             for s in range(2)]
                    wgu_sb = [Ef(nc.sbuf_tensor(f"wgu{tag}{s}", [128, 3 * 4096], BF16))
                              for s in range(2)]
                    wd_sb = Ef(nc.sbuf_tensor(f"wd{tag}", [128, NJ * 2048], BF16))
                    g_ps = Ef(nc.psum_tensor(f"g{tag}", [128, 512], F32))
                    u_ps = Ef(nc.psum_tensor(f"u{tag}", [128, 512], F32))
                    b_ps = Ef(nc.psum_tensor(f"b{tag}", [128, 2048], F32))
                    tr_ps = [Ef(nc.psum_tensor(f"tr{tag}{s}", [128, 512], BF16))
                             for s in range(2)]

                    sgu = [p.new(f"wgu{tag}{s}") for s in range(2)]
                    swd = p.new(f"swd{tag}")
                    sgin = p.new(f"gin{tag}")
                    so1 = p.new(f"so{tag}")
                    swb = p.new(f"swb{tag}")
                    half_free = {}
                    half_ld = {}
                    wb_done = {}
                    wd_free = {}
                    wd_ld = {}
                    wdwb_done = {}
                    gin_free = {}
                    gin_free_dve = {}
                    gin_ld = {}

                    nst = len(sts)

                    def load_half(e, h):
                        idx = 2 * e + h
                        if idx >= 2:
                            p.wait("gp", "pe", half_free[idx - 2])
                            if f32_src:
                                p.wait("gp", swb, wb_done[idx - 2])
                        if f32_src:
                            i = gp.dma_start(
                                wgu_sb[idx % 2][:, :].rearrange("p (j n) -> p j n", j=3),
                                wgu[e, 3 * h:3 * h + 3].rearrange("j p n -> p j n"))
                        else:
                            i = gp.dma_start(wgu_sb[idx % 2][:, :], wgub[e, h])
                        half_ld[idx] = (sgu[idx % 2], p.inc(i, sgu[idx % 2], 16))

                    def load_wd(e):
                        qe, q = ("gp", gp) if f32_src else ("sync", sy)
                        if e >= 1:
                            p.wait(qe, "pe", wd_free[e - 1])
                            if f32_src:
                                p.wait(qe, swb, wdwb_done[e - 1])
                        i = q.dma_start(wd_sb[:, :], wdn[e] if f32_src else wdb[e])
                        wd_ld[e] = p.inc(i, swd, 16)

                    def load_gin(gst):
                        e, sti = divmod(gst, nst)
                        if e >= n_exp:
                            return
                        st_off, st_n = sts[sti]
                        ns = st_n // 128
                        if gst >= 1:
                            p.wait("sync", "pe", gin_free[gst - 1])
                            p.wait("sync", "dve", gin_free_dve[gst - 1])
                        i = sy.dma_start(
                            gin[:, 0:ns * XROW].rearrange(
                                "p (s c) -> p s c", s=ns),
                            xg_ap[e * cap + st_off: e * cap + st_off + st_n].rearrange(
                                "(s p) c -> p s c", p=128))
                        gin_ld[gst] = (sgin, p.inc(i, sgin, 16))

                    load_half(0, 0)
                    load_half(0, 1)
                    load_wd(0)
                    load_gin(0)

                    a_sg = None
                    a_tt = None
                    tp_free = {0: 0, 1: 0}
                    out_free = {0: 0, 1: 0}
                    d_cb_prev = 0

                    for e in range(n_exp):
                        if f32_src:
                            for h in range(2):
                                idx = 2 * e + h
                                p.wait("act", *half_ld[idx])
                                i = sc.dma_start(wgub[e, h], wgu_sb[idx % 2][:, :])
                                wb_done[idx] = p.inc(i, swb, 16)
                        for sti, (st_off, st_n) in enumerate(sts):
                            gst = e * nst + sti
                            ns = subs_of_st[sti]
                            p.wait("pe", *gin_ld[gst])
                            p.wait("dve", *gin_ld[gst])
                            p.wait("dve", "dve", d_cb_prev)
                            for s in range(ns):
                                i = dv.tensor_copy(
                                    cbuf[:, st_off // 128 + s: st_off // 128 + s + 1],
                                    gin[:, s * XROW + D + e: s * XROW + D + e + 1])
                                d_cb = p.inc(i, "dve")
                            d_cb_prev = d_cb
                            for s in range(ns):
                                for kc in range(NKC):
                                    bank = kc % 2
                                    p.wait("pe", "dve", tp_free[bank])
                                    i = te.transpose(
                                        tr_ps[bank][:, 0:128],
                                        gin[:, s * XROW + kc * 128: s * XROW + (kc + 1) * 128],
                                        idn[:, :])
                                    t_pe = p.inc(i, "pe")
                                    p.wait("dve", "pe", t_pe)
                                    i = dv.tensor_copy(
                                        ginT[:, kc * 512 + s * 128: kc * 512 + (s + 1) * 128],
                                        tr_ps[bank][:, 0:128])
                                    tp_free[bank] = p.inc(i, "dve")
                            gin_free[gst] = p.cnt["pe"]
                            gin_free_dve[gst] = p.cnt["dve"]
                            load_gin(gst + 1)
                            for j in range(NJ):
                                wt = wgu_sb[(2 * e + j // 3) % 2]
                                joff = (j % 3) * 4096
                                p.wait("pe", *half_ld[2 * e + j // 3])
                                if a_sg is not None:
                                    p.wait("pe", "act", a_sg)
                                p.wait("pe", "dve")
                                for kc in range(NKC):
                                    i = te.matmul(
                                        g_ps[:, 0:st_n],
                                        wt[:, joff + kc * 256: joff + kc * 256 + 128],
                                        ginT[:, kc * 512: kc * 512 + st_n],
                                        start=(kc == 0), stop=(kc == NKC - 1))
                                g_pe = p.inc(i, "pe")
                                if a_tt is not None:
                                    p.wait("pe", "dve", a_tt)
                                for kc in range(NKC):
                                    i = te.matmul(
                                        u_ps[:, 0:st_n],
                                        wt[:, joff + kc * 256 + 128: joff + kc * 256 + 256],
                                        ginT[:, kc * 512: kc * 512 + st_n],
                                        start=(kc == 0), stop=(kc == NKC - 1))
                                u_pe = p.inc(i, "pe")
                                if sti == nst - 1 and (j == 2 or j == NJ - 1):
                                    half_free[2 * e + (0 if j == 2 else 1)] = u_pe
                                p.wait("act", "pe", g_pe)
                                if a_tt is not None:
                                    p.wait("act", "dve", a_tt)
                                i = sc.activation(sil[:, 0:st_n], g_ps[:, 0:st_n],
                                                  AF.Silu)
                                a_sg = p.inc(i, "act")
                                p.wait("dve", "pe", u_pe)
                                p.wait("dve", "act", a_sg)
                                i = dv.tensor_mul(
                                    act_sb[:, j * cap + st_off: j * cap + st_off + st_n],
                                    sil[:, 0:st_n], u_ps[:, 0:st_n])
                                a_tt = p.inc(i, "dve")
                        if e + 1 < n_exp:
                            load_half(e + 1, 0)
                            load_half(e + 1, 1)
                        p.wait("pe", swd, wd_ld[e])
                        if f32_src:
                            p.wait("sync", swd, wd_ld[e])
                            i = sy.dma_start(wdb[e], wd_sb[:, :])
                            wdwb_done[e] = p.inc(i, swb, 16)
                        p.wait("pe", "dve", a_tt)
                        for sub in range(nsub_tot):
                            oslot = sub % 2
                            for dq in range(4):
                                if dq == 0:
                                    p.wait("pe", so1, out_free[oslot])
                                    p.wait("pe", "dve")
                                for j in range(NJ):
                                    i = te.matmul(
                                        b_ps[:, dq * 512:(dq + 1) * 512],
                                        act_sb[:, j * cap + sub * 128: j * cap + (sub + 1) * 128],
                                        wd_sb[:, j * 2048 + dq * 512: j * 2048 + (dq + 1) * 512],
                                        start=(j == 0), stop=(j == NJ - 1))
                                b_pe = p.inc(i, "pe")
                                p.wait("dve", "pe", b_pe)
                                p.wait("dve", so1, out_free[oslot])
                                i = dv.tensor_scalar(
                                    outst[oslot][:, dq * 512:(dq + 1) * 512],
                                    b_ps[:, dq * 512:(dq + 1) * 512],
                                    cbuf[:, sub:sub + 1], None, op0=ALU.mult)
                                d_ev = p.inc(i, "dve")
                            p.wait("act", "dve", d_ev)
                            i = sc.dma_start(
                                o_ap[e * cap + sub * 128: e * cap + (sub + 1) * 128, :],
                                outst[oslot][:, :])
                            out_free[oslot] = p.inc(i, so1, 16)
                        wd_free[e] = p.cnt["pe"]
                        if e + 1 < n_exp:
                            load_wd(e + 1)

            def emit_second_gather(o_ap, grp, tag="", tile_store=None):
                with ExitStack() as fs:
                    Ef = fs.enter_context
                    gbuf = [Ef(nc.sbuf_tensor(f"gb{tag}{grp}{s}", [128, D], BF16))
                            for s in range(4)]
                    sgb = p.new(f"sgb{tag}{grp}")
                    gb_free = {s: 0 for s in range(4)}
                    for tt in range(NTT):
                        xt = x_sb[:, tt * D:(tt + 1) * D]
                        for k in range(8):
                            slot = (tt * 8 + k) % 4
                            p.wait("gp", "dve", gb_free[slot])
                            i = gp.indirect_dma_start(
                                gbuf[slot][:, :], None, o_ap,
                                IndirectOffsetOnAxis(
                                    ap=gidxT[:, (grp * NTT + tt) * 8 + k:
                                             (grp * NTT + tt) * 8 + k + 1],
                                    axis=0))
                            g_ld = p.inc(i, sgb, 16)
                            p.wait("dve", sgb, g_ld)
                            p.wait("dve", "dve")
                            i = dv.tensor_add(xt, xt, gbuf[slot][:, :])
                            gb_free[slot] = p.inc(i, "dve")
                        if tile_store is not None:
                            tile_store(tt)

            swq = [p.new(f"wqk{s}") for s in range(2)]
            sstg = [p.new(f"stg{s}") for s in range(3)]
            sx1 = p.new("sx1")
            stg_free = {}

            def emit_proj(specs, tag):
                with ExitStack() as fs:
                    Ef = fs.enter_context
                    wbuf = [Ef(nc.sbuf_tensor(f"wqkv{tag}{s}", [128, 8192], BF16))
                            for s in range(2)]
                    stg = [Ef(nc.sbuf_tensor(f"qkvstg{tag}{s}", [128, NTOK], BF16))
                           for s in range(3)]
                    mg_ps = [Ef(nc.psum_tensor(f"mg{tag}{s}", [128, NTOK], F32))
                             for s in range(4)]
                    wld = {}
                    grp_evac = 0
                    i = gp.dma_start(wbuf[0][:, :], specs[0][1][0])
                    wld[0] = (swq[0], p.inc(i, swq[0], 16))
                    gidx = 0
                    for si, (nm, wsrc, bsb, roff, coff, dst) in enumerate(specs):
                        for mg in range(4):
                            slot = gidx % 2
                            nxt = (wsrc, mg + 1) if mg < 3 else (
                                (specs[si + 1][1], 0) if si + 1 < len(specs) else None)
                            if nxt is not None:
                                p.wait("gp", "pe")
                                i = gp.dma_start(wbuf[1 - slot][:, :], nxt[0][nxt[1]])
                                wld[gidx + 1] = (swq[1 - slot], p.inc(i, swq[1 - slot], 16))
                            p.wait("pe", *wld[gidx])
                            p.wait("pe", "act", grp_evac)
                            last_evac = 0
                            for mc in range(4):
                                for half in range(2):
                                    for kc in range(NKC):
                                        i = te.matmul(
                                            mg_ps[mc][:, half * 512:(half + 1) * 512],
                                            wbuf[slot][:, kc * 512 + mc * 128: kc * 512 + (mc + 1) * 128],
                                            hfm[:, kc * NTOK + half * 512: kc * NTOK + (half + 1) * 512],
                                            start=(kc == 0), stop=(kc == NKC - 1))
                                mc_pe = p.inc(i, "pe")
                                m_abs = mg * 4 + mc
                                sslot = m_abs % 3
                                p.wait("act", "pe", mc_pe)
                                if sslot in stg_free:
                                    p.wait("act", sstg[sslot], stg_free[sslot])
                                i = sc.activation(stg[sslot][:, :], mg_ps[mc][:, :],
                                                  AF.Identity, bias=bsb[:, m_abs:m_abs + 1])
                                a_e = p.inc(i, "act")
                                last_evac = a_e
                                p.wait("sync", "act", a_e)
                                i = sy.dma_start(
                                    dst[roff + m_abs * 128: roff + (m_abs + 1) * 128,
                                        coff:coff + NTOK],
                                    stg[sslot][:, :])
                                stg_free[sslot] = p.inc(i, sstg[sslot], 16)
                            grp_evac = last_evac
                            gidx += 1

            def emit_attention(n_heads_):
                with ExitStack() as fs:
                    Ef = fs.enter_context
                    qh = [Ef(nc.sbuf_tensor(f"qh{s}", [128, NTOK], BF16)) for s in range(2)]
                    kh = [Ef(nc.sbuf_tensor(f"kh{s}", [128, 4096], BF16)) for s in range(2)]
                    vh = [Ef(nc.sbuf_tensor(f"vh{s}", [128, 32 * VST], BF16))
                          for s in range(2)]
                    pt = [Ef(nc.sbuf_tensor(f"pt{s}", [128, 512], BF16)) for s in range(2)]
                    masks = Ef(nc.sbuf_tensor("masks", [128, 8 * 512], BF16))
                    smsk = p.new("msk")
                    i = gp.dma_start(masks[:, :].rearrange("p (s n) -> p s n", s=8),
                                     amask.rearrange("s p n -> p s n"))
                    mask_ld = p.inc(i, smsk, 16)
                    p.wait("dve", smsk, mask_ld)
                    p1_ps = [Ef(nc.psum_tensor(f"p1{s}", [128, 512], F32)) for s in range(2)]
                    sT_ps = [Ef(nc.psum_tensor(f"sT{s}", [128, 512], F32)) for s in range(2)]
                    cs_ps = [Ef(nc.psum_tensor(f"cs{s}", [128, 512], F32)) for s in range(2)]
                    tr_ps = [Ef(nc.psum_tensor(f"tr{s}", [128, 1024], BF16)) for s in range(2)]

                    for s in range(2):
                        i = dv.memset(
                            vh[s][:, :].rearrange("p (c n) -> p c n", c=32)[:, :, 128:129],
                            1.0)
                        d_ones = p.inc(i, "dve")
                    p.wait("sync", "dve", d_ones)
                    p.wait("pe", "dve", d_ones)

                    skq = [p.new(f"kq{s}") for s in range(2)]
                    svh = [p.new(f"vh{s}") for s in range(2)]
                    kld = {}
                    vld = {}
                    head_pe_done = {}

                    def load_head(h):
                        slot = h % 2
                        if h >= 2:
                            p.wait("sync", "pe", head_pe_done[h - 2])
                        i = sy.dma_start(qh[slot][:, :], q_dram[h * 128:(h + 1) * 128, :])
                        p.inc(i, skq[slot], 16)
                        i = sy.dma_start(kh[slot][:, :],
                                         kv_dram[h * 128:(h + 1) * 128, :])
                        p.inc(i, skq[slot], 16)
                        kld[h] = (skq[slot], p.cnt[skq[slot]])
                        for cc in range(32):
                            i = sy.dma_start(
                                vh[slot][:, cc * VST: cc * VST + 128],
                                kv_dram[NH * 128 + h * 128: NH * 128 + (h + 1) * 128,
                                        cc * 128:(cc + 1) * 128],
                                transpose=True)
                            p.inc(i, svh[slot], 16)
                        vld[h] = (svh[slot], p.cnt[svh[slot]])

                    for s3 in range(3):
                        p.wait("sync", f"stg{s3}")
                        p.wait("act", f"stg{s3}")
                    load_head(0)
                    cs_free = {0: 0, 1: 0}
                    pt_free = {0: 0, 1: 0}
                    p1_free = {0: 0, 1: 0}
                    sT_free = {0: 0, 1: 0}
                    for h in range(n_heads_):
                        slot = h % 2
                        if h + 1 < n_heads_:
                            load_head(h + 1)
                        p.wait("pe", *kld[h])
                        p.wait("pe", *vld[h])
                        kh4 = kh[slot][:, :].rearrange("p (r j c) -> p r j c", r=4, j=8)
                        for s in range(8):
                            C = C_S[s]
                            qsl = qh[slot][:, s * 128:(s + 1) * 128]
                            for g in range(C // 4):
                                bank = g % 2
                                p.wait("pe", "dve", p1_free[bank])
                                i = te.matmul(p1_ps[bank][:, :], qsl,
                                              kh4[:, :, J_OF_PIECE[g], :],
                                              start=True, stop=True)
                                pe_c = p.inc(i, "pe")
                                p.wait("dve", "pe", pe_c)
                                i = dv.reduce_max(mcol[:, g:g + 1], p1_ps[bank][:, :],
                                                  axis=AX.X)
                                p1_free[bank] = p.inc(i, "dve")
                            p.wait("dve", "dve")
                            i = dv.reduce_max(mcol[:, 8:9], mcol[:, 0:C // 4],
                                              axis=AX.X, negate=True)
                            p.inc(i, "dve"); p.wait("dve", "dve")
                            i = dv.tensor_copy(mcolb[:, :], mcol[:, 8:9])
                            d_mb = p.inc(i, "dve")
                            p.wait("pe", "dve", d_mb)
                            i = te.transpose(tr_ps[0][0:1, 0:128], mcolb[:, :], idn[:, :])
                            t_pe = p.inc(i, "pe")
                            p.wait("dve", "pe", t_pe)
                            i = dv.tensor_copy(mrow[:, :], tr_ps[0][0:1, 0:128])
                            d_mr = p.inc(i, "dve")
                            p.wait("pe", "dve", d_mr)
                            ngr = C // 4
                            cb = s % 2
                            ctx_pe = 0
                            for g in range(ngr):
                                bank = g % 2
                                p.wait("pe", "act", sT_free[bank])
                                for ci in range(4):
                                    k = g * 4 + ci
                                    col = _khcol(k)
                                    i = te.matmul(sT_ps[bank][:, ci * 128:(ci + 1) * 128],
                                                  kh[slot][:, col * 128:(col + 1) * 128],
                                                  qsl, start=(ci == 0), stop=False)
                                for ci in range(4):
                                    i = te.matmul(sT_ps[bank][:, ci * 128:(ci + 1) * 128],
                                                  onesr[:, :], mrow[:, :],
                                                  start=False, stop=(ci == 3))
                                g_pe = p.inc(i, "pe")
                                p.wait("act", "pe", g_pe)
                                p.wait("act", "pe", pt_free[bank])
                                i = sc.activation(pt[bank][:, :], sT_ps[bank][:, :],
                                                  AF.Exp, scale=ISQ)
                                a_exp = p.inc(i, "act")
                                sT_free[bank] = a_exp
                                if g == ngr - 1:
                                    p.wait("dve", "act", a_exp)
                                    i = dv.tensor_mul(pt[bank][:, :], pt[bank][:, :],
                                                      masks[:, s * 512:(s + 1) * 512])
                                    d_msk = p.inc(i, "dve")
                                    p.wait("pe", "dve", d_msk)
                                else:
                                    p.wait("pe", "act", a_exp)
                                if g == 0:
                                    p.wait("pe", "dve", cs_free[cb])
                                for ci in range(4):
                                    k = g * 4 + ci
                                    col = _khcol(k)
                                    i = te.matmul(cs_ps[cb][:, 0:129],
                                                  pt[bank][:, ci * 128:(ci + 1) * 128],
                                                  vh[slot][:, col * VST: col * VST + 129],
                                                  start=(k == 0), stop=(k == C - 1))
                                ctx_pe = p.inc(i, "pe")
                                pt_free[bank] = ctx_pe
                            p.wait("dve", "pe", ctx_pe)
                            i = dv.reciprocal(rsum[:, :], cs_ps[cb][:, 128:129])
                            p.inc(i, "dve"); p.wait("dve", "dve")
                            i = dv.tensor_scalar(cstage[:, :], cs_ps[cb][:, 0:128],
                                                 rsum[:, :], None, op0=ALU.mult)
                            d_cs = p.inc(i, "dve")
                            cs_free[cb] = d_cs
                            p.wait("pe", "dve", d_cs)
                            i = te.transpose(tr_ps[1][:, 0:128], cstage[:, :], idn[:, :])
                            t2 = p.inc(i, "pe")
                            p.wait("dve", "pe", t2)
                            i = dv.tensor_copy(
                                hfm[:, h * NTOK + s * 128: h * NTOK + (s + 1) * 128],
                                tr_ps[1][:, 0:128])
                            p.inc(i, "dve")
                        head_pe_done[h] = p.cnt["pe"]

            def emit_outproj():
                with ExitStack() as fs:
                    Ef = fs.enter_context
                    wo_sb = Ef(nc.sbuf_tensor("wo_sb", [128, 16384], BF16))
                    bos = Ef(nc.sbuf_tensor("bos", [128, D], F32))
                    op_ps = [Ef(nc.psum_tensor(f"op{s}", [128, D], F32)) for s in range(2)]
                    sbo = p.new("bob")
                    swo = p.new("wol")
                    i = gp.dma_start(bos[:, :], bo_bc)
                    bo_ld = p.inc(i, sbo, 16)
                    p.wait("dve", sbo, bo_ld)
                    for tt in range(NTT):
                        p.wait("dve", "dve")
                        i = dv.tensor_add(x_sb[:, tt * D:(tt + 1) * D],
                                          x_sb[:, tt * D:(tt + 1) * D], bos[:, :])
                        p.inc(i, "dve")
                    op_free = {0: 0, 1: 0}
                    for halfk in range(2):
                        if halfk:
                            p.wait("gp", "pe")
                        i = gp.dma_start(
                            wo_sb[:, :].rearrange("p (k n) -> p k n", k=8),
                            wo[halfk].rearrange("p (k n) -> p k n", k=8))
                        ld = p.inc(i, swo, 16)
                        p.wait("pe", swo, ld)
                        for tt in range(NTT):
                            bank = tt % 2
                            p.wait("pe", "dve", op_free[bank])
                            for ns in range(4):
                                for kc8 in range(8):
                                    kc = halfk * 8 + kc8
                                    i = te.matmul(
                                        op_ps[bank][:, ns * 512:(ns + 1) * 512],
                                        hfm[:, kc * NTOK + tt * 128: kc * NTOK + (tt + 1) * 128],
                                        wo_sb[:, kc8 * 2048 + ns * 512: kc8 * 2048 + (ns + 1) * 512],
                                        start=(kc8 == 0), stop=(kc8 == 7))
                            o_pe = p.inc(i, "pe")
                            p.wait("dve", "pe", o_pe)
                            p.wait("dve", "dve")
                            i = dv.tensor_add(x_sb[:, tt * D:(tt + 1) * D],
                                              x_sb[:, tt * D:(tt + 1) * D],
                                              op_ps[bank][:, :])
                            op_free[bank] = p.inc(i, "dve")

            def ln_pass(name, xst=None, pre_tile=None, bar=True):
                with ExitStack() as ps0:
                    tp = [ps0.enter_context(nc.psum_tensor(f"{name}{s}", [128, 1024], BF16))
                          for s in range(2)]
                    emit_ln(tp, xst=xst, pre_tile=pre_tile)
                    if bar:
                        p.barrier()

            with ExitStack() as p1s:
                x_sb = p1s.enter_context(
                    nc.sbuf_tensor("x_sbA", [128, NTT * D], F32))
                hfm = p1s.enter_context(
                    nc.sbuf_tensor("hfmA", [128, NKC * NTOK], BF16))
                xst = p1s.enter_context(
                    nc.sbuf_tensor("xst", [128, NTT * XROW], BF16))
                i = dv.memset(off1[0:1, :], 0.0)
                p.inc(i, "dve")
                for cp in range(4):
                    if cp > 0:
                        p.barrier()
                    xg = load_x_group(cp)
                    for e in ("pe", "dve", "act"):
                        p.wait(e, "xg_ld", xg)
                    ln_pass(f"lnA{cp}", xst=xst)
                    with ExitStack() as ps1:
                        rp = ps1.enter_context(
                            nc.psum_tensor(f"rp{cp}", [128, 512], F32))
                        lg = ps1.enter_context(
                            nc.psum_tensor(f"lg1{cp}", [128, 512], F32))
                        emit_routing(lg, rp=rp, xst=xst, grp=cp,
                                     rc=dict(cap=CAP1, xg=xg1, bnd=bnd1, nz=NZ1))
                        p.barrier()
                p.barrier()

            emit_ffn_routed("f1", CAP1, STS1, xg1, o1, n_experts)
            p.barrier()

            p2s = ctx.enter_context(ExitStack())
            x_sb = p2s.enter_context(
                nc.sbuf_tensor("x_sbB", [128, NTT * D], F32))
            hs = ExitStack()
            hfm = hs.enter_context(
                nc.sbuf_tensor("hfmB", [128, NKC * NTOK], BF16))

            with ExitStack() as g2s:
                gbuf = [g2s.enter_context(nc.sbuf_tensor(f"gbP{s}", [128, D], BF16))
                        for s in range(6)]
                sgb = p.new("sgbP")
                gb_state = {s: 0 for s in range(6)}

                def mk_pre(grp):
                    def pre(tt):
                        xt = x_sb[:, tt * D:(tt + 1) * D]
                        last = 0
                        for k in range(8):
                            slot = (tt * 8 + k) % 6
                            p.wait("gp", "dve", gb_state[slot])
                            i = gp.indirect_dma_start(
                                gbuf[slot][:, :], None, o1,
                                IndirectOffsetOnAxis(
                                    ap=gidxT[:, (grp * NTT + tt) * 8 + k:
                                             (grp * NTT + tt) * 8 + k + 1],
                                    axis=0))
                            g_ld = p.inc(i, sgb, 16)
                            p.wait("dve", sgb, g_ld)
                            p.wait("dve", "dve")
                            i = dv.tensor_add(xt, xt, gbuf[slot][:, :])
                            gb_state[slot] = p.inc(i, "dve")
                            last = gb_state[slot]
                        return last
                    return pre

                xg = load_x_group(0)
                for cp in range(4):
                    for e in ("pe", "dve", "act"):
                        p.wait(e, "xg_ld", xg)
                    ln_pass(f"lnB{cp}", pre_tile=mk_pre(cp), bar=False)
                    p.wait("sync", "dve")
                    i = sy.dma_start(
                        x1_dram[cp].rearrange("(t p) d -> p t d", p=128),
                        x_sb[:, :].rearrange("p (t d) -> p t d", t=NTT))
                    p.inc(i, sx1, 16)
                    p.wait("pe", "dve")
                    act_mark = p.cnt["act"]
                    dve_mark = p.cnt["dve"]
                    emit_proj([("k", wk, bks, 0, cp * NTOK, kv_dram),
                               ("v", wv, bvs, NH * 128, cp * NTOK, kv_dram)],
                              f"{cp}")
                    if cp < 3:
                        p.wait("gp", "act", act_mark)
                        p.wait("gp", "dve", dve_mark)
                        p.wait("gp", sx1)
                        xg = load_x_group(cp + 1)
                p.barrier()

            with ExitStack() as bs:
                Eb = bs.enter_context
                xtmp = [Eb(nc.sbuf_tensor(f"xtmp{s}", [128, D], F32)) for s in range(2)]
                sbl = p.new("xbl")
                p.wait("sync", sx1)
                for tt in range(NTT):
                    for cp in range(4):
                        slot = cp % 2
                        p.wait("sync", "dve")
                        i = sy.dma_start(
                            xtmp[slot][:, :],
                            x1_dram[cp, tt * 128:(tt + 1) * 128, :])
                        ldc = p.inc(i, sbl, 16)
                        p.wait("dve", sbl, ldc)
                        xt = x_sb[:, tt * D:(tt + 1) * D]
                        if cp == 0:
                            i = dv.tensor_scalar(xt, xtmp[slot][:, :],
                                                 ind_sb[:, 0:1], None, op0=ALU.mult)
                        else:
                            i = dv.scalar_tensor_tensor(
                                xt, xtmp[slot][:, :], ind_sb[:, cp:cp + 1], xt,
                                op0=ALU.mult, op1=ALU.add)
                        p.inc(i, "dve")
                p.barrier()
            ln_pass("lnQ")
            emit_proj([("q", wq, bqs, 0, 0, q_dram)], "q")
            p.barrier()
            emit_attention(n_heads)
            p.barrier()
            emit_outproj()
            p.barrier()
            i = dv.memset(off1[0:1, :], 0.0)
            p.inc(i, "dve"); p.wait("dve", "dve")
            with ExitStack() as f2s:
                xst2 = f2s.enter_context(
                    nc.sbuf_tensor("xst2", [128, NTT * XROW], BF16))
                ln_pass("lnC", xst=xst2)
                with ExitStack() as ps1:
                    rp2 = ps1.enter_context(
                        nc.psum_tensor("rpF2", [128, 512], F32))
                    lg = ps1.enter_context(nc.psum_tensor("lg2", [128, 512], F32))
                    emit_routing(lg, rp=rp2, xst=xst2, grp=0,
                                 rc=dict(cap=CAP2, xg=xg2, bnd=bnd2, nz=NZ2))
                    p.barrier()
            hs.close()
            emit_ffn_routed("f2", CAP2, STS2, xg2, o2, n_experts, f32_src=False)
            p.barrier()
            sst = p.new("st")

            def store_tile(tt):
                p.wait("sync", "dve")
                i = sy.dma_start(y_out[tt * 128:(tt + 1) * 128, :],
                                 x_sb[:, tt * D:(tt + 1) * D])
                p.inc(i, sst, 16)

            emit_second_gather(o2, 0, tag="s2", tile_store=store_tile)
            p.wait("sync", sst, 16 * NTT)

    return nc


def prep_shared(Wq, bq_, Wk, bk_, Wv, bv_, Wo, bo_, gate_w, gate_up, down):
    def f32(a):
        return np.ascontiguousarray(np.asarray(a, dtype=np.float32))

    gup = f32(gate_up)
    wgu_a = np.empty((NE, NJ, 128, 4096), np.float32)
    for e in range(NE):
        Wt = gup[e].T
        for j in range(NJ):
            blk = np.concatenate([Wt[:, j * 128:(j + 1) * 128],
                                  Wt[:, 768 + j * 128: 768 + (j + 1) * 128]], axis=1)
            wgu_a[e, j] = blk.reshape(16, 128, 256).transpose(1, 0, 2).reshape(128, 4096)
    dn = f32(down)
    wdn_a = np.empty((NE, 128, NJ * 2048), np.float32)
    for e in range(NE):
        Wt = dn[e].T
        wdn_a[e] = Wt.reshape(NJ, 128, 2048).transpose(1, 0, 2).reshape(128, NJ * 2048)
    gwt = f32(gate_w).T
    gw_a = np.ascontiguousarray(
        gwt.reshape(16, 128, 32).transpose(1, 0, 2).reshape(128, 512))

    def qkvw(W):
        Wt = f32(W).T
        out = np.empty((4, 128, 8192), np.float32)
        for mg in range(4):
            blk = Wt[:, mg * 512:(mg + 1) * 512]
            out[mg] = blk.reshape(16, 128, 512).transpose(1, 0, 2).reshape(128, 8192)
        return out

    Wot = f32(Wo).T
    wo_a = np.empty((2, 128, 16384), np.float32)
    for hf in range(2):
        blk = Wot[hf * 1024:(hf + 1) * 1024, :]
        wo_a[hf] = blk.reshape(8, 128, 2048).transpose(1, 0, 2).reshape(128, 16384)

    def bias16(b):
        return np.ascontiguousarray(f32(b).reshape(16, 128).T)

    return dict(
        wgu=wgu_a, wdn=wdn_a, gw=gw_a,
        wq=qkvw(Wq), wk=qkvw(Wk), wv=qkvw(Wv), wo=wo_a,
        bq=bias16(bq_), bk=bias16(bk_), bv=bias16(bv_),
        bo_bc=np.ascontiguousarray(np.tile(f32(bo_)[None, :], (128, 1))),
        ident=np.eye(128, dtype=BF),
        ones_row=np.ones((1, 128), dtype=BF),
        ones_row_f=np.ones((1, 128), np.float32),
        onec_bf=np.ones((128, 1), dtype=BF),
        triu_s=np.triu(np.ones((128, 128), np.float32), 1).astype(BF),
        iota32=np.tile(np.arange(32, dtype=np.float32)[None, :], (128, 8)),
        tbc_in=np.tile(1.0 + np.arange(32, dtype=np.float32)[None, :] * 1e-6,
                       (128, 1)),
        bigp_in=(float(BIG) + np.arange(128, dtype=np.float32))[:, None].copy(),
        eps_col=np.full((128, 1), 1e-5, np.float32),
    )


def core_mask(cp):
    tri = np.triu(np.ones((128, 128), np.float32))
    out = np.zeros((8, 128, 512), np.float32)
    tiles = _tiles_of_core(cp)
    for s in range(8):
        C = C_S[s]
        C_own = tiles[s] + 1
        for idx, k in enumerate(range(C - 4, C)):
            if k < C_own - 1:
                out[s, :, idx * 128:(idx + 1) * 128] = 1.0
            elif k == C_own - 1:
                out[s, :, idx * 128:(idx + 1) * 128] = tri
    return out.astype(BF)


def make_in_maps(inputs):
    x = np.ascontiguousarray(np.asarray(inputs["L0_attn_x"], dtype=np.float32))
    shared = prep_shared(inputs["L0_attn_Wq"], inputs["L0_attn_bq"],
                         inputs["L0_attn_Wk"], inputs["L0_attn_bk"],
                         inputs["L0_attn_Wv"], inputs["L0_attn_bv"],
                         inputs["L0_attn_Wo"], inputs["L0_attn_bo"],
                         inputs["L0_ffn_gate_weight"], inputs["L0_ffn_gate_up_proj"],
                         inputs["L0_ffn_down_proj"])
    xcov = []
    masks = [core_mask(cp) for cp in range(4)]
    for g in range(B):
        xg = x[g].reshape(32, 128, D)
        xcov.append(np.ascontiguousarray(np.stack(
            [xg[_tiles_of_core(p)].reshape(NTOK, D) for p in range(4)])))
    in_maps = []
    for c in range(8):
        cp, g = c % 4, c // 4
        ind = np.zeros((128, 4), np.float32)
        ind[:, cp] = 1.0
        m = dict(shared)
        m["x_in"] = xcov[g]
        m["ind_in"] = ind
        m["amask"] = masks[cp]
        in_maps.append(m)
    return in_maps


def gather_out(results):
    out = np.empty((B, T, D), np.float32)
    for c in range(8):
        cp, g = c % 4, c // 4
        tiles = _tiles_of_core(cp)
        yc = np.asarray(results[c]["y"], dtype=np.float32).reshape(8, 128, D)
        for s, t in enumerate(tiles):
            out[g, t * 128:(t + 1) * 128, :] = yc[s]
    return out


def kernel(**inputs):
    in_maps = make_in_maps(inputs)
    nc = build_program()
    res = run_bass_kernel_spmd(nc, in_maps, core_ids=list(range(8)))
    return gather_out(res.results)


# version 09
# speedup vs baseline: 1.2648x; 1.0199x over previous
import math
from contextlib import ExitStack

import numpy as np
import ml_dtypes

import concourse.bass as bass
import concourse.mybir as mybir
from concourse.bass import IndirectOffsetOnAxis
from concourse.bass_utils import run_bass_kernel_spmd

F32 = mybir.dt.float32
BF16 = mybir.dt.bfloat16
U32 = mybir.dt.uint32
AF = mybir.ActivationFunctionType
ALU = mybir.AluOpType
AX = mybir.AxisListType
BF = ml_dtypes.bfloat16

D = 2048
B = 2
T = 4096
NTOK = 1024
NTT = 8
NKC = D // 128
NE = 32
INTER = 768
NJ = INTER // 128
NH = 16
DH = 128
ISQ = 1.0 / math.sqrt(DH)
VST = 144

CAP1 = 1024
STS1 = [(0, 512), (512, 512)]
CAP2 = 512
STS2 = [(0, 512)]
XROW = D + 32
NZ1 = NE * CAP1
NZ2 = NE * CAP2
BIG = 1 << 20

C_S = [4, 8, 12, 16, 32, 28, 24, 20]
J_OF_PIECE = [0, 1, 2, 3, 7, 6, 5, 4]


def _tiles_of_core(cp):
    return [4 * i + cp for i in range(4)] + [31 - 4 * i - cp for i in range(4)]


def _fold_pos(k):
    if k <= 15:
        return k % 4, k // 4
    m = 31 - k
    return m % 4, 4 + m // 4


def _khcol(k):
    r, j = _fold_pos(k)
    return r * 8 + j


class Prog:
    SEMS = ("pe", "dve", "act", "coll")

    def __init__(self, nc, ctx):
        self.nc = nc
        self.ctx = ctx
        self.sem = {}
        self.cnt = {}
        self.hw = {}
        for name in self.SEMS:
            self.sem[name] = ctx.enter_context(nc.semaphore(name))
            self.cnt[name] = 0
        for e in ("pe", "dve", "act", "sync", "gp"):
            self.hw[e] = {}

    def new(self, name):
        if name not in self.sem:
            self.sem[name] = self.ctx.enter_context(self.nc.semaphore(name))
            self.cnt[name] = 0
        return name

    def eng(self, e):
        return {"pe": self.nc.tensor, "dve": self.nc.vector, "act": self.nc.scalar,
                "sync": self.nc.sync, "gp": self.nc.gpsimd}[e]

    def wait(self, e, sname, val=None):
        if val is None:
            val = self.cnt[sname]
        if val <= 0 or self.hw[e].get(sname, 0) >= val:
            return
        self.eng(e).wait_ge(self.sem[sname], val)
        self.hw[e][sname] = val

    def inc(self, inst, sname, k=1):
        inst.then_inc(self.sem[sname], k)
        self.cnt[sname] += k
        return self.cnt[sname]

    def barrier(self, engines=("pe", "dve", "act", "sync", "gp")):
        for e in engines:
            for s in list(self.sem):
                self.wait(e, s)


def build_program(n_experts=NE, n_heads=NH):
    nc = bass.Bass("TRN2", target_bir_lowering=False, debug=False)

    def din(name, shape, dt=F32):
        return nc.dram_tensor(name, list(shape), dt, kind="ExternalInput").ap()

    x_in = din("x_in", [4, NTOK, D])
    ind_in = din("ind_in", [128, 4])
    wgu = din("wgu", [NE, NJ, 128, 4096])
    wdn = din("wdn", [NE, 128, NJ * 2048])
    gw = din("gw", [128, 512])
    wq = din("wq", [4, 128, 8192])
    wk = din("wk", [4, 128, 8192])
    wv = din("wv", [4, 128, 8192])
    wo = din("wo", [2, 128, 16384])
    bq = din("bq", [128, 16])
    bk = din("bk", [128, 16])
    bv = din("bv", [128, 16])
    bo_bc = din("bo_bc", [128, D])
    ident = din("ident", [128, 128], BF16)
    ones_row = din("ones_row", [1, 128], BF16)
    ones_row_f = din("ones_row_f", [1, 128])
    onec_bf = din("onec_bf", [128, 1], BF16)
    triu_s = din("triu_s", [128, 128], BF16)
    iota32 = din("iota32", [128, 256])
    tbc_in = din("tbc_in", [128, 32])
    bigp_in = din("bigp_in", [128, 1])
    eps_col = din("eps_col", [128, 1])
    amask = din("amask", [8, 128, 512], BF16)

    y_out = nc.dram_tensor("y", [NTOK, D], F32, kind="ExternalOutput").ap()
    kv_dram = nc.dram_tensor("kv_dram", [2 * NH * 128, T], BF16, kind="Internal").ap()
    x1_dram = nc.dram_tensor("x1_dram", [4, NTOK, D], F32, kind="Internal").ap()
    q_dram = nc.dram_tensor("q_dram", [NH * 128, NTOK], BF16, kind="Internal").ap()
    xg1 = nc.dram_tensor("xg1", [NE * CAP1, XROW], BF16, kind="Internal").ap()
    o1 = nc.dram_tensor("o1", [NE * CAP1 + 1, D], BF16, kind="Internal").ap()
    xg2 = nc.dram_tensor("xg2", [NE * CAP2, XROW], BF16, kind="Internal").ap()
    o2 = nc.dram_tensor("o2", [NE * CAP2 + 1, D], BF16, kind="Internal").ap()
    wgub = nc.dram_tensor("wgub", [NE, 2, 128, 3 * 4096], BF16, kind="Internal").ap()
    wdb = nc.dram_tensor("wdb", [NE, 128, NJ * 2048], BF16, kind="Internal").ap()
    wqkvb = nc.dram_tensor("wqkvb", [16, 128, 8192], BF16, kind="Internal").ap()

    with ExitStack() as ctx:
        E = ctx.enter_context
        p = Prog(nc, ctx)
        te, dv, sc, gp, sy = nc.tensor, nc.vector, nc.scalar, nc.gpsimd, nc.sync

        x_sb = None
        hfm = None
        htm = E(nc.sbuf_tensor("htm", [128, D], BF16))
        stat = E(nc.sbuf_tensor("stat", [128, 16], F32))
        idn = E(nc.sbuf_tensor("idn", [128, 128], BF16))
        onesr = E(nc.sbuf_tensor("onesr", [1, 128], BF16))
        onesrf = E(nc.sbuf_tensor("onesrf", [1, 128], F32))
        onecb = E(nc.sbuf_tensor("onecb", [128, 1], BF16))
        trius = E(nc.sbuf_tensor("trius", [128, 128], BF16))
        iotar_sb = E(nc.sbuf_tensor("iotar_sb", [128, 256], F32))
        idxs = E(nc.sbuf_tensor("idxs", [128, 256], F32))
        tbc = E(nc.sbuf_tensor("tbc", [128, 32], F32))
        bigp = E(nc.sbuf_tensor("bigp", [128, 1], F32))
        epsc = E(nc.sbuf_tensor("epsc", [128, 1], F32))
        bqs = E(nc.sbuf_tensor("bqs", [128, 16], F32))
        bks = E(nc.sbuf_tensor("bks", [128, 16], F32))
        bvs = E(nc.sbuf_tensor("bvs", [128, 16], F32))
        gws = E(nc.sbuf_tensor("gws", [128, 512], BF16))
        comb = E(nc.sbuf_tensor("comb", [128, NTT * 32], F32))
        rt = E(nc.sbuf_tensor("rt", [128, 512], F32))
        ind_sb = E(nc.sbuf_tensor("ind_sb", [128, 4], F32))
        mcol = E(nc.sbuf_tensor("mcol", [128, 12], F32))
        mcolb = E(nc.sbuf_tensor("mcolb", [128, 1], BF16))
        mrow = E(nc.sbuf_tensor("mrow", [1, 128], BF16))
        rsum = E(nc.sbuf_tensor("rsum", [128, 1], F32))
        cstage = E(nc.sbuf_tensor("cstage", [128, 128], BF16))
        gidxT = E(nc.sbuf_tensor("gidxT", [128, 4 * NTT * 8], U32))
        off1 = E(nc.sbuf_tensor("off1", [1, 32], F32))
        ridx = E(nc.sbuf_tensor("ridx", [128, 48], F32))
        ridxu = E(nc.sbuf_tensor("ridxu", [128, 16], U32))

        with nc.Block():
            p.new("c_ld")
            p.new("xg_ld")
            p.new("xsc")
            p.new("zr")

            def load_x_group(pp, q=None):
                i = (q or gp).dma_start(
                    x_sb[:, :].rearrange("p (t d) -> p t d", t=NTT),
                    x_in[pp].rearrange("(t p) d -> p t d", p=128))
                return p.inc(i, "xg_ld", 16)

            for dst, srcap in [(idn, ident), (onesr, ones_row), (onesrf, ones_row_f),
                               (onecb, onec_bf), (trius, triu_s), (iotar_sb, iota32),
                               (tbc, tbc_in), (bigp, bigp_in), (epsc, eps_col),
                               (bqs, bq), (bks, bk), (bvs, bv), (gws, gw),
                               (ind_sb, ind_in)]:
                i = gp.dma_start(dst[:, :], srcap)
                p.inc(i, "c_ld", 16)
            for e in ("pe", "dve", "act", "sync", "gp"):
                p.wait(e, "c_ld")
            i = dv.memset(htm[0:1, :], 0.0)
            zr_d = p.inc(i, "dve")
            p.wait("act", "dve", zr_d)
            i = sc.dma_start(o1[NZ1:NZ1 + 1, :], htm[0:1, :])
            p.inc(i, "zr", 16)
            i = sc.dma_start(o2[NZ2:NZ2 + 1, :], htm[0:1, :])
            zr_c = p.inc(i, "zr", 16)
            p.wait("act", "zr", zr_c)

            bnd1 = gp.alloc_register("bnd1")
            gp.reg_mov(bnd1, NE * CAP1 - 1)
            bnd2 = gp.alloc_register("bnd2")
            gp.reg_mov(bnd2, NE * CAP2 - 1)

            def emit_ln(tp_ps, xst=None, pre_tile=None):
                tp_free = {0: 0, 1: 0}
                a_h_prev = 0
                xst_cp_prev = 0
                for tt in range(NTT):
                    xt = x_sb[:, tt * D:(tt + 1) * D]
                    if pre_tile is not None:
                        d_pre = pre_tile(tt)
                        p.wait("act", "dve", d_pre)
                    ssum = stat[:, 0:1]; ex2 = stat[:, 1:2]; mneg = stat[:, 2:3]
                    nmsq = stat[:, 3:4]; var = stat[:, 4:5]; std = stat[:, 5:6]
                    rstd = stat[:, 6:7]; nmr = stat[:, 7:8]
                    p.wait("act", "pe")
                    p.wait("act", "dve", xst_cp_prev)
                    i = sc.activation(htm[:, :], xt, AF.Square, accum_out=ex2)
                    a_sq = p.inc(i, "act")
                    p.wait("dve", "act", a_h_prev)
                    i = dv.reduce_sum(ssum, xt, axis=AX.X, negate=True)
                    p.inc(i, "dve"); p.wait("dve", "dve")
                    i = dv.tensor_scalar(mneg, ssum, 1.0 / D, None, op0=ALU.mult)
                    p.inc(i, "dve"); p.wait("dve", "dve")
                    i = dv.tensor_scalar(nmsq, mneg, mneg, -1.0, op0=ALU.mult, op1=ALU.mult)
                    p.inc(i, "dve")
                    p.wait("dve", "act", a_sq); p.wait("dve", "dve")
                    i = dv.tensor_scalar(var, ex2, 1.0 / D, None, op0=ALU.mult)
                    p.inc(i, "dve"); p.wait("dve", "dve")
                    i = dv.tensor_add(var, var, nmsq)
                    d_var = p.inc(i, "dve")
                    p.wait("act", "dve", d_var)
                    i = sc.activation(std, var, AF.Sqrt, bias=epsc[:, :])
                    a_std = p.inc(i, "act")
                    p.wait("dve", "act", a_std)
                    i = dv.reciprocal(rstd, std)
                    p.inc(i, "dve"); p.wait("dve", "dve")
                    i = dv.tensor_scalar(nmr, mneg, rstd, None, op0=ALU.mult)
                    d_nmr = p.inc(i, "dve")
                    p.wait("act", "dve", d_nmr)
                    p.wait("act", "act", a_sq)
                    i = sc.activation(htm[:, :], xt, AF.Identity, bias=nmr, scale=rstd)
                    a_h = p.inc(i, "act")
                    a_h_prev = a_h
                    if xst is not None:
                        p.wait("dve", "act", a_h)
                        p.wait("dve", "xsc")
                        i = dv.tensor_copy(xst[:, tt * XROW: tt * XROW + D],
                                           htm[:, :])
                        xst_cp_prev = p.inc(i, "dve")
                    p.wait("pe", "act", a_h)
                    for kc in range(NKC):
                        bank = kc % 2
                        p.wait("pe", "dve", tp_free[bank])
                        i = te.transpose(tp_ps[bank][:, 0:128],
                                         htm[:, kc * 128:(kc + 1) * 128], idn[:, :])
                        t_pe = p.inc(i, "pe")
                        p.wait("dve", "pe", t_pe)
                        i = dv.tensor_copy(
                            hfm[:, kc * NTOK + tt * 128: kc * NTOK + (tt + 1) * 128],
                            tp_ps[bank][:, 0:128])
                        tp_free[bank] = p.inc(i, "dve")

            def emit_routing(lg_ps, rp=None, xst=None, grp=0, rc=None):
                probs = rt[:, 0:32]; vals = rt[:, 32:40]; cum = rt[:, 40:48]
                lt = rt[:, 48:56]; act8 = rt[:, 56:64]; rw = rt[:, 64:72]
                smax = rt[:, 72:73]; ssum = rt[:, 73:74]; sinv = rt[:, 74:75]
                s8 = rt[:, 75:76]; den = rt[:, 76:77]; dinv = rt[:, 77:78]
                s2 = rt[:, 78:79]; den2 = rt[:, 79:80]; d2i = rt[:, 80:81]
                sca = rt[:, 81:82]
                mk = [rt[:, 96 + k * 32: 128 + k * 32] for k in range(8)]
                m32 = rt[:, 352:384]; slotg = rt[:, 384:416]
                tmp32 = rt[:, 416:448]

                def dstep(inst):
                    c = p.inc(inst, "dve")
                    p.wait("dve", "dve", c)
                    return c

                for tt in range(NTT):
                    p.wait("pe", "dve")
                    p.wait("pe", "act")
                    for kc in range(NKC):
                        i = te.matmul(lg_ps[:, 0:32],
                                      hfm[:, kc * NTOK + tt * 128: kc * NTOK + (tt + 1) * 128],
                                      gws[:, kc * 32:(kc + 1) * 32],
                                      start=(kc == 0), stop=(kc == NKC - 1))
                    m_pe = p.inc(i, "pe")
                    p.wait("dve", "pe", m_pe)
                    dstep(dv.reduce_max(smax, lg_ps[:, 0:32], axis=AX.X, negate=True))
                    p.wait("act", "dve")
                    p.wait("act", "pe", m_pe)
                    i = sc.activation(probs, lg_ps[:, 0:32], AF.Exp, bias=smax,
                                      accum_out=ssum)
                    a = p.inc(i, "act")
                    p.wait("dve", "act", a)
                    dstep(dv.reciprocal(sinv, ssum))
                    dstep(dv.tensor_scalar(probs, probs, sinv, None, op0=ALU.mult))
                    dstep(dv.tensor_mul(probs, probs, tbc[:, :]))
                    for k in range(8):
                        dstep(dv.reduce_max(vals[:, k:k + 1], probs, axis=AX.X))
                        dstep(dv.tensor_scalar(mk[k], probs, vals[:, k:k + 1], None,
                                               op0=ALU.is_equal))
                        dstep(dv.scalar_tensor_tensor(probs, mk[k], -2.0, probs,
                                                      op0=ALU.mult, op1=ALU.add))
                    dstep(dv.reduce_sum(s8, vals, axis=AX.X))
                    dstep(dv.tensor_scalar(den, s8, 1e-12, None, op0=ALU.max))
                    dstep(dv.reciprocal(dinv, den))
                    dstep(dv.tensor_scalar(cum, vals, dinv, None, op0=ALU.mult))
                    for k in range(1, 8):
                        dstep(dv.tensor_add(cum[:, k:k + 1], cum[:, k:k + 1],
                                            cum[:, k - 1:k]))
                    dstep(dv.tensor_scalar(lt, cum, 0.8, None, op0=ALU.is_lt))
                    dstep(dv.memset(act8[:, 0:1], 1.0))
                    dstep(dv.tensor_copy(act8[:, 1:8], lt[:, 0:7]))
                    dstep(dv.tensor_mul(rw, vals, act8))
                    dstep(dv.reduce_sum(s2, rw, axis=AX.X))
                    dstep(dv.tensor_scalar(den2, s2, 1e-12, None, op0=ALU.max))
                    dstep(dv.reciprocal(d2i, den2))
                    dstep(dv.tensor_scalar(sca, d2i, 0.5, None, op0=ALU.mult))
                    dstep(dv.tensor_scalar(rw, rw, sca, None, op0=ALU.mult))
                    ctt = comb[:, tt * 32:(tt + 1) * 32]
                    dstep(dv.tensor_scalar(ctt, mk[0], rw[:, 0:1], None, op0=ALU.mult))
                    for k in range(1, 8):
                        dstep(dv.scalar_tensor_tensor(ctt, mk[k], rw[:, k:k + 1], ctt,
                                                      op0=ALU.mult, op1=ALU.add))
                    if rp is None:
                        continue
                    dstep(dv.tensor_copy(
                        xst[:, tt * XROW + D: tt * XROW + D + 32], ctt))
                    dstep(dv.tensor_scalar(m32, ctt, 0.0, None, op0=ALU.is_gt))
                    d_mbf = dstep(dv.tensor_copy(htm[:, 0:32], m32))
                    p.wait("pe", "dve", d_mbf)
                    i = te.matmul(rp[:, 0:32], trius[:, :], htm[:, 0:32],
                                  start=True, stop=True)
                    p.inc(i, "pe")
                    i = te.matmul(rp[0:1, 64:96], onecb[:, :], htm[:, 0:32],
                                  start=True, stop=True)
                    p.inc(i, "pe")
                    i = te.matmul(rp[:, 128:160], onesrf[:, :], off1[0:1, :],
                                  start=True, stop=True)
                    bc_pe = p.inc(i, "pe")
                    p.wait("dve", "pe", bc_pe)
                    dstep(dv.tensor_copy(tmp32, rp[:, 0:32]))
                    dstep(dv.tensor_add(slotg, tmp32, rp[:, 128:160]))
                    dstep(dv.tensor_add(off1[0:1, :], off1[0:1, :], rp[0:1, 64:96]))
                    mkall = rt[:, 96:352]
                    gcols = ridx[:, 0:8]; scols = ridx[:, 8:16]
                    ek8 = ridx[:, 16:24]; sel8 = ridx[:, 24:32]
                    base8 = ridx[:, 32:40]
                    k8 = idxs[:, :].rearrange("p (k c) -> p k c", k=8)
                    dstep(dv.tensor_mul(idxs[:, :], mkall, iotar_sb[:, :]))
                    dstep(dv.reduce_sum(ek8, k8, axis=AX.X))
                    for k in range(8):
                        dstep(dv.tensor_mul(idxs[:, k * 32:(k + 1) * 32], mk[k],
                                            slotg))
                    dstep(dv.reduce_sum(sel8, k8, axis=AX.X))
                    dstep(dv.scalar_tensor_tensor(base8, ek8, float(rc["cap"]), sel8,
                                                  op0=ALU.mult, op1=ALU.add))
                    dstep(dv.tensor_scalar(gcols, base8, float(-rc["nz"]), None,
                                           op0=ALU.add))
                    dstep(dv.tensor_mul(gcols, gcols, act8))
                    dstep(dv.tensor_scalar(gcols, gcols, float(rc["nz"]), None,
                                           op0=ALU.add))
                    dstep(dv.tensor_scalar(scols, base8, bigp[:, 0:1], None,
                                           op0=ALU.subtract))
                    dstep(dv.tensor_mul(scols, scols, act8))
                    dstep(dv.tensor_scalar(scols, scols, bigp[:, 0:1], None,
                                           op0=ALU.add))
                    gc_u = gidxT[:, (grp * NTT + tt) * 8:(grp * NTT + tt) * 8 + 8]
                    dstep(dv.tensor_copy(gc_u, gcols))
                    d_sc = dstep(dv.tensor_copy(ridxu[:, 0:8], scols))
                    p.wait("gp", "dve", d_sc)
                    for k in range(8):
                        i = gp.indirect_dma_start(
                            rc["xg"],
                            IndirectOffsetOnAxis(ap=ridxu[:, k:k + 1], axis=0),
                            xst[:, tt * XROW:(tt + 1) * XROW], None,
                            bounds_check=rc["bnd"], oob_is_err=False)
                        p.inc(i, "xsc", 16)

            def emit_ffn_routed(tag, cap, sts, xg_ap, o_ap, n_exp, f32_src=True):
                nsub_tot = cap // 128
                subs_of_st = [n // 128 for (_, n) in sts]
                with ExitStack() as fs:
                    Ef = fs.enter_context
                    gin = Ef(nc.sbuf_tensor(f"gin{tag}", [128, 4 * XROW], BF16))
                    ginT = Ef(nc.sbuf_tensor(f"ginT{tag}", [128, NKC * 512], BF16))
                    act_sb = Ef(nc.sbuf_tensor(f"acts{tag}", [128, NJ * cap], BF16))
                    cbuf = Ef(nc.sbuf_tensor(f"cbuf{tag}", [128, 16], F32))
                    sil = Ef(nc.sbuf_tensor(f"sil{tag}", [128, 512], F32))
                    outst = [Ef(nc.sbuf_tensor(f"o{tag}{s}", [128, D], BF16))
                             for s in range(2)]
                    qcast = ([Ef(nc.sbuf_tensor(f"qc{tag}{s}", [128, 8192], BF16))
                              for s in range(2)] if f32_src else None)
                    wgu_sb = [Ef(nc.sbuf_tensor(f"wgu{tag}{s}", [128, 3 * 4096], BF16))
                              for s in range(2)]
                    wd_sb = Ef(nc.sbuf_tensor(f"wd{tag}", [128, NJ * 2048], BF16))
                    g_ps = Ef(nc.psum_tensor(f"g{tag}", [128, 512], F32))
                    u_ps = Ef(nc.psum_tensor(f"u{tag}", [128, 512], F32))
                    b_ps = Ef(nc.psum_tensor(f"b{tag}", [128, 2048], F32))
                    tr_ps = [Ef(nc.psum_tensor(f"tr{tag}{s}", [128, 512], BF16))
                             for s in range(2)]

                    sgu = [p.new(f"wgu{tag}{s}") for s in range(2)]
                    swd = p.new(f"swd{tag}")
                    sgin = p.new(f"gin{tag}")
                    so1 = p.new(f"so{tag}")
                    swb = p.new(f"swb{tag}")
                    sqc = p.new(f"sqc{tag}")
                    sqw = p.new(f"sqw{tag}")

                    def qkv_chunk_src(c):
                        if c < 4:
                            return wq[c]
                        if c < 8:
                            return wk[c - 4]
                        if c < 12:
                            return wv[c - 8]
                        h, half = divmod(c - 12, 2)
                        return wo[h][:, half * 8192:(half + 1) * 8192]

                    qc_ld = {}
                    qc_wr = {}

                    def qkv_cast_step(e):
                        if not f32_src:
                            return
                        if 1 <= e <= 16:
                            c = e - 1
                            p.wait("act", sqc, qc_ld[c])
                            i = sc.dma_start(wqkvb[c], qcast[c % 2][:, :])
                            qc_wr[c] = p.inc(i, sqw, 16)
                        if e < 16:
                            if e >= 2:
                                p.wait("gp", sqw, qc_wr[e - 2])
                            i = gp.dma_start(qcast[e % 2][:, :], qkv_chunk_src(e))
                            qc_ld[e] = p.inc(i, sqc, 16)
                    half_free = {}
                    half_ld = {}
                    wb_done = {}
                    wd_free = {}
                    wd_ld = {}
                    wdwb_done = {}
                    gin_free = {}
                    gin_free_dve = {}
                    gin_ld = {}

                    nst = len(sts)

                    def load_half(e, h):
                        idx = 2 * e + h
                        if idx >= 2:
                            p.wait("gp", "pe", half_free[idx - 2])
                            if f32_src:
                                p.wait("gp", swb, wb_done[idx - 2])
                        if f32_src:
                            i = gp.dma_start(
                                wgu_sb[idx % 2][:, :].rearrange("p (j n) -> p j n", j=3),
                                wgu[e, 3 * h:3 * h + 3].rearrange("j p n -> p j n"))
                        else:
                            i = gp.dma_start(wgu_sb[idx % 2][:, :], wgub[e, h])
                        half_ld[idx] = (sgu[idx % 2], p.inc(i, sgu[idx % 2], 16))

                    def load_wd(e):
                        qe, q = ("gp", gp) if f32_src else ("sync", sy)
                        if e >= 1:
                            p.wait(qe, "pe", wd_free[e - 1])
                            if f32_src:
                                p.wait(qe, swb, wdwb_done[e - 1])
                        i = q.dma_start(wd_sb[:, :], wdn[e] if f32_src else wdb[e])
                        wd_ld[e] = p.inc(i, swd, 16)

                    def load_gin(gst):
                        e, sti = divmod(gst, nst)
                        if e >= n_exp:
                            return
                        st_off, st_n = sts[sti]
                        ns = st_n // 128
                        if gst >= 1:
                            p.wait("sync", "pe", gin_free[gst - 1])
                            p.wait("sync", "dve", gin_free_dve[gst - 1])
                        i = sy.dma_start(
                            gin[:, 0:ns * XROW].rearrange(
                                "p (s c) -> p s c", s=ns),
                            xg_ap[e * cap + st_off: e * cap + st_off + st_n].rearrange(
                                "(s p) c -> p s c", p=128))
                        gin_ld[gst] = (sgin, p.inc(i, sgin, 16))

                    load_half(0, 0)
                    load_half(0, 1)
                    load_wd(0)
                    load_gin(0)

                    a_sg = None
                    a_tt = None
                    tp_free = {0: 0, 1: 0}
                    out_free = {0: 0, 1: 0}
                    d_cb_prev = 0

                    for e in range(n_exp):
                        if f32_src:
                            for h in range(2):
                                idx = 2 * e + h
                                p.wait("act", *half_ld[idx])
                                i = sc.dma_start(wgub[e, h], wgu_sb[idx % 2][:, :])
                                wb_done[idx] = p.inc(i, swb, 16)
                        for sti, (st_off, st_n) in enumerate(sts):
                            gst = e * nst + sti
                            ns = subs_of_st[sti]
                            p.wait("pe", *gin_ld[gst])
                            p.wait("dve", *gin_ld[gst])
                            p.wait("dve", "dve", d_cb_prev)
                            for s in range(ns):
                                i = dv.tensor_copy(
                                    cbuf[:, st_off // 128 + s: st_off // 128 + s + 1],
                                    gin[:, s * XROW + D + e: s * XROW + D + e + 1])
                                d_cb = p.inc(i, "dve")
                            d_cb_prev = d_cb
                            for s in range(ns):
                                for kc in range(NKC):
                                    bank = kc % 2
                                    p.wait("pe", "dve", tp_free[bank])
                                    i = te.transpose(
                                        tr_ps[bank][:, 0:128],
                                        gin[:, s * XROW + kc * 128: s * XROW + (kc + 1) * 128],
                                        idn[:, :])
                                    t_pe = p.inc(i, "pe")
                                    p.wait("dve", "pe", t_pe)
                                    i = dv.tensor_copy(
                                        ginT[:, kc * 512 + s * 128: kc * 512 + (s + 1) * 128],
                                        tr_ps[bank][:, 0:128])
                                    tp_free[bank] = p.inc(i, "dve")
                            gin_free[gst] = p.cnt["pe"]
                            gin_free_dve[gst] = p.cnt["dve"]
                            load_gin(gst + 1)
                            for j in range(NJ):
                                wt = wgu_sb[(2 * e + j // 3) % 2]
                                joff = (j % 3) * 4096
                                p.wait("pe", *half_ld[2 * e + j // 3])
                                if a_sg is not None:
                                    p.wait("pe", "act", a_sg)
                                p.wait("pe", "dve")
                                for kc in range(NKC):
                                    i = te.matmul(
                                        g_ps[:, 0:st_n],
                                        wt[:, joff + kc * 256: joff + kc * 256 + 128],
                                        ginT[:, kc * 512: kc * 512 + st_n],
                                        start=(kc == 0), stop=(kc == NKC - 1))
                                g_pe = p.inc(i, "pe")
                                if a_tt is not None:
                                    p.wait("pe", "dve", a_tt)
                                for kc in range(NKC):
                                    i = te.matmul(
                                        u_ps[:, 0:st_n],
                                        wt[:, joff + kc * 256 + 128: joff + kc * 256 + 256],
                                        ginT[:, kc * 512: kc * 512 + st_n],
                                        start=(kc == 0), stop=(kc == NKC - 1))
                                u_pe = p.inc(i, "pe")
                                if sti == nst - 1 and (j == 2 or j == NJ - 1):
                                    half_free[2 * e + (0 if j == 2 else 1)] = u_pe
                                p.wait("act", "pe", g_pe)
                                if a_tt is not None:
                                    p.wait("act", "dve", a_tt)
                                i = sc.activation(sil[:, 0:st_n], g_ps[:, 0:st_n],
                                                  AF.Silu)
                                a_sg = p.inc(i, "act")
                                p.wait("dve", "pe", u_pe)
                                p.wait("dve", "act", a_sg)
                                i = dv.tensor_mul(
                                    act_sb[:, j * cap + st_off: j * cap + st_off + st_n],
                                    sil[:, 0:st_n], u_ps[:, 0:st_n])
                                a_tt = p.inc(i, "dve")
                        if e + 1 < n_exp:
                            load_half(e + 1, 0)
                            load_half(e + 1, 1)
                        p.wait("pe", swd, wd_ld[e])
                        if f32_src:
                            p.wait("sync", swd, wd_ld[e])
                            i = sy.dma_start(wdb[e], wd_sb[:, :])
                            wdwb_done[e] = p.inc(i, swb, 16)
                        p.wait("pe", "dve", a_tt)
                        for sub in range(nsub_tot):
                            oslot = sub % 2
                            for dq in range(4):
                                if dq == 0:
                                    p.wait("pe", so1, out_free[oslot])
                                    p.wait("pe", "dve")
                                for j in range(NJ):
                                    i = te.matmul(
                                        b_ps[:, dq * 512:(dq + 1) * 512],
                                        act_sb[:, j * cap + sub * 128: j * cap + (sub + 1) * 128],
                                        wd_sb[:, j * 2048 + dq * 512: j * 2048 + (dq + 1) * 512],
                                        start=(j == 0), stop=(j == NJ - 1))
                                b_pe = p.inc(i, "pe")
                                p.wait("dve", "pe", b_pe)
                                p.wait("dve", so1, out_free[oslot])
                                i = dv.tensor_scalar(
                                    outst[oslot][:, dq * 512:(dq + 1) * 512],
                                    b_ps[:, dq * 512:(dq + 1) * 512],
                                    cbuf[:, sub:sub + 1], None, op0=ALU.mult)
                                d_ev = p.inc(i, "dve")
                            p.wait("act", "dve", d_ev)
                            i = sc.dma_start(
                                o_ap[e * cap + sub * 128: e * cap + (sub + 1) * 128, :],
                                outst[oslot][:, :])
                            out_free[oslot] = p.inc(i, so1, 16)
                        wd_free[e] = p.cnt["pe"]
                        if e + 1 < n_exp:
                            load_wd(e + 1)
                        qkv_cast_step(e)

            def emit_second_gather(o_ap, grp, tag="", tile_store=None):
                with ExitStack() as fs:
                    Ef = fs.enter_context
                    gbuf = [Ef(nc.sbuf_tensor(f"gb{tag}{grp}{s}", [128, D], BF16))
                            for s in range(4)]
                    sgb = p.new(f"sgb{tag}{grp}")
                    gb_free = {s: 0 for s in range(4)}
                    for tt in range(NTT):
                        xt = x_sb[:, tt * D:(tt + 1) * D]
                        for k in range(8):
                            slot = (tt * 8 + k) % 4
                            p.wait("gp", "dve", gb_free[slot])
                            i = gp.indirect_dma_start(
                                gbuf[slot][:, :], None, o_ap,
                                IndirectOffsetOnAxis(
                                    ap=gidxT[:, (grp * NTT + tt) * 8 + k:
                                             (grp * NTT + tt) * 8 + k + 1],
                                    axis=0))
                            g_ld = p.inc(i, sgb, 16)
                            p.wait("dve", sgb, g_ld)
                            p.wait("dve", "dve")
                            i = dv.tensor_add(xt, xt, gbuf[slot][:, :])
                            gb_free[slot] = p.inc(i, "dve")
                        if tile_store is not None:
                            tile_store(tt)

            swq = [p.new(f"wqk{s}") for s in range(2)]
            sstg = [p.new(f"stg{s}") for s in range(3)]
            sx1 = p.new("sx1")
            stg_free = {}

            def emit_proj(specs, tag):
                with ExitStack() as fs:
                    Ef = fs.enter_context
                    wbuf = [Ef(nc.sbuf_tensor(f"wqkv{tag}{s}", [128, 8192], BF16))
                            for s in range(2)]
                    stg = [Ef(nc.sbuf_tensor(f"qkvstg{tag}{s}", [128, NTOK], BF16))
                           for s in range(3)]
                    mg_ps = [Ef(nc.psum_tensor(f"mg{tag}{s}", [128, NTOK], F32))
                             for s in range(4)]
                    wld = {}
                    grp_evac = 0
                    i = sy.dma_start(wbuf[0][:, :], specs[0][1][0])
                    wld[0] = (swq[0], p.inc(i, swq[0], 16))
                    gidx = 0
                    for si, (nm, wsrc, bsb, roff, coff, dst) in enumerate(specs):
                        for mg in range(4):
                            slot = gidx % 2
                            nxt = (wsrc, mg + 1) if mg < 3 else (
                                (specs[si + 1][1], 0) if si + 1 < len(specs) else None)
                            if nxt is not None:
                                p.wait("sync", "pe")
                                i = sy.dma_start(wbuf[1 - slot][:, :], nxt[0][nxt[1]])
                                wld[gidx + 1] = (swq[1 - slot], p.inc(i, swq[1 - slot], 16))
                            p.wait("pe", *wld[gidx])
                            p.wait("pe", "act", grp_evac)
                            last_evac = 0
                            for mc in range(4):
                                for half in range(2):
                                    for kc in range(NKC):
                                        i = te.matmul(
                                            mg_ps[mc][:, half * 512:(half + 1) * 512],
                                            wbuf[slot][:, kc * 512 + mc * 128: kc * 512 + (mc + 1) * 128],
                                            hfm[:, kc * NTOK + half * 512: kc * NTOK + (half + 1) * 512],
                                            start=(kc == 0), stop=(kc == NKC - 1))
                                mc_pe = p.inc(i, "pe")
                                m_abs = mg * 4 + mc
                                sslot = m_abs % 3
                                p.wait("act", "pe", mc_pe)
                                if sslot in stg_free:
                                    p.wait("act", sstg[sslot], stg_free[sslot])
                                i = sc.activation(stg[sslot][:, :], mg_ps[mc][:, :],
                                                  AF.Identity, bias=bsb[:, m_abs:m_abs + 1])
                                a_e = p.inc(i, "act")
                                last_evac = a_e
                                p.wait("sync", "act", a_e)
                                i = sy.dma_start(
                                    dst[roff + m_abs * 128: roff + (m_abs + 1) * 128,
                                        coff:coff + NTOK],
                                    stg[sslot][:, :])
                                stg_free[sslot] = p.inc(i, sstg[sslot], 16)
                            grp_evac = last_evac
                            gidx += 1

            def emit_attention(n_heads_):
                with ExitStack() as fs:
                    Ef = fs.enter_context
                    qh = [Ef(nc.sbuf_tensor(f"qh{s}", [128, NTOK], BF16)) for s in range(2)]
                    kh = [Ef(nc.sbuf_tensor(f"kh{s}", [128, 4096], BF16)) for s in range(2)]
                    vh = [Ef(nc.sbuf_tensor(f"vh{s}", [128, 32 * VST], BF16))
                          for s in range(2)]
                    pt = [Ef(nc.sbuf_tensor(f"pt{s}", [128, 512], BF16)) for s in range(2)]
                    masks = Ef(nc.sbuf_tensor("masks", [128, 8 * 512], BF16))
                    smsk = p.new("msk")
                    i = gp.dma_start(masks[:, :].rearrange("p (s n) -> p s n", s=8),
                                     amask.rearrange("s p n -> p s n"))
                    mask_ld = p.inc(i, smsk, 16)
                    p.wait("dve", smsk, mask_ld)
                    p1_ps = [Ef(nc.psum_tensor(f"p1{s}", [128, 512], F32)) for s in range(2)]
                    sT_ps = [Ef(nc.psum_tensor(f"sT{s}", [128, 512], F32)) for s in range(2)]
                    cs_ps = [Ef(nc.psum_tensor(f"cs{s}", [128, 512], F32)) for s in range(2)]
                    tr_ps = [Ef(nc.psum_tensor(f"tr{s}", [128, 1024], BF16)) for s in range(2)]

                    for s in range(2):
                        i = dv.memset(
                            vh[s][:, :].rearrange("p (c n) -> p c n", c=32)[:, :, 128:129],
                            1.0)
                        d_ones = p.inc(i, "dve")
                    p.wait("sync", "dve", d_ones)
                    p.wait("pe", "dve", d_ones)

                    skq = [p.new(f"kq{s}") for s in range(2)]
                    svh = [p.new(f"vh{s}") for s in range(2)]
                    kld = {}
                    vld = {}
                    head_pe_done = {}

                    def load_head(h):
                        slot = h % 2
                        if h >= 2:
                            p.wait("sync", "pe", head_pe_done[h - 2])
                        i = sy.dma_start(qh[slot][:, :], q_dram[h * 128:(h + 1) * 128, :])
                        p.inc(i, skq[slot], 16)
                        i = sy.dma_start(kh[slot][:, :],
                                         kv_dram[h * 128:(h + 1) * 128, :])
                        p.inc(i, skq[slot], 16)
                        kld[h] = (skq[slot], p.cnt[skq[slot]])
                        for cc in range(32):
                            i = sy.dma_start(
                                vh[slot][:, cc * VST: cc * VST + 128],
                                kv_dram[NH * 128 + h * 128: NH * 128 + (h + 1) * 128,
                                        cc * 128:(cc + 1) * 128],
                                transpose=True)
                            p.inc(i, svh[slot], 16)
                        vld[h] = (svh[slot], p.cnt[svh[slot]])

                    for s3 in range(3):
                        p.wait("sync", f"stg{s3}")
                        p.wait("act", f"stg{s3}")
                    load_head(0)
                    cs_free = {0: 0, 1: 0}
                    pt_free = {0: 0, 1: 0}
                    p1_free = {0: 0, 1: 0}
                    sT_free = {0: 0, 1: 0}
                    for h in range(n_heads_):
                        slot = h % 2
                        if h + 1 < n_heads_:
                            load_head(h + 1)
                        p.wait("pe", *kld[h])
                        p.wait("pe", *vld[h])
                        kh4 = kh[slot][:, :].rearrange("p (r j c) -> p r j c", r=4, j=8)
                        for s in range(8):
                            C = C_S[s]
                            qsl = qh[slot][:, s * 128:(s + 1) * 128]
                            for g in range(C // 4):
                                bank = g % 2
                                p.wait("pe", "dve", p1_free[bank])
                                i = te.matmul(p1_ps[bank][:, :], qsl,
                                              kh4[:, :, J_OF_PIECE[g], :],
                                              start=True, stop=True)
                                pe_c = p.inc(i, "pe")
                                p.wait("dve", "pe", pe_c)
                                i = dv.reduce_max(mcol[:, g:g + 1], p1_ps[bank][:, :],
                                                  axis=AX.X)
                                p1_free[bank] = p.inc(i, "dve")
                            p.wait("dve", "dve")
                            i = dv.reduce_max(mcol[:, 8:9], mcol[:, 0:C // 4],
                                              axis=AX.X, negate=True)
                            p.inc(i, "dve"); p.wait("dve", "dve")
                            i = dv.tensor_copy(mcolb[:, :], mcol[:, 8:9])
                            d_mb = p.inc(i, "dve")
                            p.wait("pe", "dve", d_mb)
                            i = te.transpose(tr_ps[0][0:1, 0:128], mcolb[:, :], idn[:, :])
                            t_pe = p.inc(i, "pe")
                            p.wait("dve", "pe", t_pe)
                            i = dv.tensor_copy(mrow[:, :], tr_ps[0][0:1, 0:128])
                            d_mr = p.inc(i, "dve")
                            p.wait("pe", "dve", d_mr)
                            ngr = C // 4
                            cb = s % 2
                            ctx_pe = 0
                            for g in range(ngr):
                                bank = g % 2
                                p.wait("pe", "act", sT_free[bank])
                                for ci in range(4):
                                    k = g * 4 + ci
                                    col = _khcol(k)
                                    i = te.matmul(sT_ps[bank][:, ci * 128:(ci + 1) * 128],
                                                  kh[slot][:, col * 128:(col + 1) * 128],
                                                  qsl, start=(ci == 0), stop=False)
                                for ci in range(4):
                                    i = te.matmul(sT_ps[bank][:, ci * 128:(ci + 1) * 128],
                                                  onesr[:, :], mrow[:, :],
                                                  start=False, stop=(ci == 3))
                                g_pe = p.inc(i, "pe")
                                p.wait("act", "pe", g_pe)
                                p.wait("act", "pe", pt_free[bank])
                                i = sc.activation(pt[bank][:, :], sT_ps[bank][:, :],
                                                  AF.Exp, scale=ISQ)
                                a_exp = p.inc(i, "act")
                                sT_free[bank] = a_exp
                                if g == ngr - 1:
                                    p.wait("dve", "act", a_exp)
                                    i = dv.tensor_mul(pt[bank][:, :], pt[bank][:, :],
                                                      masks[:, s * 512:(s + 1) * 512])
                                    d_msk = p.inc(i, "dve")
                                    p.wait("pe", "dve", d_msk)
                                else:
                                    p.wait("pe", "act", a_exp)
                                if g == 0:
                                    p.wait("pe", "dve", cs_free[cb])
                                for ci in range(4):
                                    k = g * 4 + ci
                                    col = _khcol(k)
                                    i = te.matmul(cs_ps[cb][:, 0:129],
                                                  pt[bank][:, ci * 128:(ci + 1) * 128],
                                                  vh[slot][:, col * VST: col * VST + 129],
                                                  start=(k == 0), stop=(k == C - 1))
                                ctx_pe = p.inc(i, "pe")
                                pt_free[bank] = ctx_pe
                            p.wait("dve", "pe", ctx_pe)
                            i = dv.reciprocal(rsum[:, :], cs_ps[cb][:, 128:129])
                            p.inc(i, "dve"); p.wait("dve", "dve")
                            i = dv.tensor_scalar(cstage[:, :], cs_ps[cb][:, 0:128],
                                                 rsum[:, :], None, op0=ALU.mult)
                            d_cs = p.inc(i, "dve")
                            cs_free[cb] = d_cs
                            p.wait("pe", "dve", d_cs)
                            i = te.transpose(tr_ps[1][:, 0:128], cstage[:, :], idn[:, :])
                            t2 = p.inc(i, "pe")
                            p.wait("dve", "pe", t2)
                            i = dv.tensor_copy(
                                hfm[:, h * NTOK + s * 128: h * NTOK + (s + 1) * 128],
                                tr_ps[1][:, 0:128])
                            p.inc(i, "dve")
                        head_pe_done[h] = p.cnt["pe"]

            def emit_outproj():
                with ExitStack() as fs:
                    Ef = fs.enter_context
                    wo_sb = Ef(nc.sbuf_tensor("wo_sb", [128, 16384], BF16))
                    bos = Ef(nc.sbuf_tensor("bos", [128, D], F32))
                    op_ps = [Ef(nc.psum_tensor(f"op{s}", [128, D], F32)) for s in range(2)]
                    sbo = p.new("bob")
                    swo = p.new("wol")
                    i = gp.dma_start(bos[:, :], bo_bc)
                    bo_ld = p.inc(i, sbo, 16)
                    p.wait("dve", sbo, bo_ld)
                    for tt in range(NTT):
                        p.wait("dve", "dve")
                        i = dv.tensor_add(x_sb[:, tt * D:(tt + 1) * D],
                                          x_sb[:, tt * D:(tt + 1) * D], bos[:, :])
                        p.inc(i, "dve")
                    op_free = {0: 0, 1: 0}
                    for halfk in range(2):
                        if halfk:
                            p.wait("gp", "pe")
                        for hh in range(2):
                            i = gp.dma_start(
                                wo_sb[:, hh * 8192:(hh + 1) * 8192],
                                wqkvb[12 + 2 * halfk + hh])
                            ld = p.inc(i, swo, 16)
                        p.wait("pe", swo, ld)
                        for tt in range(NTT):
                            bank = tt % 2
                            p.wait("pe", "dve", op_free[bank])
                            for ns in range(4):
                                for kc8 in range(8):
                                    kc = halfk * 8 + kc8
                                    i = te.matmul(
                                        op_ps[bank][:, ns * 512:(ns + 1) * 512],
                                        hfm[:, kc * NTOK + tt * 128: kc * NTOK + (tt + 1) * 128],
                                        wo_sb[:, kc8 * 2048 + ns * 512: kc8 * 2048 + (ns + 1) * 512],
                                        start=(kc8 == 0), stop=(kc8 == 7))
                            o_pe = p.inc(i, "pe")
                            p.wait("dve", "pe", o_pe)
                            p.wait("dve", "dve")
                            i = dv.tensor_add(x_sb[:, tt * D:(tt + 1) * D],
                                              x_sb[:, tt * D:(tt + 1) * D],
                                              op_ps[bank][:, :])
                            op_free[bank] = p.inc(i, "dve")

            def ln_pass(name, xst=None, pre_tile=None, bar=True):
                with ExitStack() as ps0:
                    tp = [ps0.enter_context(nc.psum_tensor(f"{name}{s}", [128, 1024], BF16))
                          for s in range(2)]
                    emit_ln(tp, xst=xst, pre_tile=pre_tile)
                    if bar:
                        p.barrier()

            with ExitStack() as p1s:
                x_sb = p1s.enter_context(
                    nc.sbuf_tensor("x_sbA", [128, NTT * D], F32))
                hfm = p1s.enter_context(
                    nc.sbuf_tensor("hfmA", [128, NKC * NTOK], BF16))
                xst = p1s.enter_context(
                    nc.sbuf_tensor("xst", [128, NTT * XROW], BF16))
                i = dv.memset(off1[0:1, :], 0.0)
                p.inc(i, "dve")
                for cp in range(4):
                    if cp > 0:
                        p.barrier()
                    xg = load_x_group(cp)
                    for e in ("pe", "dve", "act"):
                        p.wait(e, "xg_ld", xg)
                    ln_pass(f"lnA{cp}", xst=xst)
                    with ExitStack() as ps1:
                        rp = ps1.enter_context(
                            nc.psum_tensor(f"rp{cp}", [128, 512], F32))
                        lg = ps1.enter_context(
                            nc.psum_tensor(f"lg1{cp}", [128, 512], F32))
                        emit_routing(lg, rp=rp, xst=xst, grp=cp,
                                     rc=dict(cap=CAP1, xg=xg1, bnd=bnd1, nz=NZ1))
                        p.barrier()
                p.barrier()

            emit_ffn_routed("f1", CAP1, STS1, xg1, o1, n_experts)
            p.barrier()

            p2s = ctx.enter_context(ExitStack())
            x_sb = p2s.enter_context(
                nc.sbuf_tensor("x_sbB", [128, NTT * D], F32))
            hs = ExitStack()
            hfm = hs.enter_context(
                nc.sbuf_tensor("hfmB", [128, NKC * NTOK], BF16))

            with ExitStack() as g2s:
                gbuf = [g2s.enter_context(nc.sbuf_tensor(f"gbP{s}", [128, D], BF16))
                        for s in range(6)]
                sgb = p.new("sgbP")
                gb_state = {s: 0 for s in range(6)}

                def mk_pre(grp):
                    def pre(tt):
                        xt = x_sb[:, tt * D:(tt + 1) * D]
                        last = 0
                        for k in range(8):
                            slot = (tt * 8 + k) % 6
                            p.wait("gp", "dve", gb_state[slot])
                            i = gp.indirect_dma_start(
                                gbuf[slot][:, :], None, o1,
                                IndirectOffsetOnAxis(
                                    ap=gidxT[:, (grp * NTT + tt) * 8 + k:
                                             (grp * NTT + tt) * 8 + k + 1],
                                    axis=0))
                            g_ld = p.inc(i, sgb, 16)
                            p.wait("dve", sgb, g_ld)
                            p.wait("dve", "dve")
                            i = dv.tensor_add(xt, xt, gbuf[slot][:, :])
                            gb_state[slot] = p.inc(i, "dve")
                            last = gb_state[slot]
                        return last
                    return pre

                xg = load_x_group(0)
                for cp in range(4):
                    for e in ("pe", "dve", "act"):
                        p.wait(e, "xg_ld", xg)
                    ln_pass(f"lnB{cp}", pre_tile=mk_pre(cp), bar=False)
                    p.wait("sync", "dve")
                    i = sy.dma_start(
                        x1_dram[cp].rearrange("(t p) d -> p t d", p=128),
                        x_sb[:, :].rearrange("p (t d) -> p t d", t=NTT))
                    p.inc(i, sx1, 16)
                    p.wait("pe", "dve")
                    act_mark = p.cnt["act"]
                    dve_mark = p.cnt["dve"]
                    emit_proj([("k", [wqkvb[4 + i] for i in range(4)], bks,
                                0, cp * NTOK, kv_dram),
                               ("v", [wqkvb[8 + i] for i in range(4)], bvs,
                                NH * 128, cp * NTOK, kv_dram)],
                              f"{cp}")
                    if cp < 3:
                        p.wait("gp", "act", act_mark)
                        p.wait("gp", "dve", dve_mark)
                        p.wait("gp", sx1)
                        xg = load_x_group(cp + 1)
                p.barrier()

            with ExitStack() as bs:
                Eb = bs.enter_context
                xtmp = [Eb(nc.sbuf_tensor(f"xtmp{s}", [128, D], F32)) for s in range(2)]
                sbl = p.new("xbl")
                p.wait("sync", sx1)
                for tt in range(NTT):
                    for cp in range(4):
                        slot = cp % 2
                        p.wait("sync", "dve")
                        i = sy.dma_start(
                            xtmp[slot][:, :],
                            x1_dram[cp, tt * 128:(tt + 1) * 128, :])
                        ldc = p.inc(i, sbl, 16)
                        p.wait("dve", sbl, ldc)
                        xt = x_sb[:, tt * D:(tt + 1) * D]
                        if cp == 0:
                            i = dv.tensor_scalar(xt, xtmp[slot][:, :],
                                                 ind_sb[:, 0:1], None, op0=ALU.mult)
                        else:
                            i = dv.scalar_tensor_tensor(
                                xt, xtmp[slot][:, :], ind_sb[:, cp:cp + 1], xt,
                                op0=ALU.mult, op1=ALU.add)
                        p.inc(i, "dve")
                p.barrier()
            ln_pass("lnQ")
            emit_proj([("q", [wqkvb[i] for i in range(4)], bqs, 0, 0, q_dram)], "q")
            p.barrier()
            emit_attention(n_heads)
            p.barrier()
            emit_outproj()
            p.barrier()
            i = dv.memset(off1[0:1, :], 0.0)
            p.inc(i, "dve"); p.wait("dve", "dve")
            with ExitStack() as f2s:
                xst2 = f2s.enter_context(
                    nc.sbuf_tensor("xst2", [128, NTT * XROW], BF16))
                ln_pass("lnC", xst=xst2)
                with ExitStack() as ps1:
                    rp2 = ps1.enter_context(
                        nc.psum_tensor("rpF2", [128, 512], F32))
                    lg = ps1.enter_context(nc.psum_tensor("lg2", [128, 512], F32))
                    emit_routing(lg, rp=rp2, xst=xst2, grp=0,
                                 rc=dict(cap=CAP2, xg=xg2, bnd=bnd2, nz=NZ2))
                    p.barrier()
            hs.close()
            emit_ffn_routed("f2", CAP2, STS2, xg2, o2, n_experts, f32_src=False)
            p.barrier()
            sst = p.new("st")

            def store_tile(tt):
                p.wait("sync", "dve")
                i = sy.dma_start(y_out[tt * 128:(tt + 1) * 128, :],
                                 x_sb[:, tt * D:(tt + 1) * D])
                p.inc(i, sst, 16)

            emit_second_gather(o2, 0, tag="s2", tile_store=store_tile)
            p.wait("sync", sst, 16 * NTT)

    return nc


def prep_shared(Wq, bq_, Wk, bk_, Wv, bv_, Wo, bo_, gate_w, gate_up, down):
    def f32(a):
        return np.ascontiguousarray(np.asarray(a, dtype=np.float32))

    gup = f32(gate_up)
    wgu_a = np.empty((NE, NJ, 128, 4096), np.float32)
    for e in range(NE):
        Wt = gup[e].T
        for j in range(NJ):
            blk = np.concatenate([Wt[:, j * 128:(j + 1) * 128],
                                  Wt[:, 768 + j * 128: 768 + (j + 1) * 128]], axis=1)
            wgu_a[e, j] = blk.reshape(16, 128, 256).transpose(1, 0, 2).reshape(128, 4096)
    dn = f32(down)
    wdn_a = np.empty((NE, 128, NJ * 2048), np.float32)
    for e in range(NE):
        Wt = dn[e].T
        wdn_a[e] = Wt.reshape(NJ, 128, 2048).transpose(1, 0, 2).reshape(128, NJ * 2048)
    gwt = f32(gate_w).T
    gw_a = np.ascontiguousarray(
        gwt.reshape(16, 128, 32).transpose(1, 0, 2).reshape(128, 512))

    def qkvw(W):
        Wt = f32(W).T
        out = np.empty((4, 128, 8192), np.float32)
        for mg in range(4):
            blk = Wt[:, mg * 512:(mg + 1) * 512]
            out[mg] = blk.reshape(16, 128, 512).transpose(1, 0, 2).reshape(128, 8192)
        return out

    Wot = f32(Wo).T
    wo_a = np.empty((2, 128, 16384), np.float32)
    for hf in range(2):
        blk = Wot[hf * 1024:(hf + 1) * 1024, :]
        wo_a[hf] = blk.reshape(8, 128, 2048).transpose(1, 0, 2).reshape(128, 16384)

    def bias16(b):
        return np.ascontiguousarray(f32(b).reshape(16, 128).T)

    return dict(
        wgu=wgu_a, wdn=wdn_a, gw=gw_a,
        wq=qkvw(Wq), wk=qkvw(Wk), wv=qkvw(Wv), wo=wo_a,
        bq=bias16(bq_), bk=bias16(bk_), bv=bias16(bv_),
        bo_bc=np.ascontiguousarray(np.tile(f32(bo_)[None, :], (128, 1))),
        ident=np.eye(128, dtype=BF),
        ones_row=np.ones((1, 128), dtype=BF),
        ones_row_f=np.ones((1, 128), np.float32),
        onec_bf=np.ones((128, 1), dtype=BF),
        triu_s=np.triu(np.ones((128, 128), np.float32), 1).astype(BF),
        iota32=np.tile(np.arange(32, dtype=np.float32)[None, :], (128, 8)),
        tbc_in=np.tile(1.0 + np.arange(32, dtype=np.float32)[None, :] * 1e-6,
                       (128, 1)),
        bigp_in=(float(BIG) + np.arange(128, dtype=np.float32))[:, None].copy(),
        eps_col=np.full((128, 1), 1e-5, np.float32),
    )


def core_mask(cp):
    tri = np.triu(np.ones((128, 128), np.float32))
    out = np.zeros((8, 128, 512), np.float32)
    tiles = _tiles_of_core(cp)
    for s in range(8):
        C = C_S[s]
        C_own = tiles[s] + 1
        for idx, k in enumerate(range(C - 4, C)):
            if k < C_own - 1:
                out[s, :, idx * 128:(idx + 1) * 128] = 1.0
            elif k == C_own - 1:
                out[s, :, idx * 128:(idx + 1) * 128] = tri
    return out.astype(BF)


def make_in_maps(inputs):
    x = np.ascontiguousarray(np.asarray(inputs["L0_attn_x"], dtype=np.float32))
    shared = prep_shared(inputs["L0_attn_Wq"], inputs["L0_attn_bq"],
                         inputs["L0_attn_Wk"], inputs["L0_attn_bk"],
                         inputs["L0_attn_Wv"], inputs["L0_attn_bv"],
                         inputs["L0_attn_Wo"], inputs["L0_attn_bo"],
                         inputs["L0_ffn_gate_weight"], inputs["L0_ffn_gate_up_proj"],
                         inputs["L0_ffn_down_proj"])
    xcov = []
    masks = [core_mask(cp) for cp in range(4)]
    for g in range(B):
        xg = x[g].reshape(32, 128, D)
        xcov.append(np.ascontiguousarray(np.stack(
            [xg[_tiles_of_core(p)].reshape(NTOK, D) for p in range(4)])))
    in_maps = []
    for c in range(8):
        cp, g = c % 4, c // 4
        ind = np.zeros((128, 4), np.float32)
        ind[:, cp] = 1.0
        m = dict(shared)
        m["x_in"] = xcov[g]
        m["ind_in"] = ind
        m["amask"] = masks[cp]
        in_maps.append(m)
    return in_maps


def gather_out(results):
    out = np.empty((B, T, D), np.float32)
    for c in range(8):
        cp, g = c % 4, c // 4
        tiles = _tiles_of_core(cp)
        yc = np.asarray(results[c]["y"], dtype=np.float32).reshape(8, 128, D)
        for s, t in enumerate(tiles):
            out[g, t * 128:(t + 1) * 128, :] = yc[s]
    return out


def kernel(**inputs):
    in_maps = make_in_maps(inputs)
    nc = build_program()
    res = run_bass_kernel_spmd(nc, in_maps, core_ids=list(range(8)))
    return gather_out(res.results)
